# version 25
# baseline (speedup 1.0000x reference)
import contextlib
import numpy as np
import ml_dtypes

try:
    import concourse.bass as bass
except ImportError:
    import sys
    sys.path.insert(0, "/opt/trn_rl_repo")
    import concourse.bass as bass
import concourse.mybir as mybir
from concourse.bass_utils import run_bass_kernel_spmd

F32 = mybir.dt.float32
BF16 = mybir.dt.bfloat16
I8 = mybir.dt.int8
I32 = mybir.dt.int32
AF = mybir.ActivationFunctionType
ALU = mybir.AluOpType
AX = mybir.AxisListType

H = 2048
IFF = 8192
KC = H // 128
OCN = IFF // 128
G = 512
TTN = G // 128
NG = 4
TOK = 2048
ENGINES = ("sync", "tensor", "vector", "scalar", "gpsimd")
SELF_GAP = 4

_TAPS = set()



def _w_sync_c(pr, sem, v):
    pr.wait_ge(sem, v)

def _w_tensor_c(pr, sem, v):
    pr.wait_ge(sem, v)

def _w_vector_c(pr, sem, v):
    pr.wait_ge(sem, v)

def _w_scalar_c(pr, sem, v):
    pr.wait_ge(sem, v)

def _w_gpsimd_c(pr, sem, v):
    pr.wait_ge(sem, v)

def _w_sync_d(pr, sem, v):
    pr.wait_ge(sem, v)

def _w_tensor_d(pr, sem, v):
    pr.wait_ge(sem, v)

def _w_vector_d(pr, sem, v):
    pr.wait_ge(sem, v)

def _w_scalar_d(pr, sem, v):
    pr.wait_ge(sem, v)

def _w_gpsimd_d(pr, sem, v):
    pr.wait_ge(sem, v)

_WAITERS = {("sync", "c"): _w_sync_c, ("tensor", "c"): _w_tensor_c,
            ("vector", "c"): _w_vector_c, ("scalar", "c"): _w_scalar_c,
            ("gpsimd", "c"): _w_gpsimd_c,
            ("sync", "d"): _w_sync_d, ("tensor", "d"): _w_tensor_d,
            ("vector", "d"): _w_vector_d, ("scalar", "d"): _w_scalar_d,
            ("gpsimd", "d"): _w_gpsimd_d}


class Buf:
    __slots__ = ("w", "r")

    def __init__(self):
        self.w = None
        self.r = {}


NSLOT = {"sync": 24, "gpsimd": 16, "vector": 2, "scalar": 2, "tensor": 1}


class Prog:

    def __init__(self, nc):
        self.nc = nc
        self.cnt = {e: 0 for e in ENGINES}
        self.dcnt = {(e, k): 0 for e in ENGINES for k in range(NSLOT[e])}
        self.dslot = {e: 0 for e in ENGINES}
        self.idx = {e: 0 for e in ENGINES}
        self.pend = {e: [] for e in ENGINES}
        self.waited = {}
        self.semc = {}
        self.semd = {}

    def add(self, eng, fn, reads=(), writes=(), dma=False, cc=False, extra=()):
        idx = self.idx[eng] + 1
        waits = {}

        def dep(t):
            if t is None:
                return
            pe, pk, pc, pidx = t
            if pc <= 0:
                return
            if pk == "c" and pe == eng:
                if pe != "tensor" and idx - pidx <= SELF_GAP:
                    waits[(pe, pk)] = max(waits.get((pe, pk), 0), pc)
            else:
                waits[(pe, pk)] = max(waits.get((pe, pk), 0), pc)

        for b in reads:
            dep(b.w)
        for b in writes:
            dep(b.w)
            for pe, (pk, pc, pidx) in b.r.items():
                dep((pe, pk, pc, pidx))
        for t in extra:
            dep(t)

        if cc:
            self.cccnt = getattr(self, "cccnt", 0) + 1
            kind = ("cc", 0)
            mycnt = self.cccnt
        elif dma:
            k = self.dslot[eng]
            self.dslot[eng] = (k + 1) % NSLOT[eng]
            if self.dcnt[(eng, k)] > 0:
                waits[(eng, ("d", k))] = max(
                    waits.get((eng, ("d", k)), 0), self.dcnt[(eng, k)])
            self.dcnt[(eng, k)] += 16
            kind = ("d", k)
            mycnt = self.dcnt[(eng, k)]
        else:
            self.cnt[eng] += 1
            kind = "c"
            mycnt = self.cnt[eng]

        if cc:
            pass
        final = []
        for (pe, pk), pc in sorted(waits.items(), key=str):
            key = (eng, pe, pk)
            if self.waited.get(key, -1) >= pc:
                continue
            self.waited[key] = pc
            final.append((pe, pk, pc))

        self.idx[eng] = idx
        self.pend[eng].append((final, fn, kind))
        t = (eng, kind, mycnt, idx)
        for b in reads:
            b.r[eng] = (kind, mycnt, idx)
        for b in writes:
            b.w = t
            b.r = {}
        return t

    def flush(self, blk):
        for eng in ENGINES:
            ops = self.pend[eng]
            if not ops:
                continue
            self.pend[eng] = []

            def emit(proxy, ops=ops, eng=eng):
                for waits, fn, kind in ops:
                    for pe, pk, pc in waits:
                        if pk == "c":
                            _WAITERS[(pe, "c")](proxy, self.semc[pe], pc)
                        elif pk[0] == "cc":
                            proxy.wait_ge(self.ccsem, 1)
                        else:
                            _WAITERS[(pe, "d")](proxy, self.semd[pe][pk[1]], pc)
                    instr = fn(proxy)
                    if kind == "c":
                        instr.then_inc(self.semc[eng], 1)
                    elif kind[0] == "cc":
                        instr.then_inc(self.ccsem)
                    else:
                        instr.then_inc(self.semd[eng][kind[1]], 16)

            getattr(blk, eng)(emit)

    def barrier(self):
        toks = []
        if getattr(self, "cccnt", 0) > 0:
            toks.append(("gpsimd", ("cc", 0), 1, 0))
        for e in ENGINES:
            if self.cnt[e] > 0:
                toks.append((e, "c", self.cnt[e], self.idx[e]))
            for k in range(NSLOT[e]):
                if self.dcnt[(e, k)] > 0:
                    toks.append((e, ("d", k), self.dcnt[(e, k)], self.idx[e]))
        for e in ENGINES:
            self.add(e, lambda p: p.nop(), extra=toks)


def _blin_swz(w):
    return np.ascontiguousarray(
        np.asarray(w, np.float32).reshape(16, 128, 16, 128)
        .transpose(0, 3, 2, 1).reshape(2048, 2048))


def _ffn_swz(w):
    return np.ascontiguousarray(
        np.asarray(w, np.int8).reshape(16, 128, 64, 128)
        .transpose(2, 1, 0, 3).reshape(8192, 2048))


def build_program(n_pre=NG, n_main=NG, do_prep=True, do_groups=True,
                  do_f1=True, do_f2=True):
    nc = bass.Bass("TRN2", target_bir_lowering=False)
    p = Prog(nc)
    taps = []

    def din(name, shape, dtype):
        return nc.dram_tensor(name, shape, dtype, kind="ExternalInput").ap()

    xm = din("xm", [TOK, H], F32)
    selmask = din("selmask", [128, 8], F32)
    wsw = {m: din(f"w{m}sw", [H, H], F32) for m in "figo"}
    wg8 = din("wg8", [IFF, H], I8)
    wu8 = din("wu8", [IFF, H], I8)
    wd8 = din("wd8", [IFF, H], I8)
    norm_r = {m: din(f"norm{m}_r", [1, H], F32) for m in "fig"}
    normo_fm_h = din("normo_fm_h", [128, KC], F32)
    rmsw_fm_h = din("rmsw_fm_h", [128, KC], F32)
    sgate_fm_h = din("sgate_fm_h", [128, OCN], F32)
    sup_fm_h = din("sup_fm_h", [128, OCN], F32)
    sdown_r = din("sdown_r", [1, H], F32)
    identb_h = din("identb_h", [128, 128], BF16)
    identf_h = din("identf_h", [128, 128], F32)
    onesf_h = din("onesf_h", [1, 128], F32)
    onesb_h = din("onesb_h", [128, 1], BF16)
    onescf_h = din("onescf_h", [128, 1], F32)
    out_d = nc.dram_tensor("out", [TOK, H], F32, kind="ExternalOutput").ap()

    wq_s = {m: nc.dram_tensor(f"wq_{m}", [H, H], BF16, kind="Internal").ap()
            for m in "figo"}
    wdb_s = nc.dram_tensor("wdb", [IFF, H], BF16, kind="Internal").ap()
    xnT_s = nc.dram_tensor("xnT_s", [H, TOK], BF16, kind="Internal").ap()
    sattn_s = nc.dram_tensor("sattn_s", [TOK, H], F32, kind="Internal").ap()
    fuse_s = nc.dram_tensor("fuse_s", [IFF, TOK], BF16, kind="Internal").ap()
    sloc_s = nc.dram_tensor("sloc_s", [4 * 128, KC * G], F32, kind="Internal").ap()
    cprod_s = nc.dram_tensor("cprod_s", [4 * 128, KC * G], F32, kind="Internal").ap()
    xqg_s = nc.dram_tensor("xqg_s", [4 * 128, KC * G], BF16, kind="Internal").ap()
    carry_s = nc.dram_tensor("carry_s", [128, KC], F32, kind="Internal").ap()
    ag_s = nc.dram_tensor("ag_s", [8 * 128, KC], F32, kind="Internal").ap()
    sgrow_s = nc.dram_tensor("sgrow_s", [4, 512], F32, kind="Internal").ap()

    dbuf = {name: Buf() for name in
            ["wq_f", "wq_i", "wq_g", "wq_o", "wdb",
             "xnT", "sattn", "fuse", "out", "sloc", "cprod", "xqg",
             "carry", "ag", "sgrow"]}

    ctx = contextlib.ExitStack()
    for e in ENGINES:
        p.semc[e] = ctx.enter_context(nc.semaphore(f"sc_{e}"))
        p.semd[e] = [ctx.enter_context(nc.semaphore(f"sd_{e}{k}"))
                     for k in range(NSLOT[e])]
    p.ccsem = ctx.enter_context(nc.semaphore("ccsem"))

    def sb(name, shape, dtype, scope):
        h = scope.enter_context(nc.sbuf_tensor(name, shape, dtype))
        return h.ap(), Buf()

    def tap(name, ap, buf, shape, dtype):
        if name not in _TAPS:
            return
        dtype = {"float32": F32, "bfloat16": BF16}[dtype] if isinstance(dtype, str) else dtype
        t = nc.dram_tensor("tap_" + name, shape, dtype, kind="ExternalOutput").ap()
        taps.append(name)
        p.add("gpsimd", lambda g, t=t, ap=ap: g.dma_start(out=t, in_=ap),
              reads=[buf], dma=True)

    identb, b_identb = sb("identb", [128, 128], BF16, ctx)
    identf, b_identf = sb("identf", [128, 128], F32, ctx)
    onesf, b_onesf = sb("onesf", [1, 128], F32, ctx)
    onesb, b_onesb = sb("onesb", [128, 1], BF16, ctx)
    onescf, b_onescf = sb("onescf", [128, 1], F32, ctx)
    normo_fm, b_normo = sb("normo_fm", [128, KC], F32, ctx)
    rmsw_fm, b_rmsw = sb("rmsw_fm", [128, KC], F32, ctx)
    sgate_fm, b_sgate = sb("sgate_fm", [128, OCN], F32, ctx)
    sup_fm, b_sup = sb("sup_fm", [128, OCN], F32, ctx)
    swc, b_swc_t = sb("swc", [128, 8], F32, ctx)
    swcol = {m: swc[:, j:j + 1] for j, m in enumerate("figo")}
    swocol = {m: swc[:, 4 + j:5 + j] for j, m in enumerate("figo")}
    b_swcol = {m: Buf() for m in "figo"}
    b_swocol = {m: Buf() for m in "figo"}

    pall_t = ctx.enter_context(nc.psum_tensor("pall", [128, 4096], F32))
    pall = pall_t.ap()
    pallb = pall.bitcast(BF16)
    ps = [pall[:, 512 * i:512 * (i + 1)] for i in range(8)]
    ps2 = [pall[:, 1024 * i:1024 * (i + 1)] for i in range(4)]
    b_ps = [Buf() for _ in range(8)]

    def psb(i):
        return pallb[:, 1024 * i:1024 * (i + 1)]

    for ap_, buf_, src in [(identb, b_identb, identb_h), (identf, b_identf, identf_h),
                           (onesf, b_onesf, onesf_h), (onesb, b_onesb, onesb_h), (onescf, b_onescf, onescf_h),
                           (normo_fm, b_normo, normo_fm_h), (rmsw_fm, b_rmsw, rmsw_fm_h),
                           (sgate_fm, b_sgate, sgate_fm_h), (sup_fm, b_sup, sup_fm_h)]:
        p.add("sync", lambda s, o=ap_, i=src: s.dma_start(out=o, in_=i),
              writes=[buf_], dma=True)

    blk = ctx.enter_context(nc.Block())

    if do_prep:
        with contextlib.ExitStack() as sc:
            wfp, b_wfp = zip(*[sb(f"wfp{i}", [128, H], F32, sc) for i in range(4)])
            wab, b_wab = zip(*[sb(f"wab{i}", [128, H], F32, sc) for i in range(2)])
            q32, b_q32 = zip(*[sb(f"q32_{i}", [128, H], I32, sc) for i in range(2)])
            qbf, b_qbf = zip(*[sb(f"qbf{i}", [128, H], BF16, sc) for i in range(2)])
            prow, b_prow = sb("prow", [1, 512], F32, sc)

            def s1_iter(m, ec):
                i0 = 2 + ec % 2
                ea = ec % 2
                src = wsw[m]
                p.add("sync", lambda s, o=wfp[i0], i=src[ec * 128:(ec + 1) * 128, :]:
                      s.dma_start(out=o, in_=i), writes=[b_wfp[i0]], dma=True)
                p.add("scalar", lambda a, i0=i0, ea=ea: a.activation(
                    out=wab[ea], in_=wfp[i0], func=AF.Abs),
                    reads=[b_wfp[i0]], writes=[b_wab[ea]])
                for j in range(4):
                    p.add("tensor", lambda t, ea=ea, j=j, st=(ec == 0 and j == 0),
                          sp=(ec == KC - 1 and j == 3):
                          t.matmul(ps[5][0:1, :], onescf,
                                   wab[ea][:, j * 512:(j + 1) * 512],
                                   start=st, stop=sp),
                          reads=[b_wab[ea], b_onescf], writes=[b_ps[5]])

            def s1_tail(m):
                p.add("vector", lambda v: v.tensor_reduce(
                    out=prow[0:1, 0:1], in_=ps[5][0:1, :], axis=AX.X, op=ALU.add),
                    reads=[b_ps[5]], writes=[b_prow])
                p.add("vector", lambda v: v.tensor_scalar(
                    out=prow[0:1, 1:2], in0=prow[0:1, 0:1], scalar1=1.0 / (H * H),
                    scalar2=1e-5, op0=ALU.mult, op1=ALU.max),
                    reads=[b_prow], writes=[b_prow])
                p.add("vector", lambda v: v.tensor_scalar(
                    out=prow[0:1, 2:3], in0=prow[0:1, 1:2], scalar1=1.0 / 127.0,
                    scalar2=None, op0=ALU.mult), reads=[b_prow], writes=[b_prow])
                p.add("vector", lambda v: v.reciprocal(
                    out=prow[0:1, 3:4], in_=prow[0:1, 1:2]),
                    reads=[b_prow], writes=[b_prow])
                p.add("tensor", lambda t: t.matmul(
                    ps[6][:, 0:2], onesf, prow[0:1, 2:4], start=True, stop=True),
                    reads=[b_prow, b_onesf], writes=[b_ps[6]])
                jm = "figo".index(m)
                p.add("vector", lambda v, jm=jm, m=m: v.tensor_copy(
                    out=swc[:, 4 + jm:5 + jm], in_=ps[6][:, 0:1]),
                    reads=[b_ps[6]], writes=[b_swocol[m]])
                p.add("vector", lambda v, jm=jm, m=m: v.tensor_copy(
                    out=swc[:, jm:jm + 1], in_=ps[6][:, 1:2]),
                    reads=[b_ps[6]], writes=[b_swcol[m]])

            def s2_iter(m, ec):
                i0 = ec % 2
                src = wsw[m]
                p.add("sync", lambda s, o=wfp[i0], i=src[ec * 128:(ec + 1) * 128, :]:
                      s.dma_start(out=o, in_=i), writes=[b_wfp[i0]], dma=True)
                p.add("vector", lambda v, i0=i0, m=m: v.tensor_scalar(
                    out=q32[i0], in0=wfp[i0], scalar1=swcol[m], scalar2=None,
                    op0=ALU.mult),
                    reads=[b_wfp[i0], b_swcol[m]], writes=[b_q32[i0]])
                p.add("vector", lambda v, i0=i0: v.tensor_scalar(
                    out=qbf[i0], in0=q32[i0], scalar1=1.0, scalar2=-1.0,
                    op0=ALU.min, op1=ALU.max), reads=[b_q32[i0]], writes=[b_qbf[i0]])
                p.add("gpsimd", lambda g, i0=i0, m=m, ec=ec: g.dma_start(
                    out=wq_s[m][ec * 128:(ec + 1) * 128, :], in_=qbf[i0]),
                    reads=[b_qbf[i0]], writes=[dbuf["wq_" + m]], dma=True)

            order = "figo"
            for ec in range(KC):
                s1_iter("f", ec)
            s1_tail("f")
            for mi in range(4):
                m = order[mi]
                nxt = order[mi + 1] if mi + 1 < 4 else None
                for ec in range(KC):
                    s2_iter(m, ec)
                    if nxt is not None:
                        s1_iter(nxt, ec)
                if nxt is not None:
                    s1_tail(nxt)
            p.barrier()
            p.flush(blk)

    if do_groups:
      with contextlib.ExitStack() as sc:
        normbc, b_normbc = zip(*[sb(f"normbc{i}", [128, H], F32, sc) for i in range(1)])
        A, b_A = sb("Abuf", [128, KC * G], F32, sc)
        sfull, b_s = sb("sfull", [128, KC * G], F32, sc)
        xg, b_xg = zip(*[sb(f"xg{i}", [128, H], F32, sc) for i in range(2)])
        wb, b_wb = sb("wbuf", [128, H], F32, sc)
        xq8 = wb.bitcast(I8)[:, 0:H]
        b_xq8 = b_wb
        xqtm, b_xqtm = zip(*[sb(f"xqtm{i}", [128, H], BF16, sc) for i in range(2)])
        xqT2, b_xqT2_t = sb("xqT2", [128, 2 * KC * G], BF16, sc)
        xqT = [xqT2[:, 0:KC * G], xqT2[:, KC * G:]]
        b_xqT = [Buf(), Buf()]
        yqT, b_yqT = xqT[0], b_xqT[0]
        wqblk, b_wqblk = zip(*[sb(f"wqblk{i}", [128, H], BF16, sc) for i in range(4)])
        ch, b_ch = zip(*[sb(f"ch{i}", [128, 1024], F32, sc) for i in range(6)])
        zf, b_zf = ch[0], b_ch[0]
        zi, b_zi = ch[1], b_ch[1]
        thc, b_thc = ch[2], b_ch[2]
        fc, b_fc = ch[3], b_ch[3]
        fm1, b_fm1 = ch[4], b_ch[4]
        fic, b_fic = ch[5], b_ch[5]
        zg, b_zg = ch[0], b_ch[0]
        thg, b_thg = ch[2], b_ch[2]
        fgc, b_fgc = ch[3], b_ch[3]
        shatc, b_shatc = ch[4], b_ch[4]
        tyc, b_tyc = ch[5], b_ch[5]
        wyc, b_wyc = ch[1], b_ch[1]
        mrun, b_mrun = zip(*[sb(f"mrun{i}", [128, 512], F32, sc) for i in range(2)])
        sc3, b_sc3 = zip(*[sb(f"scbc{i}", [128, 1024], F32, sc) for i in range(3)])
        s2c, b_s2c = zip(*[sb(f"s2c{i}", [128, 512], BF16, sc) for i in range(1)])
        yq8c, b_yq8c = sb("yq8c", [128, 512], I8, sc)
        safm, b_safm = sb("safm", [128, 512], F32, sc)
        sasm, b_sasm = zip(*[sb(f"sasm{i}", [128, 128], F32, sc) for i in range(2)])
        xnstage, b_xnstage = sb("xnstage", [128, 8 * 128], BF16, sc)
        carry, b_carry = sb("carry", [128, KC], F32, sc)
        cols, b_cols = sb("cols", [128, 64], F32, sc)
        rows, b_rows = sb("rows", [1, 1024], F32, sc)

        p.add("vector", lambda v: v.memset(carry, 0.0), writes=[b_carry])

        def ln_pg_loads(xsrc, g, pg):
            for ti in range(2):
                tt = pg * 2 + ti
                r0 = (g % NG) * G + tt * 128
                p.add("sync", lambda s, ti=ti, i=xsrc[r0:r0 + 128, :]:
                      s.dma_start(out=xg[ti], in_=i), writes=[b_xg[ti]], dma=True)
                p.add("vector", lambda v, tt=tt, ti=ti: v.reduce_sum(
                    out=cols[:, tt:tt + 1], in_=xg[ti], axis=AX.X),
                    reads=[b_xg[ti]], writes=[b_cols])
                p.add("scalar", lambda a, tt=tt, ti=ti: a.activation(
                    out=xqtm[ti], in_=xg[ti], func=AF.Square,
                    accum_out=cols[:, 8 + tt:9 + tt]),
                    reads=[b_xg[ti]], writes=[b_xqtm[ti], b_cols])

        def ln_pg_chain(pg):
            t0 = pg * 2
            p.add("vector", lambda v, t0=t0: v.tensor_scalar(
                out=cols[:, 4 + t0:6 + t0], in0=cols[:, t0:t0 + 2],
                scalar1=1.0 / H, scalar2=None, op0=ALU.mult),
                reads=[b_cols], writes=[b_cols])
            p.add("vector", lambda v, t0=t0: v.tensor_tensor(
                out=cols[:, 12 + t0:14 + t0], in0=cols[:, 4 + t0:6 + t0],
                in1=cols[:, 4 + t0:6 + t0], op=ALU.mult),
                reads=[b_cols], writes=[b_cols])
            p.add("vector", lambda v, t0=t0: v.scalar_tensor_tensor(
                out=cols[:, 12 + t0:14 + t0], in0=cols[:, 8 + t0:10 + t0],
                scalar=1.0 / H, in1=cols[:, 12 + t0:14 + t0],
                op0=ALU.mult, op1=ALU.subtract),
                reads=[b_cols], writes=[b_cols])
            p.add("vector", lambda v, t0=t0: v.tensor_scalar(
                out=cols[:, 12 + t0:14 + t0], in0=cols[:, 12 + t0:14 + t0],
                scalar1=1e-5, scalar2=None, op0=ALU.add),
                reads=[b_cols], writes=[b_cols])
            p.add("scalar", lambda a, t0=t0: a.activation(
                out=cols[:, 12 + t0:14 + t0], in_=cols[:, 12 + t0:14 + t0],
                func=AF.Sqrt), reads=[b_cols], writes=[b_cols])
            p.add("vector", lambda v, t0=t0: v.reciprocal(
                out=cols[:, 12 + t0:14 + t0], in_=cols[:, 12 + t0:14 + t0]),
                reads=[b_cols], writes=[b_cols])

        def ln_astore(pg, ti):
            tt = pg * 2 + ti
            p.add("vector", lambda v, tt=tt, ti=ti: v.tensor_scalar(
                out=A[:, tt * H:(tt + 1) * H], in0=xg[ti],
                scalar1=cols[:, 4 + tt:5 + tt],
                scalar2=cols[:, 12 + tt:13 + tt],
                op0=ALU.subtract, op1=ALU.mult),
                reads=[b_xg[ti], b_cols], writes=[b_A])
            p.add("scalar", lambda a, tt=tt, ti=ti: a.activation(
                out=xqtm[ti], in_=A[:, tt * H:(tt + 1) * H], func=AF.Square,
                accum_out=cols[:, 16 + tt:17 + tt]),
                reads=[b_A], writes=[b_xqtm[ti], b_cols])

        def ln_rms_tail():
            p.add("vector", lambda v: v.tensor_scalar(
                out=cols[:, 20:24], in0=cols[:, 16:20],
                scalar1=1.0 / H, scalar2=1e-8, op0=ALU.mult, op1=ALU.add),
                reads=[b_cols], writes=[b_cols])
            p.add("scalar", lambda a: a.activation(
                out=cols[:, 20:24], in_=cols[:, 20:24], func=AF.Sqrt),
                reads=[b_cols], writes=[b_cols])
            p.add("vector", lambda v: v.reciprocal(
                out=cols[:, 20:24], in_=cols[:, 20:24]),
                reads=[b_cols], writes=[b_cols])

        def xnT_tt(g, tt):
            i0 = tt % 2
            p.add("vector", lambda v, i0=i0, tt=tt: v.tensor_copy(
                out=xqtm[i0], in_=A[:, tt * H:(tt + 1) * H]),
                reads=[b_A], writes=[b_xqtm[i0]])
            gl = (g - NG) * G + tt * 128
            for half in range(2):
                for kc4 in range(2):
                    for j in range(4):
                        kc = half * 8 + kc4 * 4 + j
                        p.add("tensor", lambda t, i0=i0, kc=kc, j=j: t.transpose(
                            psb(6)[:, j * 128:(j + 1) * 128],
                            xqtm[i0][:, kc * 128:(kc + 1) * 128], identb),
                            reads=[b_xqtm[i0], b_identb], writes=[b_ps[6]])
                    p.add("scalar", lambda a, kc4=kc4: a.copy(
                        out=xnstage[:, kc4 * 512:(kc4 + 1) * 512],
                        in_=psb(6)[:, 0:512]),
                        reads=[b_ps[6]], writes=[b_xnstage])
                p.add("gpsimd", lambda gp, gl=gl, half=half: gp.dma_start(
                    out=xnT_s[half * 1024:(half + 1) * 1024, gl:gl + 128]
                        .rearrange("(kc p) t -> p kc t", p=128),
                    in_=xnstage.rearrange("p (kc t) -> p kc t", t=128)),
                    reads=[b_xnstage], writes=[dbuf["xnT"]], dma=True)

        def s1_ln_gen(g):
            ln_pg_loads(xm, g, 0)
            yield
            ln_pg_chain(0)
            yield
            ln_astore(0, 0)
            yield
            ln_astore(0, 1)
            xnT_tt(g + NG, 0)
            yield
            ln_pg_loads(xm, g, 1)
            yield
            ln_pg_chain(1)
            xnT_tt(g + NG, 1)
            yield
            ln_astore(1, 0)
            ln_astore(1, 1)
            yield
            ln_rms_tail()
            xnT_tt(g + NG, 2)
            yield
            xnT_tt(g + NG, 3)

        def bcast_scale(row_ap, sidx):
            p.add("tensor", lambda t, row_ap=row_ap: t.matmul(
                ps[6], onesf, row_ap, start=True, stop=True),
                reads=[b_rows, b_onesf], writes=[b_ps[6]])
            p.add("scalar", lambda a, sidx=sidx: a.copy(
                out=sc3[sidx][:, 0:512], in_=ps[6]),
                reads=[b_ps[6]], writes=[b_sc3[sidx]])
            p.add("scalar", lambda a, sidx=sidx: a.copy(
                out=sc3[sidx][:, 512:1024], in_=ps[6]),
                reads=[b_ps[6]], writes=[b_sc3[sidx]])

        def quant(br, dstT, b_dstT, row_only=False):
            nb, b_nb = normbc[0], b_normbc[0]
            for tt in range(TTN):
                p.add("vector", lambda v, tt=tt, nb=nb: v.tensor_tensor(
                    out=wb, in0=A[:, tt * H:(tt + 1) * H], in1=nb, op=ALU.mult),
                    reads=[b_A, b_nb], writes=[b_wb])
                p.add("vector", lambda v, tt=tt: v.reduce_max(
                    out=cols[:, 24 + tt:25 + tt], in_=wb, axis=AX.X,
                    apply_absolute_value=True), reads=[b_wb], writes=[b_cols])
            p.add("vector", lambda v: v.tensor_scalar(
                out=cols[:, 28:32], in0=cols[:, 24:28],
                scalar1=1e-5, scalar2=1.0 / 127.0, op0=ALU.max, op1=ALU.mult),
                reads=[b_cols], writes=[b_cols])
            p.add("vector", lambda v: v.reciprocal(
                out=cols[:, 32:36], in_=cols[:, 28:32]),
                reads=[b_cols], writes=[b_cols])
            p.add("vector", lambda v: v.tensor_tensor(
                out=cols[:, 36:40], in0=cols[:, 28:32], in1=cols[:, 20:24],
                op=ALU.mult), reads=[b_cols], writes=[b_cols])
            p.add("vector", lambda v, br=br: v.tensor_scalar(
                out=cols[:, 36:40], in0=cols[:, 36:40],
                scalar1=swocol[br], scalar2=127.0, op0=ALU.mult, op1=ALU.mult),
                reads=[b_cols, b_swocol[br]], writes=[b_cols])
            for tt in range(TTN):
                p.add("vector", lambda v, tt=tt, nb=nb: v.scalar_tensor_tensor(
                    out=xq8, in0=A[:, tt * H:(tt + 1) * H],
                    scalar=cols[:, 32 + tt:33 + tt], in1=nb,
                    op0=ALU.mult, op1=ALU.mult),
                    reads=[b_A, b_cols, b_nb], writes=[b_xq8])
                i0 = tt % 2
                p.add("scalar", lambda a, i0=i0: a.copy(
                    out=xqtm[i0], in_=xq8), reads=[b_xq8], writes=[b_xqtm[i0]])
                for kc4 in range(4):
                    for j in range(4):
                        kc = kc4 * 4 + j
                        p.add("tensor", lambda t, i0=i0, kc=kc, j=j: t.transpose(
                            psb(4)[:, j * 128:(j + 1) * 128],
                            xqtm[i0][:, kc * 128:(kc + 1) * 128], identb),
                            reads=[b_xqtm[i0], b_identb], writes=[b_ps[4]])
                    p.add("scalar", lambda a, kc4=kc4, tt=tt, dstT=dstT: a.copy(
                        out=dstT.rearrange("p (kc t) -> p kc t", t=G)
                            [:, kc4 * 4:(kc4 + 1) * 4, tt * 128:(tt + 1) * 128],
                        in_=psb(4)[:, 0:512].rearrange("p (kc t) -> p kc t", t=128)),
                        reads=[b_ps[4]], writes=[b_dstT])
            for tt in range(TTN):
                p.add("tensor", lambda t, tt=tt: t.transpose(
                    ps[5][0:1, tt * 128:(tt + 1) * 128],
                    cols[:, 36 + tt:37 + tt], identf),
                    reads=[b_cols, b_identf], writes=[b_ps[5]])
            p.add("scalar", lambda a: a.copy(
                out=rows[0:1, 0:512], in_=ps[5][0:1, 0:512]),
                reads=[b_ps[5]], writes=[b_rows])
            if row_only:
                return
            sidx = {"f": 0, "i": 1, "g": 2}[br]
            bcast_scale(rows[0:1, 0:512], sidx)

        def wq_stream(m, ec, slot):
            p.add("sync", lambda s, o=wqblk[slot], i=wq_s[m][ec * 128:(ec + 1) * 128, :]:
                  s.dma_start(out=o, in_=i),
                  reads=[dbuf["wq_" + m]], writes=[b_wqblk[slot]], dma=True)

        def blin_mm(srcT, b_srcT, wslot, bank, chunk):
            for kc in range(KC):
                p.add("tensor", lambda t, kc=kc: t.matmul(
                    ps[bank], wqblk[wslot][:, kc * 128:(kc + 1) * 128],
                    srcT[:, kc * G:kc * G + G],
                    start=(kc == 0), stop=(kc == KC - 1)),
                    reads=[b_wqblk[wslot], b_srcT], writes=[b_ps[bank]])

        sout = [sfull[:, 0:1024], sfull[:, 1024:2048]]
        cout = [sfull[:, 2048:3072], sfull[:, 3072:4096]]
        b_sout = [Buf(), Buf()]
        b_cout = [Buf(), Buf()]
        onestile, b_ones1 = sb("onestile", [128, 512], BF16, sc)
        smsb, b_smsb = sb("smsb", [128, 8], F32, sc)
        agsb, b_agsb = sb("agsb", [128, 128], F32, sc)
        s_in, b_s_in = zip(*[sb(f"s_in{i}", [128, KC], F32, sc) for i in range(2)])
        ccarry, b_ccarry = sb("ccarry", [128, KC], F32, sc)

        p.add("vector", lambda v: v.memset(onestile, 1.0), writes=[b_ones1])
        p.add("vector", lambda v: v.memset(ccarry, 1.0), writes=[b_ccarry])
        p.add("sync", lambda s: s.dma_start(out=smsb, in_=selmask),
              writes=[b_smsb], dma=True)

        def s1_ln(g):
            for _ in s1_ln_gen(g):
                pass
            if g == 0:
                tap("xn", A, b_A, [128, KC * G], "float32")

        def s1_quants(g):
            p.add("sync", lambda s: s.dma_start(
                out=normbc[0], in_=norm_r["g"].partition_broadcast(128)),
                writes=[b_normbc[0]], dma=True)
            quant("g", xqT[0], b_xqT[0], row_only=True)
            p.add("gpsimd", lambda gp, g=g: gp.dma_start(
                out=sgrow_s[g:g + 1, :], in_=rows[0:1, 0:512]),
                reads=[b_rows], writes=[dbuf["sgrow"]], dma=True)
            p.add("gpsimd", lambda gp, g=g: gp.dma_start(
                out=xqg_s[g * 128:(g + 1) * 128, :], in_=xqT[0]),
                reads=[b_xqT[0]], writes=[dbuf["xqg"]], dma=True)
            p.add("sync", lambda s: s.dma_start(
                out=normbc[0], in_=norm_r["f"].partition_broadcast(128)),
                writes=[b_normbc[0]], dma=True)
            quant("f", xqT[0], b_xqT[0])
            if g == 0:
                tap("xqTf", xqT[0], b_xqT[0], [128, KC * G], "bfloat16")
            p.add("sync", lambda s: s.dma_start(
                out=normbc[0], in_=norm_r["i"].partition_broadcast(128)),
                writes=[b_normbc[0]], dma=True)
            quant("i", xqT[1], b_xqT[1])

        def s1_loop(g, next_g):
            gen = s1_ln_gen(next_g) if next_g is not None else None
            for ep in range(KC // 2):
                pset = ep % 2
                ca, cb = 2 * ep, 2 * ep + 1
                sf = 2 * pset
                si = 2 - 2 * pset
                wq_stream("f", ca, sf)
                wq_stream("f", cb, sf + 1)
                wq_stream("i", ca, si)
                wq_stream("i", cb, si + 1)
                blin_mm(xqT[0], b_xqT[0], sf, 2 * pset + 0, ca)
                blin_mm(xqT[0], b_xqT[0], sf + 1, 2 * pset + 1, cb)
                blin_mm(xqT[1], b_xqT[1], si, 2 * pset + 4, ca)
                blin_mm(xqT[1], b_xqT[1], si + 1, 2 * pset + 5, cb)
                p.add("vector", lambda v, pset=pset: v.tensor_tensor(
                    out=zf, in0=ps2[pset], in1=sc3[0], op=ALU.mult),
                    reads=[b_ps[2 * pset], b_ps[2 * pset + 1], b_sc3[0]],
                    writes=[b_zf])
                p.add("vector", lambda v, pset=pset: v.tensor_tensor(
                    out=zi, in0=ps2[2 + pset], in1=sc3[1], op=ALU.mult),
                    reads=[b_ps[4 + 2 * pset], b_ps[5 + 2 * pset], b_sc3[1]],
                    writes=[b_zi])
                p.add("scalar", lambda a: a.activation(
                    out=thc, in_=zf, func=AF.Tanh, scale=0.5),
                    reads=[b_zf], writes=[b_thc])
                p.add("gpsimd", lambda gp: gp.tensor_scalar(
                    out=fc, in0=thc, scalar1=0.5, scalar2=0.5,
                    op0=ALU.mult, op1=ALU.add), reads=[b_thc], writes=[b_fc])
                p.add("gpsimd", lambda gp: gp.tensor_scalar(
                    out=fm1, in0=fc, scalar1=-1.0, scalar2=1.0,
                    op0=ALU.mult, op1=ALU.add), reads=[b_fc], writes=[b_fm1])
                p.add("scalar", lambda a: a.activation(
                    out=thc, in_=zi, func=AF.Tanh, scale=0.5),
                    reads=[b_zi], writes=[b_thc])
                p.add("gpsimd", lambda gp: gp.tensor_scalar(
                    out=fic, in0=thc, scalar1=0.5, scalar2=0.5,
                    op0=ALU.mult, op1=ALU.add), reads=[b_thc], writes=[b_fic])
                p.add("gpsimd", lambda gp: gp.tensor_tensor(
                    out=thc, in0=fic, in1=fm1, op=ALU.mult),
                    reads=[b_fic, b_fm1], writes=[b_thc])
                if gen is not None:
                    next(gen, None)
                p.add("vector", lambda v: v.tensor_tensor(
                    out=zf, in0=zi, in1=thc, op=ALU.mult),
                    reads=[b_zi, b_thc], writes=[b_zf])
                so, bso = sout[pset], b_sout[pset]
                co, bco = cout[pset], b_cout[pset]
                for sub, cx in ((0, ca), (1, cb)):
                    p.add("vector", lambda v, sub=sub, cx=cx, so=so: v.tensor_tensor_scan(
                        out=so[:, sub * G:(sub + 1) * G],
                        data0=fc[:, sub * G:(sub + 1) * G],
                        data1=zf[:, sub * G:(sub + 1) * G],
                        initial=carry[:, cx:cx + 1], op0=ALU.mult, op1=ALU.add),
                        reads=[b_fc, b_zf, b_carry], writes=[bso])
                    p.add("vector", lambda v, sub=sub, cx=cx, co=co: v.tensor_tensor_scan(
                        out=co[:, sub * G:(sub + 1) * G],
                        data0=fc[:, sub * G:(sub + 1) * G],
                        data1=onestile,
                        initial=ccarry[:, cx:cx + 1], op0=ALU.mult, op1=ALU.mult),
                        reads=[b_fc, b_ones1, b_ccarry], writes=[bco])
                p.add("vector", lambda v, ca=ca, so=so: v.tensor_copy(
                    out=carry[:, ca:ca + 2],
                    in_=so.rearrange("p (c t) -> p c t", t=G)
                        [:, 0:2, G - 1:G].rearrange("p c t -> p (c t)")),
                    reads=[bso], writes=[b_carry])
                p.add("vector", lambda v, ca=ca, co=co: v.tensor_copy(
                    out=ccarry[:, ca:ca + 2],
                    in_=co.rearrange("p (c t) -> p c t", t=G)
                        [:, 0:2, G - 1:G].rearrange("p c t -> p (c t)")),
                    reads=[bco], writes=[b_ccarry])
                r0 = g * 128
                p.add("gpsimd", lambda gp, r0=r0, ep=ep, so=so: gp.dma_start(
                    out=sloc_s[r0:r0 + 128, ep * 1024:(ep + 1) * 1024], in_=so),
                    reads=[bso], writes=[dbuf["sloc"]], dma=True)
                p.add("gpsimd", lambda gp, r0=r0, ep=ep, co=co: gp.dma_start(
                    out=cprod_s[r0:r0 + 128, ep * 1024:(ep + 1) * 1024], in_=co),
                    reads=[bco], writes=[dbuf["cprod"]], dma=True)
            if gen is not None:
                for _ in gen:
                    pass

        def s1_group_driver():
            s1_ln(0)
            s1_quants(0)
            for g in range(4):
                s1_loop(g, g + 1 if g < 3 else None)
                if g < 3:
                    s1_quants(g + 1)

        def exchange():
            p.add("gpsimd", lambda gp: gp.dma_start(out=carry_s, in_=carry),
                  reads=[b_carry], writes=[dbuf["carry"]], dma=True)
            p.add("gpsimd", lambda gp: gp.collective_compute(
                "AllGather", ALU.bypass, replica_groups=[list(range(8))],
                ins=[carry_s.opt()], outs=[ag_s.opt()]),
                reads=[dbuf["carry"]], writes=[dbuf["ag"]], cc=True)
            p.add("sync", lambda s: s.dma_start(
                out=agsb.rearrange("p (j c) -> p j c", c=KC),
                in_=ag_s.rearrange("(j p) c -> p j c", p=128)),
                reads=[dbuf["ag"]], writes=[b_agsb], dma=True)
            p.add("vector", lambda v: v.tensor_scalar(
                out=s_in[0], in0=agsb[:, 0:KC], scalar1=smsb[:, 0:1],
                scalar2=None, op0=ALU.mult),
                reads=[b_agsb, b_smsb], writes=[b_s_in[0]])
            for j in range(1, 8):
                p.add("vector", lambda v, j=j: v.scalar_tensor_tensor(
                    out=s_in[j % 2], in0=agsb[:, j * KC:(j + 1) * KC],
                    scalar=smsb[:, j:j + 1], in1=s_in[(j + 1) % 2],
                    op0=ALU.mult, op1=ALU.add),
                    reads=[b_agsb, b_smsb, b_s_in[(j + 1) % 2]],
                    writes=[b_s_in[j % 2]])

        sfin, b_sfin = s_in[7 % 2], b_s_in[7 % 2]

        def d_load(g):
            r0 = g * 128
            p.add("scalar", lambda a, r0=r0: a.dma_start(
                out=sfull, in_=sloc_s[r0:r0 + 128, :]),
                reads=[dbuf["sloc"]],
                writes=[b_s, b_sout[0], b_sout[1], b_cout[0], b_cout[1]], dma=True)
            p.add("scalar", lambda a, r0=r0: a.dma_start(
                out=A, in_=cprod_s[r0:r0 + 128, :]),
                reads=[dbuf["cprod"]], writes=[b_A], dma=True)

        def d_fix_head(g):
            gq, b_gq = xqT[g % 2], b_xqT[g % 2]
            r0 = g * 128
            p.add("scalar", lambda a, r0=r0, gq=gq: a.dma_start(
                out=gq, in_=xqg_s[r0:r0 + 128, :]),
                reads=[dbuf["xqg"]], writes=[b_gq], dma=True)
            p.add("sync", lambda s, g=g: s.dma_start(
                out=rows[0:1, 0:512], in_=sgrow_s[g:g + 1, :]),
                reads=[dbuf["sgrow"]], writes=[b_rows], dma=True)
            bcast_scale(rows[0:1, 0:512], 2)

        def d_fix_slices(g):
            for ep in range(KC // 2):
                for sub in range(2):
                    cx = 2 * ep + sub
                    p.add("vector", lambda v, ep=ep, sub=sub, cx=cx:
                          v.scalar_tensor_tensor(
                              out=zf[:, sub * G:(sub + 1) * G],
                              in0=A[:, cx * G:(cx + 1) * G],
                              scalar=sfin[:, cx:cx + 1],
                              in1=sfull[:, cx * G:(cx + 1) * G],
                              op0=ALU.mult, op1=ALU.add),
                          reads=[b_A, b_sfin, b_s], writes=[b_zf])
                wins = [b_sout[0], b_sout[1], b_cout[0], b_cout[1]]
                ww = [b_s] + ([wins[ep]] if ep < 4 else [])
                p.add("scalar", lambda a, ep=ep: a.copy(
                    out=sfull[:, ep * 1024:(ep + 1) * 1024], in_=zf),
                    reads=[b_zf], writes=ww)
                for sub in range(2):
                    ec = 2 * ep + sub
                    p.add("scalar", lambda a, ec=ec: a.activation(
                        out=s2c[0], in_=sfull[:, ec * G:(ec + 1) * G],
                        func=AF.Square), reads=[b_s], writes=[b_s2c[0]])
                    p.add("tensor", lambda t, ec=ec: t.matmul(
                        ps[3][0:1, :], onesb, s2c[0],
                        start=(ec == 0), stop=(ec == KC - 1)),
                        reads=[b_s2c[0], b_onesb], writes=[b_ps[3]])
                yield
            if g == 0:
                tap("s_main", sfull, b_s, [128, KC * G], "float32")

        def d_fix_tail(g):
            p.add("vector", lambda v: v.tensor_scalar(
                out=rows[0:1, 512:1024], in0=ps[3][0:1, :], scalar1=1.0 / H,
                scalar2=1e-5, op0=ALU.mult, op1=ALU.add),
                reads=[b_ps[3]], writes=[b_rows])
            p.add("scalar", lambda a: a.activation(
                out=rows[0:1, 512:1024], in_=rows[0:1, 512:1024], func=AF.Sqrt),
                reads=[b_rows], writes=[b_rows])
            p.add("vector", lambda v: v.reciprocal(
                out=rows[0:1, 512:1024], in_=rows[0:1, 512:1024]),
                reads=[b_rows], writes=[b_rows])
            bcast_scale(rows[0:1, 512:1024], 0)
        def d_part1(g):
            rstdbc, b_rstdbc = sc3[0], b_sc3[0]
            zg2 = [zg, wb[:, 0:1024]]
            b_zg2 = [b_zg, b_wb]
            sh2 = [shatc, wb[:, 1024:2048]]
            b_sh2 = [b_shatc, b_wb]
            ty2 = [tyc, xqtm[0].bitcast(F32)[:, 0:1024]]
            b_ty2 = [b_tyc, b_xqtm[0]]

            def front(i):
                pset = i % 2
                pi = i % 2
                ca, cb = 2 * i, 2 * i + 1
                sg_ = 2 * pset
                wq_stream("g", ca, sg_)
                wq_stream("g", cb, sg_ + 1)
                blin_mm(xqT[g % 2], b_xqT[g % 2], sg_, 2 * pset + 0, ca)
                blin_mm(xqT[g % 2], b_xqT[g % 2], sg_ + 1, 2 * pset + 1, cb)
                p.add("vector", lambda v, pset=pset, pi=pi: v.tensor_tensor(
                    out=zg2[pi], in0=ps2[pset], in1=sc3[2], op=ALU.mult),
                    reads=[b_ps[2 * pset], b_ps[2 * pset + 1], b_sc3[2]],
                    writes=[b_zg2[pi]])
                for sub, cx in ((0, ca), (1, cb)):
                    p.add("vector", lambda v, sub=sub, cx=cx, pi=pi:
                          v.scalar_tensor_tensor(
                              out=sh2[pi][:, sub * G:(sub + 1) * G],
                              in0=sfull[:, cx * G:(cx + 1) * G],
                              scalar=rmsw_fm[:, cx:cx + 1], in1=rstdbc[:, 0:512],
                              op0=ALU.mult, op1=ALU.mult),
                          reads=[b_s, b_rmsw, b_rstdbc], writes=[b_sh2[pi]])
                p.add("scalar", lambda a, pi=pi: a.activation(
                    out=thg, in_=zg2[pi], func=AF.Tanh, scale=0.5),
                    reads=[b_zg2[pi]], writes=[b_thg])
                p.add("gpsimd", lambda gp: gp.tensor_scalar(
                    out=fgc, in0=thg, scalar1=0.5, scalar2=0.5,
                    op0=ALU.mult, op1=ALU.add), reads=[b_thg], writes=[b_fgc])
                p.add("gpsimd", lambda gp, pi=pi: gp.tensor_tensor(
                    out=ty2[pi], in0=fgc, in1=sh2[pi], op=ALU.mult),
                    reads=[b_fgc, b_sh2[pi]], writes=[b_ty2[pi]])

            def tail(i):
                pi = i % 2
                ca, cb = 2 * i, 2 * i + 1
                p.add("vector", lambda v, ca=ca, pi=pi: v.tensor_tensor(
                    out=A[:, ca * G:(ca + 2) * G], in0=zg2[pi], in1=ty2[pi],
                    op=ALU.mult), reads=[b_zg2[pi], b_ty2[pi]], writes=[b_A])
                for sub, cx in ((0, ca), (1, cb)):
                    p.add("vector", lambda v, sub=sub, cx=cx: v.tensor_scalar(
                        out=wyc[:, sub * G:(sub + 1) * G],
                        in0=A[:, cx * G:(cx + 1) * G],
                        scalar1=normo_fm[:, cx:cx + 1], scalar2=None,
                        op0=ALU.mult),
                        reads=[b_A, b_normo], writes=[b_wyc])
                if i == 0:
                    p.add("vector", lambda v: v.tensor_tensor(
                        out=mrun[0], in0=wyc[:, 0:512], in1=wyc[:, 512:1024],
                        op=ALU.max), reads=[b_wyc], writes=[b_mrun[0]])
                    p.add("vector", lambda v: v.tensor_tensor(
                        out=mrun[1], in0=wyc[:, 0:512], in1=wyc[:, 512:1024],
                        op=ALU.min), reads=[b_wyc], writes=[b_mrun[1]])
                else:
                    for sub in range(2):
                        p.add("vector", lambda v, sub=sub: v.tensor_tensor(
                            out=mrun[0], in0=mrun[0],
                            in1=wyc[:, sub * G:(sub + 1) * G], op=ALU.max),
                            reads=[b_wyc, b_mrun[0]], writes=[b_mrun[0]])
                        p.add("vector", lambda v, sub=sub: v.tensor_tensor(
                            out=mrun[1], in0=mrun[1],
                            in1=wyc[:, sub * G:(sub + 1) * G], op=ALU.min),
                            reads=[b_wyc, b_mrun[1]], writes=[b_mrun[1]])

            def acc(i):
                for cx in (2 * i, 2 * i + 1):
                    p.add("scalar", lambda a, cx=cx: a.activation(
                        out=s2c[0], in_=A[:, cx * G:(cx + 1) * G],
                        func=AF.Square), reads=[b_A], writes=[b_s2c[0]])
                    p.add("tensor", lambda t, cx=cx: t.matmul(
                        ps[5][0:1, :], onesb, s2c[0],
                        start=(cx == 0), stop=(cx == KC - 1)),
                        reads=[b_s2c[0], b_onesb], writes=[b_ps[5]])

            front(0)
            for i in range(KC // 2):
                if i + 1 < KC // 2:
                    front(i + 1)
                tail(i)
                if i - 1 >= 0:
                    acc(i - 1)
            acc(KC // 2 - 1)
            if g == 0:
                tap("y", A, b_A, [128, KC * G], "float32")
            p.add("vector", lambda v: v.tensor_scalar(
                out=rows[0:1, 512:1024], in0=ps[5][0:1, :], scalar1=1.0 / H,
                scalar2=1e-8, op0=ALU.mult, op1=ALU.add),
                reads=[b_ps[5]], writes=[b_rows])
            p.add("scalar", lambda a: a.activation(
                out=rows[0:1, 512:1024], in_=rows[0:1, 512:1024], func=AF.Sqrt),
                reads=[b_rows], writes=[b_rows])
            p.add("vector", lambda v: v.reciprocal(
                out=rows[0:1, 512:1024], in_=rows[0:1, 512:1024]),
                reads=[b_rows], writes=[b_rows])
            p.add("vector", lambda v: v.tensor_scalar(
                out=mrun[1], in0=mrun[1], scalar1=-1.0, scalar2=None,
                op0=ALU.mult), reads=[b_mrun[1]], writes=[b_mrun[1]])
            p.add("vector", lambda v: v.tensor_tensor(
                out=mrun[0], in0=mrun[0], in1=mrun[1], op=ALU.max),
                reads=[b_mrun[0], b_mrun[1]], writes=[b_mrun[0]])
            for j in range(4):
                p.add("tensor", lambda t, j=j: t.transpose(
                    ps[7][:, j * 128:(j + 1) * 128],
                    mrun[0][:, j * 128:(j + 1) * 128], identf),
                    reads=[b_mrun[0], b_identf], writes=[b_ps[7]])
            for j in range(4):
                p.add("vector", lambda v, j=j: v.reduce_max(
                    out=cols[:, 40 + j:41 + j], in_=ps[7][:, j * 128:(j + 1) * 128],
                    axis=AX.X), reads=[b_ps[7]], writes=[b_cols])
            p.add("vector", lambda v: v.tensor_scalar(
                out=cols[:, 44:48], in0=cols[:, 40:44],
                scalar1=1e-5, scalar2=1.0 / 127.0, op0=ALU.max, op1=ALU.mult),
                reads=[b_cols], writes=[b_cols])
            p.add("vector", lambda v: v.reciprocal(
                out=cols[:, 48:52], in_=cols[:, 44:48]),
                reads=[b_cols], writes=[b_cols])
            p.add("vector", lambda v: v.tensor_scalar(
                out=cols[:, 52:56], in0=cols[:, 44:48],
                scalar1=swocol["o"], scalar2=127.0, op0=ALU.mult, op1=ALU.mult),
                reads=[b_cols, b_swocol["o"]], writes=[b_cols])
            for j in range(4):
                p.add("tensor", lambda t, j=j: t.transpose(
                    ps[5][0:1, j * 128:(j + 1) * 128], cols[:, 48 + j:49 + j],
                    identf), reads=[b_cols, b_identf], writes=[b_ps[5]])
            p.add("scalar", lambda a: a.copy(
                out=rows[0:1, 0:512], in_=ps[5][0:1, 0:512]),
                reads=[b_ps[5]], writes=[b_rows])
            bcast_scale(rows[0:1, 0:512], 2)
            for j in range(4):
                p.add("tensor", lambda t, j=j: t.transpose(
                    ps[5][0:1, j * 128:(j + 1) * 128], cols[:, 52 + j:53 + j],
                    identf), reads=[b_cols, b_identf], writes=[b_ps[5]])
            p.add("scalar", lambda a: a.copy(
                out=rows[0:1, 0:512], in_=ps[5][0:1, 0:512]),
                reads=[b_ps[5]], writes=[b_rows])
            p.add("vector", lambda v: v.tensor_tensor(
                out=rows[0:1, 0:512], in0=rows[0:1, 0:512],
                in1=rows[0:1, 512:1024], op=ALU.mult),
                reads=[b_rows], writes=[b_rows])
            bcast_scale(rows[0:1, 0:512], 1)
            yq82 = [yq8c, s2c[0].bitcast(I8)[:, 0:512]]
            b_yq82 = [b_yq8c, b_s2c[0]]
            for ec in range(KC):
                e2 = ec % 2
                p.add("scalar", lambda a, e2=e2, ec=ec: a.activation(
                    out=wyc[:, e2 * G:(e2 + 1) * G],
                    in_=A[:, ec * G:(ec + 1) * G], func=AF.Copy,
                    scale=normo_fm[:, ec:ec + 1]),
                    reads=[b_A, b_normo], writes=[b_wyc])
                p.add("vector", lambda v, e2=e2: v.tensor_tensor(
                    out=yq82[e2], in0=wyc[:, e2 * G:(e2 + 1) * G],
                    in1=sc3[2][:, 0:512], op=ALU.mult),
                    reads=[b_wyc, b_sc3[2]], writes=[b_yq82[e2]])
                if e2 == 0:
                    p.add("gpsimd", lambda gp, ec=ec, g=g, e2=e2: gp.tensor_copy(
                        out=xqT[g % 2][:, ec * G:(ec + 1) * G], in_=yq82[e2]),
                        reads=[b_yq82[e2]], writes=[b_xqT[g % 2]])
                else:
                    p.add("scalar", lambda a, ec=ec, g=g, e2=e2: a.activation(
                        out=xqT[g % 2][:, ec * G:(ec + 1) * G], in_=yq82[e2],
                        func=AF.Copy),
                        reads=[b_yq82[e2]], writes=[b_xqT[g % 2]])
            if g == 0:
                tap("yqT", yqT, b_yqT, [128, KC * G], "bfloat16")
        def d_part2(g, fixgen):
            yqg, b_yqg = xqT[g % 2], b_xqT[g % 2]
            for ep in range(KC // 2):
                pset = ep % 2
                ca, cb = 2 * ep, 2 * ep + 1
                so_ = 2 - 2 * pset
                wq_stream("o", ca, so_)
                wq_stream("o", cb, so_ + 1)
                blin_mm(yqg, b_yqg, so_, 4 + 2 * pset, ca)
                blin_mm(yqg, b_yqg, so_ + 1, 5 + 2 * pset, cb)
                if fixgen is not None:
                    next(fixgen, None)
                for sub, cx in ((0, ca), (1, cb)):
                    bank = 4 + 2 * pset + sub
                    p.add("vector", lambda v, bank=bank: v.tensor_tensor(
                        out=safm, in0=ps[bank], in1=sc3[1][:, 0:512],
                        op=ALU.mult),
                        reads=[b_ps[bank], b_sc3[1]], writes=[b_safm])
                    for j in range(4):
                        p.add("tensor", lambda t, j=j: t.transpose(
                            ps[0][:, j * 128:j * 128 + 128],
                            safm[:, j * 128:(j + 1) * 128], identf),
                            reads=[b_safm, b_identf], writes=[b_ps[0]])
                    p.add("scalar", lambda a, sub=sub: a.copy(
                        out=mrun[sub], in_=ps[0][:, 0:512]),
                        reads=[b_ps[0]], writes=[b_mrun[sub]])
                    p.add("gpsimd", lambda gp, sub=sub, cx=cx, g=g: gp.dma_start(
                        out=sattn_s[g * G:(g + 1) * G, cx * 128:(cx + 1) * 128]
                            .rearrange("(j q) c -> q j c", q=128),
                        in_=mrun[sub].rearrange("q (j c) -> q j c", c=128)),
                        reads=[b_mrun[sub]], writes=[dbuf["sattn"]], dma=True)
            if fixgen is not None:
                for _ in fixgen:
                    pass
            if g == 0:
                tap("sattn0", safm, b_safm, [128, 512], "float32")

        s1_group_driver()
        d_load(0)
        d_fix_head(0)
        exchange()
        for _ in d_fix_slices(0):
            pass
        d_fix_tail(0)
        for g in range(4):
            d_part1(g)
            if g + 1 < 4:
                d_load(g + 1)
                d_fix_head(g + 1)
                fixgen = d_fix_slices(g + 1)
            else:
                fixgen = None
            d_part2(g, fixgen)
            if g + 1 < 4:
                d_fix_tail(g + 1)
        p.barrier()
        p.flush(blk)

    if do_f1:
        with contextlib.ExitStack() as sc:
            xnT_sb, b_xnT_sb = sb("xnT_sb", [128, KC * TOK], BF16, sc)
            wg8t, b_wg8t = zip(*[sb(f"wg8t{i}", [128, H], I8, sc) for i in range(4)])
            wu8t, b_wu8t = zip(*[sb(f"wu8t{i}", [128, H], I8, sc) for i in range(4)])
            wgblk, b_wgblk = zip(*[sb(f"wgblk{i}", [128, H], BF16, sc) for i in range(3)])
            wublk, b_wublk = zip(*[sb(f"wublk{i}", [128, H], BF16, sc) for i in range(3)])
            gbf, b_gbf = zip(*[sb(f"gbf{i}", [128, 512], F32, sc) for i in range(4)])
            ubf, b_ubf = zip(*[sb(f"ubf{i}", [128, 512], F32, sc) for i in range(4)])
            fbf, b_fbf = zip(*[sb(f"fbf{i}", [128, 512], BF16, sc) for i in range(4)])
            cengs = ["vector", "scalar"]

            def conv(eng, out, in_, reads, writes):
                if eng == "scalar":
                    p.add(eng, lambda a, o=out, i=in_: a.activation(
                        out=o, in_=i, func=AF.Copy), reads=reads, writes=writes)
                else:
                    p.add(eng, lambda v, o=out, i=in_: v.tensor_copy(
                        out=o, in_=i), reads=reads, writes=writes)

            for kc in range(KC):
                p.add("sync", lambda s, kc=kc: s.dma_start(
                    out=xnT_sb[:, kc * TOK:(kc + 1) * TOK],
                    in_=xnT_s[kc * 128:(kc + 1) * 128, :]),
                    reads=[dbuf["xnT"]], writes=[b_xnT_sb], dma=True)
            for oc in range(OCN):
                o2 = oc % 3
                i4 = oc % 4
                sl = slice(oc * 128, (oc + 1) * 128)
                p.add("sync", lambda s, o=wg8t[i4], i=wg8[sl, :]:
                      s.dma_start(out=o, in_=i), writes=[b_wg8t[i4]], dma=True)
                p.add("sync", lambda s, o=wu8t[i4], i=wu8[sl, :]:
                      s.dma_start(out=o, in_=i), writes=[b_wu8t[i4]], dma=True)
                conv(cengs[oc % 2], wgblk[o2], wg8t[i4],
                     [b_wg8t[i4]], [b_wgblk[o2]])
                conv(cengs[(oc + 1) % 2], wublk[o2], wu8t[i4],
                     [b_wu8t[i4]], [b_wublk[o2]])
                for hc in range(KC):
                    for tt in range(4):
                        p.add("tensor", lambda t, hc=hc, tt=tt, o2=o2:
                              t.matmul(ps[tt],
                                       wgblk[o2][:, hc * 128:(hc + 1) * 128],
                                       xnT_sb[:, hc * TOK + tt * 512:
                                              hc * TOK + tt * 512 + 512],
                                       start=(hc == 0), stop=(hc == KC - 1)),
                              reads=[b_wgblk[o2], b_xnT_sb],
                              writes=[b_ps[tt]])
                for hc in range(KC):
                    for tt in range(4):
                        p.add("tensor", lambda t, hc=hc, tt=tt, o2=o2:
                              t.matmul(ps[4 + tt],
                                       wublk[o2][:, hc * 128:(hc + 1) * 128],
                                       xnT_sb[:, hc * TOK + tt * 512:
                                              hc * TOK + tt * 512 + 512],
                                       start=(hc == 0), stop=(hc == KC - 1)),
                              reads=[b_wublk[o2], b_xnT_sb],
                              writes=[b_ps[4 + tt]])
                for tt in range(4):
                    p.add("scalar", lambda a, tt=tt, oc=oc:
                          a.activation(out=gbf[tt], in_=ps[tt], func=AF.Silu,
                                       scale=sgate_fm[:, oc:oc + 1]),
                          reads=[b_ps[tt], b_sgate], writes=[b_gbf[tt]])
                    p.add("vector", lambda v, tt=tt, oc=oc:
                          v.tensor_scalar(out=ubf[tt], in0=ps[4 + tt],
                                          scalar1=sup_fm[:, oc:oc + 1],
                                          scalar2=None, op0=ALU.mult),
                          reads=[b_ps[4 + tt], b_sup], writes=[b_ubf[tt]])
                    p.add("vector", lambda v, tt=tt: v.tensor_tensor(
                        out=fbf[tt], in0=gbf[tt], in1=ubf[tt], op=ALU.mult),
                        reads=[b_gbf[tt], b_ubf[tt]], writes=[b_fbf[tt]])
                    p.add("gpsimd", lambda gp, oc=oc, tt=tt: gp.dma_start(
                        out=fuse_s[oc * 128:(oc + 1) * 128,
                                   tt * 512:(tt + 1) * 512],
                        in_=fbf[tt]),
                        reads=[b_fbf[tt]], writes=[dbuf["fuse"]], dma=True)
            p.barrier()
            p.flush(blk)

    for m in "figo":
        if do_prep and f"wq_{m}" in _TAPS:
            t = nc.dram_tensor(f"tap_wq_{m}", [H, H], BF16,
                               kind="ExternalOutput").ap()
            taps.append(f"wq_{m}")
            p.add("gpsimd", lambda g, t=t, m=m: g.dma_start(out=t, in_=wq_s[m]),
                  reads=[dbuf["wq_" + m]], dma=True)
    if do_prep and "swc" in _TAPS:
        t = nc.dram_tensor("tap_swc", [128, 8], F32, kind="ExternalOutput").ap()
        taps.append("swc")
        p.add("gpsimd", lambda g, t=t: g.dma_start(out=t, in_=swc),
              reads=[b_swcol["o"], b_swocol["o"]], dma=True)
    p.flush(blk)
    if do_f1 and "fuse_full" in _TAPS:
        t = nc.dram_tensor("tap_fuse_full", [IFF, TOK], BF16,
                           kind="ExternalOutput").ap()
        taps.append("fuse_full")
        p.add("gpsimd", lambda g, t=t: g.dma_start(out=t, in_=fuse_s),
              reads=[dbuf["fuse"]], dma=True)
    if do_groups and "sattn_full" in _TAPS:
        t = nc.dram_tensor("tap_sattn_full", [TOK, H], F32,
                           kind="ExternalOutput").ap()
        taps.append("sattn_full")
        p.add("gpsimd", lambda g, t=t: g.dma_start(out=t, in_=sattn_s),
              reads=[dbuf["sattn"]], dma=True)
    if do_groups and "xnT_full" in _TAPS:
        t = nc.dram_tensor("tap_xnT_full", [H, TOK], BF16,
                           kind="ExternalOutput").ap()
        taps.append("xnT_full")
        p.add("gpsimd", lambda g, t=t: g.dma_start(out=t, in_=xnT_s),
              reads=[dbuf["xnT"]], dma=True)
    p.flush(blk)

    if do_f2:
        with contextlib.ExitStack() as sc:
            sdownbc, b_sdownbc = sb("sdownbc", [128, H], F32, sc)
            wdblk, b_wdblk = zip(*[sb(f"wdblk{i}", [128, H], BF16, sc) for i in range(4)])
            wd8t, b_wd8t = zip(*[sb(f"wd8t{i}", [128, H], I8, sc) for i in range(4)])
            fch, b_fch = zip(*[sb(f"fch{i}", [128, 256], BF16, sc) for i in range(4)])
            satile, b_satile = zip(*[sb(f"satile{i}", [128, H], F32, sc) for i in range(2)])
            xtile, b_xtile = zip(*[sb(f"xtile{i}", [128, H], F32, sc) for i in range(2)])
            otile, b_otile = zip(*[sb(f"otile{i}", [128, H], F32, sc) for i in range(2)])
            t1c, b_t1c = zip(*[sb(f"t1c{i}", [128, 512], F32, sc) for i in range(2)])
            gfc, b_gfc = zip(*[sb(f"gfc{i}", [128, 512], F32, sc) for i in range(2)])
            sgc, b_sgc = zip(*[sb(f"sgc{i}", [128, 512], F32, sc) for i in range(2)])
            p.add("sync", lambda s: s.dma_start(
                out=sdownbc, in_=sdown_r.partition_broadcast(128)),
                writes=[b_sdownbc], dma=True)
            for pa in range(8):
                t0 = pa * 256
                for ic in range(OCN):
                    i2 = ic % 4
                    p.add("scalar", lambda a, ic=ic, i2=i2: a.dma_start(
                        out=wd8t[i2], in_=wd8[ic * 128:(ic + 1) * 128, :]),
                        writes=[b_wd8t[i2]], dma=True)
                    if ic % 2 == 0:
                        p.add("vector", lambda v, i2=i2: v.tensor_copy(
                            out=wdblk[i2], in_=wd8t[i2]),
                            reads=[b_wd8t[i2]], writes=[b_wdblk[i2]])
                    else:
                        p.add("scalar", lambda a, i2=i2: a.activation(
                            out=wdblk[i2], in_=wd8t[i2], func=AF.Copy),
                            reads=[b_wd8t[i2]], writes=[b_wdblk[i2]])
                    p.add("sync", lambda s, ic=ic, i2=i2, t0=t0: s.dma_start(
                        out=fch[i2], in_=fuse_s[ic * 128:(ic + 1) * 128, t0:t0 + 256]),
                        reads=[dbuf["fuse"]], writes=[b_fch[i2]], dma=True)
                    for ti in range(2):
                        for nh in range(4):
                            p.add("tensor", lambda t, ic=ic, i2=i2, ti=ti, nh=nh:
                                  t.matmul(ps[ti * 4 + nh],
                                           fch[i2][:, ti * 128:(ti + 1) * 128],
                                           wdblk[i2][:, nh * 512:(nh + 1) * 512],
                                           start=(ic == 0), stop=(ic == OCN - 1)),
                                  reads=[b_fch[i2], b_wdblk[i2]],
                                  writes=[b_ps[ti * 4 + nh]])
                for ti in range(2):
                    r0 = t0 + ti * 128
                    i2 = ti % 2
                    p.add("gpsimd", lambda gp, r0=r0, i2=i2: gp.dma_start(
                        out=xtile[i2], in_=xm[r0:r0 + 128, :]),
                        writes=[b_xtile[i2]], dma=True)
                    p.add("gpsimd", lambda gp, r0=r0, i2=i2: gp.dma_start(
                        out=satile[i2], in_=sattn_s[r0:r0 + 128, :]),
                        reads=[dbuf["sattn"]], writes=[b_satile[i2]], dma=True)
                    for nh in range(4):
                        n2 = nh % 2
                        p.add("vector", lambda v, ti=ti, nh=nh, n2=n2: v.tensor_tensor(
                            out=t1c[n2], in0=ps[ti * 4 + nh],
                            in1=sdownbc[:, nh * 512:(nh + 1) * 512], op=ALU.mult),
                            reads=[b_ps[ti * 4 + nh], b_sdownbc], writes=[b_t1c[n2]])
                        p.add("scalar", lambda a, n2=n2: a.activation(
                            out=gfc[n2], in_=t1c[n2], func=AF.Silu),
                            reads=[b_t1c[n2]], writes=[b_gfc[n2]])
                        p.add("vector", lambda v, i2=i2, nh=nh, n2=n2: v.tensor_tensor(
                            out=sgc[n2], in0=gfc[n2],
                            in1=satile[i2][:, nh * 512:(nh + 1) * 512], op=ALU.mult),
                            reads=[b_gfc[n2], b_satile[i2]], writes=[b_sgc[n2]])
                        p.add("vector", lambda v, i2=i2, nh=nh, n2=n2: v.tensor_tensor(
                            out=otile[i2][:, nh * 512:(nh + 1) * 512], in0=sgc[n2],
                            in1=xtile[i2][:, nh * 512:(nh + 1) * 512], op=ALU.add),
                            reads=[b_sgc[n2], b_xtile[i2]], writes=[b_otile[i2]])
                    p.add("gpsimd", lambda gp, r0=r0, i2=i2: gp.dma_start(
                        out=out_d[r0:r0 + 128, :], in_=otile[i2]),
                        reads=[b_otile[i2]], writes=[dbuf["out"]], dma=True)
            p.add("sync", lambda s: s.nop(),
                  extra=[dbuf["out"].w] if dbuf["out"].w else [])
            p.flush(blk)

    p.flush(blk)
    ctx.close()
    return nc, taps


def _host_prep(Wi, Wf, Wg, Wo, wg8, wu8, wd8, norms):
    bf = ml_dtypes.bfloat16
    normi, normf, normg, normo, rmsw, sgate, sup, sdown = norms
    return {
        "wfsw": _blin_swz(Wf),
        "wisw": _blin_swz(Wi),
        "wgsw": _blin_swz(Wg),
        "wosw": _blin_swz(Wo),
        "wg8": _ffn_swz(wg8),
        "wu8": _ffn_swz(wu8),
        "wd8": np.ascontiguousarray(np.asarray(wd8, np.int8)),
        "normf_r": np.asarray(normf, np.float32).reshape(1, H),
        "normi_r": np.asarray(normi, np.float32).reshape(1, H),
        "normg_r": np.asarray(normg, np.float32).reshape(1, H),
        "normo_fm_h": np.ascontiguousarray(
            np.asarray(normo, np.float32).reshape(KC, 128).T),
        "rmsw_fm_h": np.ascontiguousarray(
            np.asarray(rmsw, np.float32).reshape(KC, 128).T),
        "sgate_fm_h": np.ascontiguousarray(
            np.asarray(sgate, np.float32).reshape(OCN, 128).T),
        "sup_fm_h": np.ascontiguousarray(
            np.asarray(sup, np.float32).reshape(OCN, 128).T),
        "sdown_r": np.asarray(sdown, np.float32).reshape(1, H),
        "identb_h": np.eye(128).astype(bf),
        "identf_h": np.eye(128, dtype=np.float32),
        "onesf_h": np.ones((1, 128), np.float32),
        "onesb_h": np.ones((128, 1), bf),
        "onescf_h": np.ones((128, 1), np.float32),
    }


def kernel(L0_attn_x, L0_attn_Wi, L0_attn_Wf, L0_attn_Wg, L0_attn_Wo,
           L0_attn_norm_i, L0_attn_norm_f, L0_attn_norm_g, L0_attn_norm_o,
           L0_attn_rms_w, L0_ffn_w_gate, L0_ffn_w_up, L0_ffn_w_down,
           L0_ffn_s_gate, L0_ffn_s_up, L0_ffn_s_down):
    x = np.asarray(L0_attn_x, np.float32)
    B, L, _ = x.shape
    shared = _host_prep(L0_attn_Wi, L0_attn_Wf, L0_attn_Wg, L0_attn_Wo,
                        L0_ffn_w_gate, L0_ffn_w_up, L0_ffn_w_down,
                        (L0_attn_norm_i, L0_attn_norm_f, L0_attn_norm_g,
                         L0_attn_norm_o, L0_attn_rms_w, L0_ffn_s_gate,
                         L0_ffn_s_up, L0_ffn_s_down))
    nc, _ = build_program()
    in_maps = []
    for c in range(8):
        b = c % 4
        hf = c // 4
        m = dict(shared)
        m["xm"] = np.ascontiguousarray(x[b, hf * TOK:(hf + 1) * TOK])
        sm = np.zeros((128, 8), np.float32)
        if hf == 1:
            sm[:, b] = 1.0
        m["selmask"] = sm
        in_maps.append(m)
    res = run_bass_kernel_spmd(nc, in_maps, core_ids=list(range(8)))
    out = np.empty((B, L, H), np.float32)
    for c, r in enumerate(res.results):
        b = c % 4
        hf = c // 4
        out[b, hf * TOK:(hf + 1) * TOK] = r["out"]
    return out


def _dev_run(in_map, taps=(), n=1):
    _TAPS.update(taps)
    nc, _ = build_program()
    outs = []
    for _ in range(n):
        res = run_bass_kernel_spmd(nc, [dict(in_map)], core_ids=[0])
        outs.append(res.results[0])
    return outs


# version 26
# speedup vs baseline: 1.0079x; 1.0079x over previous
import contextlib
import numpy as np
import ml_dtypes

try:
    import concourse.bass as bass
except ImportError:
    import sys
    sys.path.insert(0, "/opt/trn_rl_repo")
    import concourse.bass as bass
import concourse.mybir as mybir
from concourse.bass_utils import run_bass_kernel_spmd

F32 = mybir.dt.float32
BF16 = mybir.dt.bfloat16
I8 = mybir.dt.int8
I32 = mybir.dt.int32
AF = mybir.ActivationFunctionType
ALU = mybir.AluOpType
AX = mybir.AxisListType

H = 2048
IFF = 8192
KC = H // 128
OCN = IFF // 128
G = 512
TTN = G // 128
NG = 4
TOK = 2048
ENGINES = ("sync", "tensor", "vector", "scalar", "gpsimd")
SELF_GAP = 4

_TAPS = set()



def _w_sync_c(pr, sem, v):
    pr.wait_ge(sem, v)

def _w_tensor_c(pr, sem, v):
    pr.wait_ge(sem, v)

def _w_vector_c(pr, sem, v):
    pr.wait_ge(sem, v)

def _w_scalar_c(pr, sem, v):
    pr.wait_ge(sem, v)

def _w_gpsimd_c(pr, sem, v):
    pr.wait_ge(sem, v)

def _w_sync_d(pr, sem, v):
    pr.wait_ge(sem, v)

def _w_tensor_d(pr, sem, v):
    pr.wait_ge(sem, v)

def _w_vector_d(pr, sem, v):
    pr.wait_ge(sem, v)

def _w_scalar_d(pr, sem, v):
    pr.wait_ge(sem, v)

def _w_gpsimd_d(pr, sem, v):
    pr.wait_ge(sem, v)

_WAITERS = {("sync", "c"): _w_sync_c, ("tensor", "c"): _w_tensor_c,
            ("vector", "c"): _w_vector_c, ("scalar", "c"): _w_scalar_c,
            ("gpsimd", "c"): _w_gpsimd_c,
            ("sync", "d"): _w_sync_d, ("tensor", "d"): _w_tensor_d,
            ("vector", "d"): _w_vector_d, ("scalar", "d"): _w_scalar_d,
            ("gpsimd", "d"): _w_gpsimd_d}


class Buf:
    __slots__ = ("w", "r")

    def __init__(self):
        self.w = None
        self.r = {}


NSLOT = {"sync": 24, "gpsimd": 16, "vector": 2, "scalar": 2, "tensor": 1}


class Prog:

    def __init__(self, nc):
        self.nc = nc
        self.cnt = {e: 0 for e in ENGINES}
        self.dcnt = {(e, k): 0 for e in ENGINES for k in range(NSLOT[e])}
        self.dslot = {e: 0 for e in ENGINES}
        self.idx = {e: 0 for e in ENGINES}
        self.pend = {e: [] for e in ENGINES}
        self.waited = {}
        self.semc = {}
        self.semd = {}

    def add(self, eng, fn, reads=(), writes=(), dma=False, cc=False, extra=()):
        idx = self.idx[eng] + 1
        waits = {}

        def dep(t):
            if t is None:
                return
            pe, pk, pc, pidx = t
            if pc <= 0:
                return
            if pk == "c" and pe == eng:
                if pe != "tensor" and idx - pidx <= SELF_GAP:
                    waits[(pe, pk)] = max(waits.get((pe, pk), 0), pc)
            else:
                waits[(pe, pk)] = max(waits.get((pe, pk), 0), pc)

        for b in reads:
            dep(b.w)
        for b in writes:
            dep(b.w)
            for pe, (pk, pc, pidx) in b.r.items():
                dep((pe, pk, pc, pidx))
        for t in extra:
            dep(t)

        if cc:
            self.cccnt = getattr(self, "cccnt", 0) + 1
            kind = ("cc", 0)
            mycnt = self.cccnt
        elif dma:
            k = self.dslot[eng]
            self.dslot[eng] = (k + 1) % NSLOT[eng]
            if self.dcnt[(eng, k)] > 0:
                waits[(eng, ("d", k))] = max(
                    waits.get((eng, ("d", k)), 0), self.dcnt[(eng, k)])
            self.dcnt[(eng, k)] += 16
            kind = ("d", k)
            mycnt = self.dcnt[(eng, k)]
        else:
            self.cnt[eng] += 1
            kind = "c"
            mycnt = self.cnt[eng]

        if cc:
            pass
        final = []
        for (pe, pk), pc in sorted(waits.items(), key=str):
            key = (eng, pe, pk)
            if self.waited.get(key, -1) >= pc:
                continue
            self.waited[key] = pc
            final.append((pe, pk, pc))

        self.idx[eng] = idx
        self.pend[eng].append((final, fn, kind))
        t = (eng, kind, mycnt, idx)
        for b in reads:
            b.r[eng] = (kind, mycnt, idx)
        for b in writes:
            b.w = t
            b.r = {}
        return t

    def flush(self, blk):
        for eng in ENGINES:
            ops = self.pend[eng]
            if not ops:
                continue
            self.pend[eng] = []

            def emit(proxy, ops=ops, eng=eng):
                for waits, fn, kind in ops:
                    for pe, pk, pc in waits:
                        if pk == "c":
                            _WAITERS[(pe, "c")](proxy, self.semc[pe], pc)
                        elif pk[0] == "cc":
                            proxy.wait_ge(self.ccsem, 1)
                        else:
                            _WAITERS[(pe, "d")](proxy, self.semd[pe][pk[1]], pc)
                    instr = fn(proxy)
                    if kind == "c":
                        instr.then_inc(self.semc[eng], 1)
                    elif kind[0] == "cc":
                        instr.then_inc(self.ccsem)
                    else:
                        instr.then_inc(self.semd[eng][kind[1]], 16)

            getattr(blk, eng)(emit)

    def barrier(self):
        toks = []
        if getattr(self, "cccnt", 0) > 0:
            toks.append(("gpsimd", ("cc", 0), 1, 0))
        for e in ENGINES:
            if self.cnt[e] > 0:
                toks.append((e, "c", self.cnt[e], self.idx[e]))
            for k in range(NSLOT[e]):
                if self.dcnt[(e, k)] > 0:
                    toks.append((e, ("d", k), self.dcnt[(e, k)], self.idx[e]))
        for e in ENGINES:
            self.add(e, lambda p: p.nop(), extra=toks)


def _blin_swz(w):
    return np.ascontiguousarray(
        np.asarray(w, np.float32).reshape(16, 128, 16, 128)
        .transpose(0, 3, 2, 1).reshape(2048, 2048))


def _ffn_swz(w):
    return np.ascontiguousarray(
        np.asarray(w, np.int8).reshape(16, 128, 64, 128)
        .transpose(2, 1, 0, 3).reshape(8192, 2048))


def build_program(n_pre=NG, n_main=NG, do_prep=True, do_groups=True,
                  do_f1=True, do_f2=True):
    nc = bass.Bass("TRN2", target_bir_lowering=False)
    p = Prog(nc)
    taps = []

    def din(name, shape, dtype):
        return nc.dram_tensor(name, shape, dtype, kind="ExternalInput").ap()

    xm = din("xm", [TOK, H], F32)
    selmask = din("selmask", [128, 8], F32)
    wsw = {m: din(f"w{m}sw", [H, H], F32) for m in "figo"}
    wg8 = din("wg8", [IFF, H], I8)
    wu8 = din("wu8", [IFF, H], I8)
    wd8 = din("wd8", [IFF, H], I8)
    norm_r = {m: din(f"norm{m}_r", [1, H], F32) for m in "fig"}
    normo_fm_h = din("normo_fm_h", [128, KC], F32)
    rmsw_fm_h = din("rmsw_fm_h", [128, KC], F32)
    sgate_fm_h = din("sgate_fm_h", [128, OCN], F32)
    sup_fm_h = din("sup_fm_h", [128, OCN], F32)
    sdown_r = din("sdown_r", [1, H], F32)
    identb_h = din("identb_h", [128, 128], BF16)
    identf_h = din("identf_h", [128, 128], F32)
    onesf_h = din("onesf_h", [1, 128], F32)
    onesb_h = din("onesb_h", [128, 1], BF16)
    onescf_h = din("onescf_h", [128, 1], F32)
    out_d = nc.dram_tensor("out", [TOK, H], F32, kind="ExternalOutput").ap()

    wq_s = {m: nc.dram_tensor(f"wq_{m}", [H, H], BF16, kind="Internal").ap()
            for m in "figo"}
    wdb_s = nc.dram_tensor("wdb", [IFF, H], BF16, kind="Internal").ap()
    xnT_s = nc.dram_tensor("xnT_s", [H, TOK], BF16, kind="Internal").ap()
    sattn_s = nc.dram_tensor("sattn_s", [TOK, H], F32, kind="Internal").ap()
    fuse_s = nc.dram_tensor("fuse_s", [IFF, TOK], BF16, kind="Internal").ap()
    sloc_s = nc.dram_tensor("sloc_s", [4 * 128, KC * G], F32, kind="Internal").ap()
    cprod_s = nc.dram_tensor("cprod_s", [4 * 128, KC * G], F32, kind="Internal").ap()
    xqg_s = nc.dram_tensor("xqg_s", [4 * 128, KC * G], BF16, kind="Internal").ap()
    carry_s = nc.dram_tensor("carry_s", [128, KC], F32, kind="Internal").ap()
    ag_s = nc.dram_tensor("ag_s", [8 * 128, KC], F32, kind="Internal").ap()
    sgrow_s = nc.dram_tensor("sgrow_s", [4, 512], F32, kind="Internal").ap()

    dbuf = {name: Buf() for name in
            ["wq_f", "wq_i", "wq_g", "wq_o", "wdb",
             "xnT", "sattn", "fuse", "out", "sloc", "cprod", "xqg",
             "carry", "ag", "sgrow"]}

    ctx = contextlib.ExitStack()
    for e in ENGINES:
        p.semc[e] = ctx.enter_context(nc.semaphore(f"sc_{e}"))
        p.semd[e] = [ctx.enter_context(nc.semaphore(f"sd_{e}{k}"))
                     for k in range(NSLOT[e])]
    p.ccsem = ctx.enter_context(nc.semaphore("ccsem"))

    def sb(name, shape, dtype, scope):
        h = scope.enter_context(nc.sbuf_tensor(name, shape, dtype))
        return h.ap(), Buf()

    def tap(name, ap, buf, shape, dtype):
        if name not in _TAPS:
            return
        dtype = {"float32": F32, "bfloat16": BF16}[dtype] if isinstance(dtype, str) else dtype
        t = nc.dram_tensor("tap_" + name, shape, dtype, kind="ExternalOutput").ap()
        taps.append(name)
        p.add("gpsimd", lambda g, t=t, ap=ap: g.dma_start(out=t, in_=ap),
              reads=[buf], dma=True)

    identb, b_identb = sb("identb", [128, 128], BF16, ctx)
    identf, b_identf = sb("identf", [128, 128], F32, ctx)
    onesf, b_onesf = sb("onesf", [1, 128], F32, ctx)
    onesb, b_onesb = sb("onesb", [128, 1], BF16, ctx)
    onescf, b_onescf = sb("onescf", [128, 1], F32, ctx)
    normo_fm, b_normo = sb("normo_fm", [128, KC], F32, ctx)
    rmsw_fm, b_rmsw = sb("rmsw_fm", [128, KC], F32, ctx)
    sgate_fm, b_sgate = sb("sgate_fm", [128, OCN], F32, ctx)
    sup_fm, b_sup = sb("sup_fm", [128, OCN], F32, ctx)
    swc, b_swc_t = sb("swc", [128, 8], F32, ctx)
    swcol = {m: swc[:, j:j + 1] for j, m in enumerate("figo")}
    swocol = {m: swc[:, 4 + j:5 + j] for j, m in enumerate("figo")}
    b_swcol = {m: Buf() for m in "figo"}
    b_swocol = {m: Buf() for m in "figo"}

    pall_t = ctx.enter_context(nc.psum_tensor("pall", [128, 4096], F32))
    pall = pall_t.ap()
    pallb = pall.bitcast(BF16)
    ps = [pall[:, 512 * i:512 * (i + 1)] for i in range(8)]
    ps2 = [pall[:, 1024 * i:1024 * (i + 1)] for i in range(4)]
    b_ps = [Buf() for _ in range(8)]

    def psb(i):
        return pallb[:, 1024 * i:1024 * (i + 1)]

    for ap_, buf_, src in [(identb, b_identb, identb_h), (identf, b_identf, identf_h),
                           (onesf, b_onesf, onesf_h), (onesb, b_onesb, onesb_h), (onescf, b_onescf, onescf_h),
                           (normo_fm, b_normo, normo_fm_h), (rmsw_fm, b_rmsw, rmsw_fm_h),
                           (sgate_fm, b_sgate, sgate_fm_h), (sup_fm, b_sup, sup_fm_h)]:
        p.add("sync", lambda s, o=ap_, i=src: s.dma_start(out=o, in_=i),
              writes=[buf_], dma=True)

    blk = ctx.enter_context(nc.Block())

    if do_prep:
        with contextlib.ExitStack() as sc:
            wfp, b_wfp = zip(*[sb(f"wfp{i}", [128, H], F32, sc) for i in range(4)])
            wab, b_wab = zip(*[sb(f"wab{i}", [128, H], F32, sc) for i in range(2)])
            q32, b_q32 = zip(*[sb(f"q32_{i}", [128, H], I32, sc) for i in range(2)])
            qbf, b_qbf = zip(*[sb(f"qbf{i}", [128, H], BF16, sc) for i in range(2)])
            prow, b_prow = sb("prow", [1, 512], F32, sc)

            def s1_iter(m, ec):
                i0 = 2 + ec % 2
                ea = ec % 2
                src = wsw[m]
                p.add("sync", lambda s, o=wfp[i0], i=src[ec * 128:(ec + 1) * 128, :]:
                      s.dma_start(out=o, in_=i), writes=[b_wfp[i0]], dma=True)
                p.add("scalar", lambda a, i0=i0, ea=ea: a.activation(
                    out=wab[ea], in_=wfp[i0], func=AF.Abs),
                    reads=[b_wfp[i0]], writes=[b_wab[ea]])
                for j in range(4):
                    p.add("tensor", lambda t, ea=ea, j=j, st=(ec == 0 and j == 0),
                          sp=(ec == KC - 1 and j == 3):
                          t.matmul(ps[5][0:1, :], onescf,
                                   wab[ea][:, j * 512:(j + 1) * 512],
                                   start=st, stop=sp),
                          reads=[b_wab[ea], b_onescf], writes=[b_ps[5]])

            def s1_tail(m):
                p.add("vector", lambda v: v.tensor_reduce(
                    out=prow[0:1, 0:1], in_=ps[5][0:1, :], axis=AX.X, op=ALU.add),
                    reads=[b_ps[5]], writes=[b_prow])
                p.add("vector", lambda v: v.tensor_scalar(
                    out=prow[0:1, 1:2], in0=prow[0:1, 0:1], scalar1=1.0 / (H * H),
                    scalar2=1e-5, op0=ALU.mult, op1=ALU.max),
                    reads=[b_prow], writes=[b_prow])
                p.add("vector", lambda v: v.tensor_scalar(
                    out=prow[0:1, 2:3], in0=prow[0:1, 1:2], scalar1=1.0 / 127.0,
                    scalar2=None, op0=ALU.mult), reads=[b_prow], writes=[b_prow])
                p.add("vector", lambda v: v.reciprocal(
                    out=prow[0:1, 3:4], in_=prow[0:1, 1:2]),
                    reads=[b_prow], writes=[b_prow])
                p.add("tensor", lambda t: t.matmul(
                    ps[6][:, 0:2], onesf, prow[0:1, 2:4], start=True, stop=True),
                    reads=[b_prow, b_onesf], writes=[b_ps[6]])
                jm = "figo".index(m)
                p.add("vector", lambda v, jm=jm, m=m: v.tensor_copy(
                    out=swc[:, 4 + jm:5 + jm], in_=ps[6][:, 0:1]),
                    reads=[b_ps[6]], writes=[b_swocol[m]])
                p.add("vector", lambda v, jm=jm, m=m: v.tensor_copy(
                    out=swc[:, jm:jm + 1], in_=ps[6][:, 1:2]),
                    reads=[b_ps[6]], writes=[b_swcol[m]])

            def s2_iter(m, ec):
                i0 = ec % 2
                src = wsw[m]
                p.add("sync", lambda s, o=wfp[i0], i=src[ec * 128:(ec + 1) * 128, :]:
                      s.dma_start(out=o, in_=i), writes=[b_wfp[i0]], dma=True)
                p.add("vector", lambda v, i0=i0, m=m: v.tensor_scalar(
                    out=q32[i0], in0=wfp[i0], scalar1=swcol[m], scalar2=None,
                    op0=ALU.mult),
                    reads=[b_wfp[i0], b_swcol[m]], writes=[b_q32[i0]])
                p.add("vector", lambda v, i0=i0: v.tensor_scalar(
                    out=qbf[i0], in0=q32[i0], scalar1=1.0, scalar2=-1.0,
                    op0=ALU.min, op1=ALU.max), reads=[b_q32[i0]], writes=[b_qbf[i0]])
                p.add("gpsimd", lambda g, i0=i0, m=m, ec=ec: g.dma_start(
                    out=wq_s[m][ec * 128:(ec + 1) * 128, :], in_=qbf[i0]),
                    reads=[b_qbf[i0]], writes=[dbuf["wq_" + m]], dma=True)

            order = "figo"
            for ec in range(KC):
                s1_iter("f", ec)
            s1_tail("f")
            for mi in range(4):
                m = order[mi]
                nxt = order[mi + 1] if mi + 1 < 4 else None
                for ec in range(KC):
                    s2_iter(m, ec)
                    if nxt is not None:
                        s1_iter(nxt, ec)
                if nxt is not None:
                    s1_tail(nxt)
            p.barrier()
            p.flush(blk)

    if do_groups:
      with contextlib.ExitStack() as sc:
        normbc, b_normbc = zip(*[sb(f"normbc{i}", [128, H], F32, sc) for i in range(1)])
        A, b_A = sb("Abuf", [128, KC * G], F32, sc)
        sfull, b_s = sb("sfull", [128, KC * G], F32, sc)
        xg, b_xg = zip(*[sb(f"xg{i}", [128, H], F32, sc) for i in range(2)])
        wb, b_wb = sb("wbuf", [128, H], F32, sc)
        xq8 = wb.bitcast(I8)[:, 0:H]
        b_xq8 = b_wb
        xqtm, b_xqtm = zip(*[sb(f"xqtm{i}", [128, H], BF16, sc) for i in range(2)])
        xqT2, b_xqT2_t = sb("xqT2", [128, 2 * KC * G], BF16, sc)
        xqT = [xqT2[:, 0:KC * G], xqT2[:, KC * G:]]
        b_xqT = [Buf(), Buf()]
        yqT, b_yqT = xqT[0], b_xqT[0]
        wqblk, b_wqblk = zip(*[sb(f"wqblk{i}", [128, H], BF16, sc) for i in range(4)])
        ch, b_ch = zip(*[sb(f"ch{i}", [128, 1024], F32, sc) for i in range(6)])
        zf, b_zf = ch[0], b_ch[0]
        zi, b_zi = ch[1], b_ch[1]
        thc, b_thc = ch[2], b_ch[2]
        fc, b_fc = ch[3], b_ch[3]
        fm1, b_fm1 = ch[4], b_ch[4]
        fic, b_fic = ch[5], b_ch[5]
        zg, b_zg = ch[0], b_ch[0]
        thg, b_thg = ch[2], b_ch[2]
        fgc, b_fgc = ch[3], b_ch[3]
        shatc, b_shatc = ch[4], b_ch[4]
        tyc, b_tyc = ch[5], b_ch[5]
        wyc, b_wyc = ch[1], b_ch[1]
        mrun, b_mrun = zip(*[sb(f"mrun{i}", [128, 512], F32, sc) for i in range(2)])
        sc3, b_sc3 = zip(*[sb(f"scbc{i}", [128, 1024], F32, sc) for i in range(3)])
        s2c, b_s2c = zip(*[sb(f"s2c{i}", [128, 512], BF16, sc) for i in range(1)])
        yq8c, b_yq8c = sb("yq8c", [128, 512], I8, sc)
        safm, b_safm = sb("safm", [128, 512], F32, sc)
        sasm, b_sasm = zip(*[sb(f"sasm{i}", [128, 128], F32, sc) for i in range(2)])
        xnstage, b_xnstage = sb("xnstage", [128, 8 * 128], BF16, sc)
        carry, b_carry = sb("carry", [128, KC], F32, sc)
        cols, b_cols = sb("cols", [128, 64], F32, sc)
        rows, b_rows = sb("rows", [1, 1024], F32, sc)

        p.add("vector", lambda v: v.memset(carry, 0.0), writes=[b_carry])

        def ln_pg_loads(xsrc, g, pg):
            for ti in range(2):
                tt = pg * 2 + ti
                r0 = (g % NG) * G + tt * 128
                p.add("sync", lambda s, ti=ti, i=xsrc[r0:r0 + 128, :]:
                      s.dma_start(out=xg[ti], in_=i), writes=[b_xg[ti]], dma=True)
                p.add("vector", lambda v, tt=tt, ti=ti: v.reduce_sum(
                    out=cols[:, tt:tt + 1], in_=xg[ti], axis=AX.X),
                    reads=[b_xg[ti]], writes=[b_cols])
                p.add("scalar", lambda a, tt=tt, ti=ti: a.activation(
                    out=xqtm[ti], in_=xg[ti], func=AF.Square,
                    accum_out=cols[:, 8 + tt:9 + tt]),
                    reads=[b_xg[ti]], writes=[b_xqtm[ti], b_cols])

        def ln_pg_chain(pg):
            t0 = pg * 2
            p.add("vector", lambda v, t0=t0: v.tensor_scalar(
                out=cols[:, 4 + t0:6 + t0], in0=cols[:, t0:t0 + 2],
                scalar1=1.0 / H, scalar2=None, op0=ALU.mult),
                reads=[b_cols], writes=[b_cols])
            p.add("vector", lambda v, t0=t0: v.tensor_tensor(
                out=cols[:, 12 + t0:14 + t0], in0=cols[:, 4 + t0:6 + t0],
                in1=cols[:, 4 + t0:6 + t0], op=ALU.mult),
                reads=[b_cols], writes=[b_cols])
            p.add("vector", lambda v, t0=t0: v.scalar_tensor_tensor(
                out=cols[:, 12 + t0:14 + t0], in0=cols[:, 8 + t0:10 + t0],
                scalar=1.0 / H, in1=cols[:, 12 + t0:14 + t0],
                op0=ALU.mult, op1=ALU.subtract),
                reads=[b_cols], writes=[b_cols])
            p.add("vector", lambda v, t0=t0: v.tensor_scalar(
                out=cols[:, 12 + t0:14 + t0], in0=cols[:, 12 + t0:14 + t0],
                scalar1=1e-5, scalar2=None, op0=ALU.add),
                reads=[b_cols], writes=[b_cols])
            p.add("scalar", lambda a, t0=t0: a.activation(
                out=cols[:, 12 + t0:14 + t0], in_=cols[:, 12 + t0:14 + t0],
                func=AF.Sqrt), reads=[b_cols], writes=[b_cols])
            p.add("vector", lambda v, t0=t0: v.reciprocal(
                out=cols[:, 12 + t0:14 + t0], in_=cols[:, 12 + t0:14 + t0]),
                reads=[b_cols], writes=[b_cols])

        def ln_astore(pg, ti):
            tt = pg * 2 + ti
            p.add("vector", lambda v, tt=tt, ti=ti: v.tensor_scalar(
                out=A[:, tt * H:(tt + 1) * H], in0=xg[ti],
                scalar1=cols[:, 4 + tt:5 + tt],
                scalar2=cols[:, 12 + tt:13 + tt],
                op0=ALU.subtract, op1=ALU.mult),
                reads=[b_xg[ti], b_cols], writes=[b_A])
            p.add("scalar", lambda a, tt=tt, ti=ti: a.activation(
                out=xqtm[ti], in_=A[:, tt * H:(tt + 1) * H], func=AF.Square,
                accum_out=cols[:, 16 + tt:17 + tt]),
                reads=[b_A], writes=[b_xqtm[ti], b_cols])

        def ln_rms_tail():
            p.add("vector", lambda v: v.tensor_scalar(
                out=cols[:, 20:24], in0=cols[:, 16:20],
                scalar1=1.0 / H, scalar2=1e-8, op0=ALU.mult, op1=ALU.add),
                reads=[b_cols], writes=[b_cols])
            p.add("scalar", lambda a: a.activation(
                out=cols[:, 20:24], in_=cols[:, 20:24], func=AF.Sqrt),
                reads=[b_cols], writes=[b_cols])
            p.add("vector", lambda v: v.reciprocal(
                out=cols[:, 20:24], in_=cols[:, 20:24]),
                reads=[b_cols], writes=[b_cols])

        def xnT_tt(g, tt):
            i0 = tt % 2
            p.add("vector", lambda v, i0=i0, tt=tt: v.tensor_copy(
                out=xqtm[i0], in_=A[:, tt * H:(tt + 1) * H]),
                reads=[b_A], writes=[b_xqtm[i0]])
            gl = (g - NG) * G + tt * 128
            for half in range(2):
                for kc4 in range(2):
                    for j in range(4):
                        kc = half * 8 + kc4 * 4 + j
                        p.add("tensor", lambda t, i0=i0, kc=kc, j=j: t.transpose(
                            psb(6)[:, j * 128:(j + 1) * 128],
                            xqtm[i0][:, kc * 128:(kc + 1) * 128], identb),
                            reads=[b_xqtm[i0], b_identb], writes=[b_ps[6]])
                    p.add("scalar", lambda a, kc4=kc4: a.copy(
                        out=xnstage[:, kc4 * 512:(kc4 + 1) * 512],
                        in_=psb(6)[:, 0:512]),
                        reads=[b_ps[6]], writes=[b_xnstage])
                p.add("gpsimd", lambda gp, gl=gl, half=half: gp.dma_start(
                    out=xnT_s[half * 1024:(half + 1) * 1024, gl:gl + 128]
                        .rearrange("(kc p) t -> p kc t", p=128),
                    in_=xnstage.rearrange("p (kc t) -> p kc t", t=128)),
                    reads=[b_xnstage], writes=[dbuf["xnT"]], dma=True)

        def s1_ln_gen(g):
            ln_pg_loads(xm, g, 0)
            yield
            ln_pg_chain(0)
            yield
            ln_astore(0, 0)
            yield
            ln_astore(0, 1)
            xnT_tt(g + NG, 0)
            yield
            ln_pg_loads(xm, g, 1)
            yield
            ln_pg_chain(1)
            xnT_tt(g + NG, 1)
            yield
            ln_astore(1, 0)
            ln_astore(1, 1)
            yield
            ln_rms_tail()
            xnT_tt(g + NG, 2)
            yield
            xnT_tt(g + NG, 3)

        def bcast_scale(row_ap, sidx):
            p.add("tensor", lambda t, row_ap=row_ap: t.matmul(
                ps[6], onesf, row_ap, start=True, stop=True),
                reads=[b_rows, b_onesf], writes=[b_ps[6]])
            p.add("scalar", lambda a, sidx=sidx: a.copy(
                out=sc3[sidx][:, 0:512], in_=ps[6]),
                reads=[b_ps[6]], writes=[b_sc3[sidx]])
            p.add("scalar", lambda a, sidx=sidx: a.copy(
                out=sc3[sidx][:, 512:1024], in_=ps[6]),
                reads=[b_ps[6]], writes=[b_sc3[sidx]])

        def quant(br, dstT, b_dstT, row_only=False):
            nb, b_nb = normbc[0], b_normbc[0]
            for tt in range(TTN):
                p.add("vector", lambda v, tt=tt, nb=nb: v.tensor_tensor(
                    out=wb, in0=A[:, tt * H:(tt + 1) * H], in1=nb, op=ALU.mult),
                    reads=[b_A, b_nb], writes=[b_wb])
                p.add("vector", lambda v, tt=tt: v.reduce_max(
                    out=cols[:, 24 + tt:25 + tt], in_=wb, axis=AX.X,
                    apply_absolute_value=True), reads=[b_wb], writes=[b_cols])
            p.add("vector", lambda v: v.tensor_scalar(
                out=cols[:, 28:32], in0=cols[:, 24:28],
                scalar1=1e-5, scalar2=1.0 / 127.0, op0=ALU.max, op1=ALU.mult),
                reads=[b_cols], writes=[b_cols])
            p.add("vector", lambda v: v.reciprocal(
                out=cols[:, 32:36], in_=cols[:, 28:32]),
                reads=[b_cols], writes=[b_cols])
            p.add("vector", lambda v: v.tensor_tensor(
                out=cols[:, 36:40], in0=cols[:, 28:32], in1=cols[:, 20:24],
                op=ALU.mult), reads=[b_cols], writes=[b_cols])
            p.add("vector", lambda v, br=br: v.tensor_scalar(
                out=cols[:, 36:40], in0=cols[:, 36:40],
                scalar1=swocol[br], scalar2=127.0, op0=ALU.mult, op1=ALU.mult),
                reads=[b_cols, b_swocol[br]], writes=[b_cols])
            for tt in range(TTN):
                p.add("vector", lambda v, tt=tt, nb=nb: v.scalar_tensor_tensor(
                    out=xq8, in0=A[:, tt * H:(tt + 1) * H],
                    scalar=cols[:, 32 + tt:33 + tt], in1=nb,
                    op0=ALU.mult, op1=ALU.mult),
                    reads=[b_A, b_cols, b_nb], writes=[b_xq8])
                i0 = tt % 2
                p.add("scalar", lambda a, i0=i0: a.copy(
                    out=xqtm[i0], in_=xq8), reads=[b_xq8], writes=[b_xqtm[i0]])
                for kc4 in range(4):
                    for j in range(4):
                        kc = kc4 * 4 + j
                        p.add("tensor", lambda t, i0=i0, kc=kc, j=j: t.transpose(
                            psb(4)[:, j * 128:(j + 1) * 128],
                            xqtm[i0][:, kc * 128:(kc + 1) * 128], identb),
                            reads=[b_xqtm[i0], b_identb], writes=[b_ps[4]])
                    p.add("scalar", lambda a, kc4=kc4, tt=tt, dstT=dstT: a.copy(
                        out=dstT.rearrange("p (kc t) -> p kc t", t=G)
                            [:, kc4 * 4:(kc4 + 1) * 4, tt * 128:(tt + 1) * 128],
                        in_=psb(4)[:, 0:512].rearrange("p (kc t) -> p kc t", t=128)),
                        reads=[b_ps[4]], writes=[b_dstT])
            for tt in range(TTN):
                p.add("tensor", lambda t, tt=tt: t.transpose(
                    ps[5][0:1, tt * 128:(tt + 1) * 128],
                    cols[:, 36 + tt:37 + tt], identf),
                    reads=[b_cols, b_identf], writes=[b_ps[5]])
            p.add("scalar", lambda a: a.copy(
                out=rows[0:1, 0:512], in_=ps[5][0:1, 0:512]),
                reads=[b_ps[5]], writes=[b_rows])
            if row_only:
                return
            sidx = {"f": 0, "i": 1, "g": 2}[br]
            bcast_scale(rows[0:1, 0:512], sidx)

        def wq_stream(m, ec, slot):
            p.add("sync", lambda s, o=wqblk[slot], i=wq_s[m][ec * 128:(ec + 1) * 128, :]:
                  s.dma_start(out=o, in_=i),
                  reads=[dbuf["wq_" + m]], writes=[b_wqblk[slot]], dma=True)

        def blin_mm(srcT, b_srcT, wslot, bank, chunk):
            for kc in range(KC):
                p.add("tensor", lambda t, kc=kc: t.matmul(
                    ps[bank], wqblk[wslot][:, kc * 128:(kc + 1) * 128],
                    srcT[:, kc * G:kc * G + G],
                    start=(kc == 0), stop=(kc == KC - 1)),
                    reads=[b_wqblk[wslot], b_srcT], writes=[b_ps[bank]])

        sout = [sfull[:, 0:1024], sfull[:, 1024:2048]]
        cout = [sfull[:, 2048:3072], sfull[:, 3072:4096]]
        b_sout = [Buf(), Buf()]
        b_cout = [Buf(), Buf()]
        onestile, b_ones1 = sb("onestile", [128, 512], BF16, sc)
        smsb, b_smsb = sb("smsb", [128, 8], F32, sc)
        agsb, b_agsb = sb("agsb", [128, 128], F32, sc)
        s_in, b_s_in = zip(*[sb(f"s_in{i}", [128, KC], F32, sc) for i in range(2)])
        ccarry, b_ccarry = sb("ccarry", [128, KC], F32, sc)

        p.add("vector", lambda v: v.memset(onestile, 1.0), writes=[b_ones1])
        p.add("vector", lambda v: v.memset(ccarry, 1.0), writes=[b_ccarry])
        p.add("sync", lambda s: s.dma_start(out=smsb, in_=selmask),
              writes=[b_smsb], dma=True)

        def s1_ln(g):
            for _ in s1_ln_gen(g):
                pass
            if g == 0:
                tap("xn", A, b_A, [128, KC * G], "float32")

        def s1_quants(g):
            p.add("sync", lambda s: s.dma_start(
                out=normbc[0], in_=norm_r["g"].partition_broadcast(128)),
                writes=[b_normbc[0]], dma=True)
            quant("g", xqT[0], b_xqT[0], row_only=True)
            p.add("gpsimd", lambda gp, g=g: gp.dma_start(
                out=sgrow_s[g:g + 1, :], in_=rows[0:1, 0:512]),
                reads=[b_rows], writes=[dbuf["sgrow"]], dma=True)
            p.add("gpsimd", lambda gp, g=g: gp.dma_start(
                out=xqg_s[g * 128:(g + 1) * 128, :], in_=xqT[0]),
                reads=[b_xqT[0]], writes=[dbuf["xqg"]], dma=True)
            p.add("sync", lambda s: s.dma_start(
                out=normbc[0], in_=norm_r["f"].partition_broadcast(128)),
                writes=[b_normbc[0]], dma=True)
            quant("f", xqT[0], b_xqT[0])
            if g == 0:
                tap("xqTf", xqT[0], b_xqT[0], [128, KC * G], "bfloat16")
            p.add("sync", lambda s: s.dma_start(
                out=normbc[0], in_=norm_r["i"].partition_broadcast(128)),
                writes=[b_normbc[0]], dma=True)
            quant("i", xqT[1], b_xqT[1])

        def s1_loop(g, next_g):
            gen = s1_ln_gen(next_g) if next_g is not None else None
            for ep in range(KC // 2):
                pset = ep % 2
                ca, cb = 2 * ep, 2 * ep + 1
                sf = 2 * pset
                si = 2 - 2 * pset
                wq_stream("f", ca, sf)
                wq_stream("f", cb, sf + 1)
                wq_stream("i", ca, si)
                wq_stream("i", cb, si + 1)
                blin_mm(xqT[0], b_xqT[0], sf, 2 * pset + 0, ca)
                blin_mm(xqT[0], b_xqT[0], sf + 1, 2 * pset + 1, cb)
                blin_mm(xqT[1], b_xqT[1], si, 2 * pset + 4, ca)
                blin_mm(xqT[1], b_xqT[1], si + 1, 2 * pset + 5, cb)
                p.add("vector", lambda v, pset=pset: v.tensor_tensor(
                    out=zf, in0=ps2[pset], in1=sc3[0], op=ALU.mult),
                    reads=[b_ps[2 * pset], b_ps[2 * pset + 1], b_sc3[0]],
                    writes=[b_zf])
                p.add("vector", lambda v, pset=pset: v.tensor_tensor(
                    out=zi, in0=ps2[2 + pset], in1=sc3[1], op=ALU.mult),
                    reads=[b_ps[4 + 2 * pset], b_ps[5 + 2 * pset], b_sc3[1]],
                    writes=[b_zi])
                p.add("scalar", lambda a: a.activation(
                    out=thc, in_=zf, func=AF.Tanh, scale=0.5),
                    reads=[b_zf], writes=[b_thc])
                p.add("gpsimd", lambda gp: gp.tensor_scalar(
                    out=fc, in0=thc, scalar1=0.5, scalar2=0.5,
                    op0=ALU.mult, op1=ALU.add), reads=[b_thc], writes=[b_fc])
                p.add("gpsimd", lambda gp: gp.tensor_scalar(
                    out=fm1, in0=fc, scalar1=-1.0, scalar2=1.0,
                    op0=ALU.mult, op1=ALU.add), reads=[b_fc], writes=[b_fm1])
                p.add("scalar", lambda a: a.activation(
                    out=thc, in_=zi, func=AF.Tanh, scale=0.5),
                    reads=[b_zi], writes=[b_thc])
                p.add("gpsimd", lambda gp: gp.tensor_scalar(
                    out=fic, in0=thc, scalar1=0.5, scalar2=0.5,
                    op0=ALU.mult, op1=ALU.add), reads=[b_thc], writes=[b_fic])
                p.add("gpsimd", lambda gp: gp.tensor_tensor(
                    out=thc, in0=fic, in1=fm1, op=ALU.mult),
                    reads=[b_fic, b_fm1], writes=[b_thc])
                if gen is not None:
                    next(gen, None)
                p.add("vector", lambda v: v.tensor_tensor(
                    out=zf, in0=zi, in1=thc, op=ALU.mult),
                    reads=[b_zi, b_thc], writes=[b_zf])
                so, bso = sout[pset], b_sout[pset]
                co, bco = cout[pset], b_cout[pset]
                for sub, cx in ((0, ca), (1, cb)):
                    p.add("vector", lambda v, sub=sub, cx=cx, so=so: v.tensor_tensor_scan(
                        out=so[:, sub * G:(sub + 1) * G],
                        data0=fc[:, sub * G:(sub + 1) * G],
                        data1=zf[:, sub * G:(sub + 1) * G],
                        initial=carry[:, cx:cx + 1], op0=ALU.mult, op1=ALU.add),
                        reads=[b_fc, b_zf, b_carry], writes=[bso])
                    p.add("vector", lambda v, sub=sub, cx=cx, co=co: v.tensor_tensor_scan(
                        out=co[:, sub * G:(sub + 1) * G],
                        data0=fc[:, sub * G:(sub + 1) * G],
                        data1=onestile,
                        initial=ccarry[:, cx:cx + 1], op0=ALU.mult, op1=ALU.mult),
                        reads=[b_fc, b_ones1, b_ccarry], writes=[bco])
                p.add("vector", lambda v, ca=ca, so=so: v.tensor_copy(
                    out=carry[:, ca:ca + 2],
                    in_=so.rearrange("p (c t) -> p c t", t=G)
                        [:, 0:2, G - 1:G].rearrange("p c t -> p (c t)")),
                    reads=[bso], writes=[b_carry])
                p.add("vector", lambda v, ca=ca, co=co: v.tensor_copy(
                    out=ccarry[:, ca:ca + 2],
                    in_=co.rearrange("p (c t) -> p c t", t=G)
                        [:, 0:2, G - 1:G].rearrange("p c t -> p (c t)")),
                    reads=[bco], writes=[b_ccarry])
                r0 = g * 128
                p.add("gpsimd", lambda gp, r0=r0, ep=ep, so=so: gp.dma_start(
                    out=sloc_s[r0:r0 + 128, ep * 1024:(ep + 1) * 1024], in_=so),
                    reads=[bso], writes=[dbuf["sloc"]], dma=True)
                p.add("gpsimd", lambda gp, r0=r0, ep=ep, co=co: gp.dma_start(
                    out=cprod_s[r0:r0 + 128, ep * 1024:(ep + 1) * 1024], in_=co),
                    reads=[bco], writes=[dbuf["cprod"]], dma=True)
            if gen is not None:
                for _ in gen:
                    pass

        def s1_group_driver():
            s1_ln(0)
            s1_quants(0)
            for g in range(4):
                s1_loop(g, g + 1 if g < 3 else None)
                if g < 3:
                    s1_quants(g + 1)

        def exchange():
            p.add("gpsimd", lambda gp: gp.dma_start(out=carry_s, in_=carry),
                  reads=[b_carry], writes=[dbuf["carry"]], dma=True)
            p.add("gpsimd", lambda gp: gp.collective_compute(
                "AllGather", ALU.bypass, replica_groups=[list(range(8))],
                ins=[carry_s.opt()], outs=[ag_s.opt()]),
                reads=[dbuf["carry"]], writes=[dbuf["ag"]], cc=True)
            p.add("sync", lambda s: s.dma_start(
                out=agsb.rearrange("p (j c) -> p j c", c=KC),
                in_=ag_s.rearrange("(j p) c -> p j c", p=128)),
                reads=[dbuf["ag"]], writes=[b_agsb], dma=True)
            p.add("vector", lambda v: v.tensor_scalar(
                out=s_in[0], in0=agsb[:, 0:KC], scalar1=smsb[:, 0:1],
                scalar2=None, op0=ALU.mult),
                reads=[b_agsb, b_smsb], writes=[b_s_in[0]])
            for j in range(1, 8):
                p.add("vector", lambda v, j=j: v.scalar_tensor_tensor(
                    out=s_in[j % 2], in0=agsb[:, j * KC:(j + 1) * KC],
                    scalar=smsb[:, j:j + 1], in1=s_in[(j + 1) % 2],
                    op0=ALU.mult, op1=ALU.add),
                    reads=[b_agsb, b_smsb, b_s_in[(j + 1) % 2]],
                    writes=[b_s_in[j % 2]])

        sfin, b_sfin = s_in[7 % 2], b_s_in[7 % 2]

        def d_load(g):
            r0 = g * 128
            p.add("scalar", lambda a, r0=r0: a.dma_start(
                out=sfull, in_=sloc_s[r0:r0 + 128, :]),
                reads=[dbuf["sloc"]],
                writes=[b_s, b_sout[0], b_sout[1], b_cout[0], b_cout[1]], dma=True)
            p.add("scalar", lambda a, r0=r0: a.dma_start(
                out=A, in_=cprod_s[r0:r0 + 128, :]),
                reads=[dbuf["cprod"]], writes=[b_A], dma=True)

        def d_fix_head(g):
            gq, b_gq = xqT[g % 2], b_xqT[g % 2]
            r0 = g * 128
            p.add("scalar", lambda a, r0=r0, gq=gq: a.dma_start(
                out=gq, in_=xqg_s[r0:r0 + 128, :]),
                reads=[dbuf["xqg"]], writes=[b_gq], dma=True)
            p.add("sync", lambda s, g=g: s.dma_start(
                out=rows[0:1, 0:512], in_=sgrow_s[g:g + 1, :]),
                reads=[dbuf["sgrow"]], writes=[b_rows], dma=True)
            bcast_scale(rows[0:1, 0:512], 2)

        def d_fix_slices(g):
            for ep in range(KC // 2):
                for sub in range(2):
                    cx = 2 * ep + sub
                    p.add("vector", lambda v, ep=ep, sub=sub, cx=cx:
                          v.scalar_tensor_tensor(
                              out=zf[:, sub * G:(sub + 1) * G],
                              in0=A[:, cx * G:(cx + 1) * G],
                              scalar=sfin[:, cx:cx + 1],
                              in1=sfull[:, cx * G:(cx + 1) * G],
                              op0=ALU.mult, op1=ALU.add),
                          reads=[b_A, b_sfin, b_s], writes=[b_zf])
                wins = [b_sout[0], b_sout[1], b_cout[0], b_cout[1]]
                ww = [b_s] + ([wins[ep]] if ep < 4 else [])
                p.add("scalar", lambda a, ep=ep: a.copy(
                    out=sfull[:, ep * 1024:(ep + 1) * 1024], in_=zf),
                    reads=[b_zf], writes=ww)
                for sub in range(2):
                    ec = 2 * ep + sub
                    p.add("scalar", lambda a, ec=ec: a.activation(
                        out=s2c[0], in_=sfull[:, ec * G:(ec + 1) * G],
                        func=AF.Square), reads=[b_s], writes=[b_s2c[0]])
                    p.add("tensor", lambda t, ec=ec: t.matmul(
                        ps[3][0:1, :], onesb, s2c[0],
                        start=(ec == 0), stop=(ec == KC - 1)),
                        reads=[b_s2c[0], b_onesb], writes=[b_ps[3]])
                yield
            if g == 0:
                tap("s_main", sfull, b_s, [128, KC * G], "float32")

        def d_fix_tail(g):
            p.add("vector", lambda v: v.tensor_scalar(
                out=rows[0:1, 512:1024], in0=ps[3][0:1, :], scalar1=1.0 / H,
                scalar2=1e-5, op0=ALU.mult, op1=ALU.add),
                reads=[b_ps[3]], writes=[b_rows])
            p.add("scalar", lambda a: a.activation(
                out=rows[0:1, 512:1024], in_=rows[0:1, 512:1024], func=AF.Sqrt),
                reads=[b_rows], writes=[b_rows])
            p.add("vector", lambda v: v.reciprocal(
                out=rows[0:1, 512:1024], in_=rows[0:1, 512:1024]),
                reads=[b_rows], writes=[b_rows])
            bcast_scale(rows[0:1, 512:1024], 0)
        def d_part1(g):
            rstdbc, b_rstdbc = sc3[0], b_sc3[0]
            zg2 = [zg, wb[:, 0:1024]]
            b_zg2 = [b_zg, b_wb]
            sh2 = [shatc, wb[:, 1024:2048]]
            b_sh2 = [b_shatc, b_wb]
            ty2 = [tyc, xqtm[0].bitcast(F32)[:, 0:1024]]
            b_ty2 = [b_tyc, b_xqtm[0]]

            def front(i):
                pset = i % 2
                pi = i % 2
                ca, cb = 2 * i, 2 * i + 1
                sg_ = 2 * pset
                wq_stream("g", ca, sg_)
                wq_stream("g", cb, sg_ + 1)
                blin_mm(xqT[g % 2], b_xqT[g % 2], sg_, 2 * pset + 0, ca)
                blin_mm(xqT[g % 2], b_xqT[g % 2], sg_ + 1, 2 * pset + 1, cb)
                p.add("vector", lambda v, pset=pset, pi=pi: v.tensor_tensor(
                    out=zg2[pi], in0=ps2[pset], in1=sc3[2], op=ALU.mult),
                    reads=[b_ps[2 * pset], b_ps[2 * pset + 1], b_sc3[2]],
                    writes=[b_zg2[pi]])
                for sub, cx in ((0, ca), (1, cb)):
                    p.add("vector", lambda v, sub=sub, cx=cx, pi=pi:
                          v.scalar_tensor_tensor(
                              out=sh2[pi][:, sub * G:(sub + 1) * G],
                              in0=sfull[:, cx * G:(cx + 1) * G],
                              scalar=rmsw_fm[:, cx:cx + 1], in1=rstdbc[:, 0:512],
                              op0=ALU.mult, op1=ALU.mult),
                          reads=[b_s, b_rmsw, b_rstdbc], writes=[b_sh2[pi]])
                p.add("scalar", lambda a, pi=pi: a.activation(
                    out=thg, in_=zg2[pi], func=AF.Tanh, scale=0.5),
                    reads=[b_zg2[pi]], writes=[b_thg])
                p.add("gpsimd", lambda gp: gp.tensor_scalar(
                    out=fgc, in0=thg, scalar1=0.5, scalar2=0.5,
                    op0=ALU.mult, op1=ALU.add), reads=[b_thg], writes=[b_fgc])
                p.add("gpsimd", lambda gp, pi=pi: gp.tensor_tensor(
                    out=ty2[pi], in0=fgc, in1=sh2[pi], op=ALU.mult),
                    reads=[b_fgc, b_sh2[pi]], writes=[b_ty2[pi]])

            def tail(i):
                pi = i % 2
                ca, cb = 2 * i, 2 * i + 1
                p.add("vector", lambda v, ca=ca, pi=pi: v.tensor_tensor(
                    out=A[:, ca * G:(ca + 2) * G], in0=zg2[pi], in1=ty2[pi],
                    op=ALU.mult), reads=[b_zg2[pi], b_ty2[pi]], writes=[b_A])
                for sub, cx in ((0, ca), (1, cb)):
                    p.add("vector", lambda v, sub=sub, cx=cx: v.tensor_scalar(
                        out=wyc[:, sub * G:(sub + 1) * G],
                        in0=A[:, cx * G:(cx + 1) * G],
                        scalar1=normo_fm[:, cx:cx + 1], scalar2=None,
                        op0=ALU.mult),
                        reads=[b_A, b_normo], writes=[b_wyc])
                if i == 0:
                    p.add("vector", lambda v: v.tensor_tensor(
                        out=mrun[0], in0=wyc[:, 0:512], in1=wyc[:, 512:1024],
                        op=ALU.max), reads=[b_wyc], writes=[b_mrun[0]])
                    p.add("vector", lambda v: v.tensor_tensor(
                        out=mrun[1], in0=wyc[:, 0:512], in1=wyc[:, 512:1024],
                        op=ALU.min), reads=[b_wyc], writes=[b_mrun[1]])
                else:
                    for sub in range(2):
                        p.add("vector", lambda v, sub=sub: v.tensor_tensor(
                            out=mrun[0], in0=mrun[0],
                            in1=wyc[:, sub * G:(sub + 1) * G], op=ALU.max),
                            reads=[b_wyc, b_mrun[0]], writes=[b_mrun[0]])
                        p.add("vector", lambda v, sub=sub: v.tensor_tensor(
                            out=mrun[1], in0=mrun[1],
                            in1=wyc[:, sub * G:(sub + 1) * G], op=ALU.min),
                            reads=[b_wyc, b_mrun[1]], writes=[b_mrun[1]])

            def acc(i):
                for cx in (2 * i, 2 * i + 1):
                    p.add("scalar", lambda a, cx=cx: a.activation(
                        out=s2c[0], in_=A[:, cx * G:(cx + 1) * G],
                        func=AF.Square), reads=[b_A], writes=[b_s2c[0]])
                    p.add("tensor", lambda t, cx=cx: t.matmul(
                        ps[5][0:1, :], onesb, s2c[0],
                        start=(cx == 0), stop=(cx == KC - 1)),
                        reads=[b_s2c[0], b_onesb], writes=[b_ps[5]])

            front(0)
            for i in range(KC // 2):
                if i + 1 < KC // 2:
                    front(i + 1)
                tail(i)
                if i - 1 >= 0:
                    acc(i - 1)
            acc(KC // 2 - 1)
            if g == 0:
                tap("y", A, b_A, [128, KC * G], "float32")
            p.add("vector", lambda v: v.tensor_scalar(
                out=rows[0:1, 512:1024], in0=ps[5][0:1, :], scalar1=1.0 / H,
                scalar2=1e-8, op0=ALU.mult, op1=ALU.add),
                reads=[b_ps[5]], writes=[b_rows])
            p.add("scalar", lambda a: a.activation(
                out=rows[0:1, 512:1024], in_=rows[0:1, 512:1024], func=AF.Sqrt),
                reads=[b_rows], writes=[b_rows])
            p.add("vector", lambda v: v.reciprocal(
                out=rows[0:1, 512:1024], in_=rows[0:1, 512:1024]),
                reads=[b_rows], writes=[b_rows])
            p.add("vector", lambda v: v.tensor_scalar(
                out=mrun[1], in0=mrun[1], scalar1=-1.0, scalar2=None,
                op0=ALU.mult), reads=[b_mrun[1]], writes=[b_mrun[1]])
            p.add("vector", lambda v: v.tensor_tensor(
                out=mrun[0], in0=mrun[0], in1=mrun[1], op=ALU.max),
                reads=[b_mrun[0], b_mrun[1]], writes=[b_mrun[0]])
            for j in range(4):
                p.add("tensor", lambda t, j=j: t.transpose(
                    ps[7][:, j * 128:(j + 1) * 128],
                    mrun[0][:, j * 128:(j + 1) * 128], identf),
                    reads=[b_mrun[0], b_identf], writes=[b_ps[7]])
            for j in range(4):
                p.add("vector", lambda v, j=j: v.reduce_max(
                    out=cols[:, 40 + j:41 + j], in_=ps[7][:, j * 128:(j + 1) * 128],
                    axis=AX.X), reads=[b_ps[7]], writes=[b_cols])
            p.add("vector", lambda v: v.tensor_scalar(
                out=cols[:, 44:48], in0=cols[:, 40:44],
                scalar1=1e-5, scalar2=1.0 / 127.0, op0=ALU.max, op1=ALU.mult),
                reads=[b_cols], writes=[b_cols])
            p.add("vector", lambda v: v.reciprocal(
                out=cols[:, 48:52], in_=cols[:, 44:48]),
                reads=[b_cols], writes=[b_cols])
            p.add("vector", lambda v: v.tensor_scalar(
                out=cols[:, 52:56], in0=cols[:, 44:48],
                scalar1=swocol["o"], scalar2=127.0, op0=ALU.mult, op1=ALU.mult),
                reads=[b_cols, b_swocol["o"]], writes=[b_cols])
            for j in range(4):
                p.add("tensor", lambda t, j=j: t.transpose(
                    ps[5][0:1, j * 128:(j + 1) * 128], cols[:, 48 + j:49 + j],
                    identf), reads=[b_cols, b_identf], writes=[b_ps[5]])
            p.add("scalar", lambda a: a.copy(
                out=rows[0:1, 0:512], in_=ps[5][0:1, 0:512]),
                reads=[b_ps[5]], writes=[b_rows])
            bcast_scale(rows[0:1, 0:512], 2)
            for j in range(4):
                p.add("tensor", lambda t, j=j: t.transpose(
                    ps[5][0:1, j * 128:(j + 1) * 128], cols[:, 52 + j:53 + j],
                    identf), reads=[b_cols, b_identf], writes=[b_ps[5]])
            p.add("scalar", lambda a: a.copy(
                out=rows[0:1, 0:512], in_=ps[5][0:1, 0:512]),
                reads=[b_ps[5]], writes=[b_rows])
            p.add("vector", lambda v: v.tensor_tensor(
                out=rows[0:1, 0:512], in0=rows[0:1, 0:512],
                in1=rows[0:1, 512:1024], op=ALU.mult),
                reads=[b_rows], writes=[b_rows])
            bcast_scale(rows[0:1, 0:512], 1)
            yq82 = [yq8c, s2c[0].bitcast(I8)[:, 0:512]]
            b_yq82 = [b_yq8c, b_s2c[0]]
            for ec in range(KC):
                e2 = ec % 2
                p.add("scalar", lambda a, e2=e2, ec=ec: a.activation(
                    out=wyc[:, e2 * G:(e2 + 1) * G],
                    in_=A[:, ec * G:(ec + 1) * G], func=AF.Copy,
                    scale=normo_fm[:, ec:ec + 1]),
                    reads=[b_A, b_normo], writes=[b_wyc])
                p.add("vector", lambda v, e2=e2: v.tensor_tensor(
                    out=yq82[e2], in0=wyc[:, e2 * G:(e2 + 1) * G],
                    in1=sc3[2][:, 0:512], op=ALU.mult),
                    reads=[b_wyc, b_sc3[2]], writes=[b_yq82[e2]])
                if e2 == 0:
                    p.add("gpsimd", lambda gp, ec=ec, g=g, e2=e2: gp.tensor_copy(
                        out=xqT[g % 2][:, ec * G:(ec + 1) * G], in_=yq82[e2]),
                        reads=[b_yq82[e2]], writes=[b_xqT[g % 2]])
                else:
                    p.add("scalar", lambda a, ec=ec, g=g, e2=e2: a.activation(
                        out=xqT[g % 2][:, ec * G:(ec + 1) * G], in_=yq82[e2],
                        func=AF.Copy),
                        reads=[b_yq82[e2]], writes=[b_xqT[g % 2]])
            if g == 0:
                tap("yqT", yqT, b_yqT, [128, KC * G], "bfloat16")
        def d_part2(g, fixgen):
            yqg, b_yqg = xqT[g % 2], b_xqT[g % 2]
            for ep in range(KC // 2):
                pset = ep % 2
                ca, cb = 2 * ep, 2 * ep + 1
                so_ = 2 - 2 * pset
                wq_stream("o", ca, so_)
                wq_stream("o", cb, so_ + 1)
                blin_mm(yqg, b_yqg, so_, 4 + 2 * pset, ca)
                blin_mm(yqg, b_yqg, so_ + 1, 5 + 2 * pset, cb)
                if fixgen is not None:
                    next(fixgen, None)
                for sub, cx in ((0, ca), (1, cb)):
                    bank = 4 + 2 * pset + sub
                    p.add("vector", lambda v, bank=bank: v.tensor_tensor(
                        out=safm, in0=ps[bank], in1=sc3[1][:, 0:512],
                        op=ALU.mult),
                        reads=[b_ps[bank], b_sc3[1]], writes=[b_safm])
                    for j in range(4):
                        p.add("tensor", lambda t, j=j: t.transpose(
                            ps[0][:, j * 128:j * 128 + 128],
                            safm[:, j * 128:(j + 1) * 128], identf),
                            reads=[b_safm, b_identf], writes=[b_ps[0]])
                    p.add("scalar", lambda a, sub=sub: a.copy(
                        out=mrun[sub], in_=ps[0][:, 0:512]),
                        reads=[b_ps[0]], writes=[b_mrun[sub]])
                    p.add("gpsimd", lambda gp, sub=sub, cx=cx, g=g: gp.dma_start(
                        out=sattn_s[g * G:(g + 1) * G, cx * 128:(cx + 1) * 128]
                            .rearrange("(j q) c -> q j c", q=128),
                        in_=mrun[sub].rearrange("q (j c) -> q j c", c=128)),
                        reads=[b_mrun[sub]], writes=[dbuf["sattn"]], dma=True)
            if fixgen is not None:
                for _ in fixgen:
                    pass
            if g == 0:
                tap("sattn0", safm, b_safm, [128, 512], "float32")

        s1_group_driver()
        d_load(0)
        d_fix_head(0)
        exchange()
        for _ in d_fix_slices(0):
            pass
        d_fix_tail(0)
        for g in range(4):
            d_part1(g)
            if g + 1 < 4:
                d_load(g + 1)
                d_fix_head(g + 1)
                fixgen = d_fix_slices(g + 1)
            else:
                fixgen = None
            d_part2(g, fixgen)
            if g + 1 < 4:
                d_fix_tail(g + 1)
        p.barrier()
        p.flush(blk)

    if do_f1:
        with contextlib.ExitStack() as sc:
            xnT_sb, b_xnT_sb = sb("xnT_sb", [128, KC * TOK], BF16, sc)
            wg8t, b_wg8t = zip(*[sb(f"wg8t{i}", [128, H], I8, sc) for i in range(4)])
            wu8t, b_wu8t = zip(*[sb(f"wu8t{i}", [128, H], I8, sc) for i in range(4)])
            wgblk, b_wgblk = zip(*[sb(f"wgblk{i}", [128, H], BF16, sc) for i in range(3)])
            wublk, b_wublk = zip(*[sb(f"wublk{i}", [128, H], BF16, sc) for i in range(3)])
            gbf, b_gbf = zip(*[sb(f"gbf{i}", [128, 512], F32, sc) for i in range(4)])
            ubf, b_ubf = zip(*[sb(f"ubf{i}", [128, 512], F32, sc) for i in range(4)])
            fbf, b_fbf = zip(*[sb(f"fbf{i}", [128, 512], BF16, sc) for i in range(4)])
            cengs = ["vector", "scalar"]

            def conv(eng, out, in_, reads, writes):
                if eng == "scalar":
                    p.add(eng, lambda a, o=out, i=in_: a.activation(
                        out=o, in_=i, func=AF.Copy), reads=reads, writes=writes)
                else:
                    p.add(eng, lambda v, o=out, i=in_: v.tensor_copy(
                        out=o, in_=i), reads=reads, writes=writes)

            for kc in range(KC):
                p.add("sync", lambda s, kc=kc: s.dma_start(
                    out=xnT_sb[:, kc * TOK:(kc + 1) * TOK],
                    in_=xnT_s[kc * 128:(kc + 1) * 128, :]),
                    reads=[dbuf["xnT"]], writes=[b_xnT_sb], dma=True)
            for oc in range(OCN):
                o2 = oc % 3
                i4 = oc % 4
                sl = slice(oc * 128, (oc + 1) * 128)
                p.add("sync", lambda s, o=wg8t[i4], i=wg8[sl, :]:
                      s.dma_start(out=o, in_=i), writes=[b_wg8t[i4]], dma=True)
                p.add("sync", lambda s, o=wu8t[i4], i=wu8[sl, :]:
                      s.dma_start(out=o, in_=i), writes=[b_wu8t[i4]], dma=True)
                conv(cengs[oc % 2], wgblk[o2], wg8t[i4],
                     [b_wg8t[i4]], [b_wgblk[o2]])
                conv(cengs[(oc + 1) % 2], wublk[o2], wu8t[i4],
                     [b_wu8t[i4]], [b_wublk[o2]])
                for hc in range(KC):
                    for tt in range(4):
                        p.add("tensor", lambda t, hc=hc, tt=tt, o2=o2:
                              t.matmul(ps[tt],
                                       wgblk[o2][:, hc * 128:(hc + 1) * 128],
                                       xnT_sb[:, hc * TOK + tt * 512:
                                              hc * TOK + tt * 512 + 512],
                                       start=(hc == 0), stop=(hc == KC - 1)),
                              reads=[b_wgblk[o2], b_xnT_sb],
                              writes=[b_ps[tt]])
                for hc in range(KC):
                    for tt in range(4):
                        p.add("tensor", lambda t, hc=hc, tt=tt, o2=o2:
                              t.matmul(ps[4 + tt],
                                       wublk[o2][:, hc * 128:(hc + 1) * 128],
                                       xnT_sb[:, hc * TOK + tt * 512:
                                              hc * TOK + tt * 512 + 512],
                                       start=(hc == 0), stop=(hc == KC - 1)),
                              reads=[b_wublk[o2], b_xnT_sb],
                              writes=[b_ps[4 + tt]])
                for tt in range(4):
                    p.add("scalar", lambda a, tt=tt, oc=oc:
                          a.activation(out=gbf[tt], in_=ps[tt], func=AF.Silu,
                                       scale=sgate_fm[:, oc:oc + 1]),
                          reads=[b_ps[tt], b_sgate], writes=[b_gbf[tt]])
                    p.add("vector", lambda v, tt=tt, oc=oc:
                          v.tensor_scalar(out=ubf[tt], in0=ps[4 + tt],
                                          scalar1=sup_fm[:, oc:oc + 1],
                                          scalar2=None, op0=ALU.mult),
                          reads=[b_ps[4 + tt], b_sup], writes=[b_ubf[tt]])
                    p.add("vector", lambda v, tt=tt: v.tensor_tensor(
                        out=fbf[tt], in0=gbf[tt], in1=ubf[tt], op=ALU.mult),
                        reads=[b_gbf[tt], b_ubf[tt]], writes=[b_fbf[tt]])
                    p.add("gpsimd", lambda gp, oc=oc, tt=tt: gp.dma_start(
                        out=fuse_s[oc * 128:(oc + 1) * 128,
                                   tt * 512:(tt + 1) * 512],
                        in_=fbf[tt]),
                        reads=[b_fbf[tt]], writes=[dbuf["fuse"]], dma=True)
            p.barrier()
            p.flush(blk)

    for m in "figo":
        if do_prep and f"wq_{m}" in _TAPS:
            t = nc.dram_tensor(f"tap_wq_{m}", [H, H], BF16,
                               kind="ExternalOutput").ap()
            taps.append(f"wq_{m}")
            p.add("gpsimd", lambda g, t=t, m=m: g.dma_start(out=t, in_=wq_s[m]),
                  reads=[dbuf["wq_" + m]], dma=True)
    if do_prep and "swc" in _TAPS:
        t = nc.dram_tensor("tap_swc", [128, 8], F32, kind="ExternalOutput").ap()
        taps.append("swc")
        p.add("gpsimd", lambda g, t=t: g.dma_start(out=t, in_=swc),
              reads=[b_swcol["o"], b_swocol["o"]], dma=True)
    p.flush(blk)
    if do_f1 and "fuse_full" in _TAPS:
        t = nc.dram_tensor("tap_fuse_full", [IFF, TOK], BF16,
                           kind="ExternalOutput").ap()
        taps.append("fuse_full")
        p.add("gpsimd", lambda g, t=t: g.dma_start(out=t, in_=fuse_s),
              reads=[dbuf["fuse"]], dma=True)
    if do_groups and "sattn_full" in _TAPS:
        t = nc.dram_tensor("tap_sattn_full", [TOK, H], F32,
                           kind="ExternalOutput").ap()
        taps.append("sattn_full")
        p.add("gpsimd", lambda g, t=t: g.dma_start(out=t, in_=sattn_s),
              reads=[dbuf["sattn"]], dma=True)
    if do_groups and "xnT_full" in _TAPS:
        t = nc.dram_tensor("tap_xnT_full", [H, TOK], BF16,
                           kind="ExternalOutput").ap()
        taps.append("xnT_full")
        p.add("gpsimd", lambda g, t=t: g.dma_start(out=t, in_=xnT_s),
              reads=[dbuf["xnT"]], dma=True)
    p.flush(blk)

    if do_f2:
        with contextlib.ExitStack() as sc:
            sdownbc, b_sdownbc = sb("sdownbc", [128, H], F32, sc)
            wdblk, b_wdblk = zip(*[sb(f"wdblk{i}", [128, H], BF16, sc) for i in range(4)])
            wd8t, b_wd8t = zip(*[sb(f"wd8t{i}", [128, H], I8, sc) for i in range(4)])
            fch, b_fch = zip(*[sb(f"fch{i}", [128, 256], BF16, sc) for i in range(4)])
            satile, b_satile = zip(*[sb(f"satile{i}", [128, H], F32, sc) for i in range(2)])
            xtile, b_xtile = zip(*[sb(f"xtile{i}", [128, H], F32, sc) for i in range(2)])
            otile, b_otile = zip(*[sb(f"otile{i}", [128, H], F32, sc) for i in range(2)])
            t1c, b_t1c = zip(*[sb(f"t1c{i}", [128, 512], F32, sc) for i in range(2)])
            gfc, b_gfc = zip(*[sb(f"gfc{i}", [128, 512], F32, sc) for i in range(2)])
            sgc, b_sgc = zip(*[sb(f"sgc{i}", [128, 512], F32, sc) for i in range(2)])
            p.add("sync", lambda s: s.dma_start(
                out=sdownbc, in_=sdown_r.partition_broadcast(128)),
                writes=[b_sdownbc], dma=True)
            for pa in range(8):
                t0 = pa * 256
                for ic in range(OCN):
                    i2 = ic % 4
                    p.add("sync", lambda s, ic=ic, i2=i2: s.dma_start(
                        out=wd8t[i2], in_=wd8[ic * 128:(ic + 1) * 128, :]),
                        writes=[b_wd8t[i2]], dma=True)
                    if ic % 2 == 0:
                        p.add("vector", lambda v, i2=i2: v.tensor_copy(
                            out=wdblk[i2], in_=wd8t[i2]),
                            reads=[b_wd8t[i2]], writes=[b_wdblk[i2]])
                    else:
                        p.add("scalar", lambda a, i2=i2: a.activation(
                            out=wdblk[i2], in_=wd8t[i2], func=AF.Copy),
                            reads=[b_wd8t[i2]], writes=[b_wdblk[i2]])
                    p.add("gpsimd", lambda gp, ic=ic, i2=i2, t0=t0: gp.dma_start(
                        out=fch[i2], in_=fuse_s[ic * 128:(ic + 1) * 128, t0:t0 + 256]),
                        reads=[dbuf["fuse"]], writes=[b_fch[i2]], dma=True)
                    for ti in range(2):
                        for nh in range(4):
                            p.add("tensor", lambda t, ic=ic, i2=i2, ti=ti, nh=nh:
                                  t.matmul(ps[ti * 4 + nh],
                                           fch[i2][:, ti * 128:(ti + 1) * 128],
                                           wdblk[i2][:, nh * 512:(nh + 1) * 512],
                                           start=(ic == 0), stop=(ic == OCN - 1)),
                                  reads=[b_fch[i2], b_wdblk[i2]],
                                  writes=[b_ps[ti * 4 + nh]])
                for ti in range(2):
                    r0 = t0 + ti * 128
                    i2 = ti % 2
                    p.add("gpsimd", lambda gp, r0=r0, i2=i2: gp.dma_start(
                        out=xtile[i2], in_=xm[r0:r0 + 128, :]),
                        writes=[b_xtile[i2]], dma=True)
                    p.add("gpsimd", lambda gp, r0=r0, i2=i2: gp.dma_start(
                        out=satile[i2], in_=sattn_s[r0:r0 + 128, :]),
                        reads=[dbuf["sattn"]], writes=[b_satile[i2]], dma=True)
                    for nh in range(4):
                        n2 = nh % 2
                        p.add("vector", lambda v, ti=ti, nh=nh, n2=n2: v.tensor_tensor(
                            out=t1c[n2], in0=ps[ti * 4 + nh],
                            in1=sdownbc[:, nh * 512:(nh + 1) * 512], op=ALU.mult),
                            reads=[b_ps[ti * 4 + nh], b_sdownbc], writes=[b_t1c[n2]])
                        p.add("scalar", lambda a, n2=n2: a.activation(
                            out=gfc[n2], in_=t1c[n2], func=AF.Silu),
                            reads=[b_t1c[n2]], writes=[b_gfc[n2]])
                        p.add("vector", lambda v, i2=i2, nh=nh, n2=n2: v.tensor_tensor(
                            out=sgc[n2], in0=gfc[n2],
                            in1=satile[i2][:, nh * 512:(nh + 1) * 512], op=ALU.mult),
                            reads=[b_gfc[n2], b_satile[i2]], writes=[b_sgc[n2]])
                        p.add("vector", lambda v, i2=i2, nh=nh, n2=n2: v.tensor_tensor(
                            out=otile[i2][:, nh * 512:(nh + 1) * 512], in0=sgc[n2],
                            in1=xtile[i2][:, nh * 512:(nh + 1) * 512], op=ALU.add),
                            reads=[b_sgc[n2], b_xtile[i2]], writes=[b_otile[i2]])
                    p.add("gpsimd", lambda gp, r0=r0, i2=i2: gp.dma_start(
                        out=out_d[r0:r0 + 128, :], in_=otile[i2]),
                        reads=[b_otile[i2]], writes=[dbuf["out"]], dma=True)
            p.add("sync", lambda s: s.nop(),
                  extra=[dbuf["out"].w] if dbuf["out"].w else [])
            p.flush(blk)

    p.flush(blk)
    ctx.close()
    return nc, taps


def _host_prep(Wi, Wf, Wg, Wo, wg8, wu8, wd8, norms):
    bf = ml_dtypes.bfloat16
    normi, normf, normg, normo, rmsw, sgate, sup, sdown = norms
    return {
        "wfsw": _blin_swz(Wf),
        "wisw": _blin_swz(Wi),
        "wgsw": _blin_swz(Wg),
        "wosw": _blin_swz(Wo),
        "wg8": _ffn_swz(wg8),
        "wu8": _ffn_swz(wu8),
        "wd8": np.ascontiguousarray(np.asarray(wd8, np.int8)),
        "normf_r": np.asarray(normf, np.float32).reshape(1, H),
        "normi_r": np.asarray(normi, np.float32).reshape(1, H),
        "normg_r": np.asarray(normg, np.float32).reshape(1, H),
        "normo_fm_h": np.ascontiguousarray(
            np.asarray(normo, np.float32).reshape(KC, 128).T),
        "rmsw_fm_h": np.ascontiguousarray(
            np.asarray(rmsw, np.float32).reshape(KC, 128).T),
        "sgate_fm_h": np.ascontiguousarray(
            np.asarray(sgate, np.float32).reshape(OCN, 128).T),
        "sup_fm_h": np.ascontiguousarray(
            np.asarray(sup, np.float32).reshape(OCN, 128).T),
        "sdown_r": np.asarray(sdown, np.float32).reshape(1, H),
        "identb_h": np.eye(128).astype(bf),
        "identf_h": np.eye(128, dtype=np.float32),
        "onesf_h": np.ones((1, 128), np.float32),
        "onesb_h": np.ones((128, 1), bf),
        "onescf_h": np.ones((128, 1), np.float32),
    }


def kernel(L0_attn_x, L0_attn_Wi, L0_attn_Wf, L0_attn_Wg, L0_attn_Wo,
           L0_attn_norm_i, L0_attn_norm_f, L0_attn_norm_g, L0_attn_norm_o,
           L0_attn_rms_w, L0_ffn_w_gate, L0_ffn_w_up, L0_ffn_w_down,
           L0_ffn_s_gate, L0_ffn_s_up, L0_ffn_s_down):
    x = np.asarray(L0_attn_x, np.float32)
    B, L, _ = x.shape
    shared = _host_prep(L0_attn_Wi, L0_attn_Wf, L0_attn_Wg, L0_attn_Wo,
                        L0_ffn_w_gate, L0_ffn_w_up, L0_ffn_w_down,
                        (L0_attn_norm_i, L0_attn_norm_f, L0_attn_norm_g,
                         L0_attn_norm_o, L0_attn_rms_w, L0_ffn_s_gate,
                         L0_ffn_s_up, L0_ffn_s_down))
    nc, _ = build_program()
    in_maps = []
    for c in range(8):
        b = c % 4
        hf = c // 4
        m = dict(shared)
        m["xm"] = np.ascontiguousarray(x[b, hf * TOK:(hf + 1) * TOK])
        sm = np.zeros((128, 8), np.float32)
        if hf == 1:
            sm[:, b] = 1.0
        m["selmask"] = sm
        in_maps.append(m)
    res = run_bass_kernel_spmd(nc, in_maps, core_ids=list(range(8)))
    out = np.empty((B, L, H), np.float32)
    for c, r in enumerate(res.results):
        b = c % 4
        hf = c // 4
        out[b, hf * TOK:(hf + 1) * TOK] = r["out"]
    return out


def _dev_run(in_map, taps=(), n=1):
    _TAPS.update(taps)
    nc, _ = build_program()
    outs = []
    for _ in range(n):
        res = run_bass_kernel_spmd(nc, [dict(in_map)], core_ids=[0])
        outs.append(res.results[0])
    return outs


# version 27
# speedup vs baseline: 1.0136x; 1.0056x over previous
import contextlib
import numpy as np
import ml_dtypes

try:
    import concourse.bass as bass
except ImportError:
    import sys
    sys.path.insert(0, "/opt/trn_rl_repo")
    import concourse.bass as bass
import concourse.mybir as mybir
from concourse.bass_utils import run_bass_kernel_spmd

F32 = mybir.dt.float32
BF16 = mybir.dt.bfloat16
I8 = mybir.dt.int8
I32 = mybir.dt.int32
AF = mybir.ActivationFunctionType
ALU = mybir.AluOpType
AX = mybir.AxisListType

H = 2048
IFF = 8192
KC = H // 128
OCN = IFF // 128
G = 512
TTN = G // 128
NG = 4
TOK = 2048
ENGINES = ("sync", "tensor", "vector", "scalar", "gpsimd")
SELF_GAP = 4

_TAPS = set()



def _w_sync_c(pr, sem, v):
    pr.wait_ge(sem, v)

def _w_tensor_c(pr, sem, v):
    pr.wait_ge(sem, v)

def _w_vector_c(pr, sem, v):
    pr.wait_ge(sem, v)

def _w_scalar_c(pr, sem, v):
    pr.wait_ge(sem, v)

def _w_gpsimd_c(pr, sem, v):
    pr.wait_ge(sem, v)

def _w_sync_d(pr, sem, v):
    pr.wait_ge(sem, v)

def _w_tensor_d(pr, sem, v):
    pr.wait_ge(sem, v)

def _w_vector_d(pr, sem, v):
    pr.wait_ge(sem, v)

def _w_scalar_d(pr, sem, v):
    pr.wait_ge(sem, v)

def _w_gpsimd_d(pr, sem, v):
    pr.wait_ge(sem, v)

_WAITERS = {("sync", "c"): _w_sync_c, ("tensor", "c"): _w_tensor_c,
            ("vector", "c"): _w_vector_c, ("scalar", "c"): _w_scalar_c,
            ("gpsimd", "c"): _w_gpsimd_c,
            ("sync", "d"): _w_sync_d, ("tensor", "d"): _w_tensor_d,
            ("vector", "d"): _w_vector_d, ("scalar", "d"): _w_scalar_d,
            ("gpsimd", "d"): _w_gpsimd_d}


class Buf:
    __slots__ = ("w", "r")

    def __init__(self):
        self.w = None
        self.r = {}


NSLOT = {"sync": 24, "gpsimd": 16, "vector": 2, "scalar": 2, "tensor": 1}


class Prog:

    def __init__(self, nc):
        self.nc = nc
        self.cnt = {e: 0 for e in ENGINES}
        self.dcnt = {(e, k): 0 for e in ENGINES for k in range(NSLOT[e])}
        self.dslot = {e: 0 for e in ENGINES}
        self.idx = {e: 0 for e in ENGINES}
        self.pend = {e: [] for e in ENGINES}
        self.waited = {}
        self.semc = {}
        self.semd = {}

    def add(self, eng, fn, reads=(), writes=(), dma=False, cc=False, extra=()):
        idx = self.idx[eng] + 1
        waits = {}

        def dep(t):
            if t is None:
                return
            pe, pk, pc, pidx = t
            if pc <= 0:
                return
            if pk == "c" and pe == eng:
                if pe != "tensor" and idx - pidx <= SELF_GAP:
                    waits[(pe, pk)] = max(waits.get((pe, pk), 0), pc)
            else:
                waits[(pe, pk)] = max(waits.get((pe, pk), 0), pc)

        for b in reads:
            dep(b.w)
        for b in writes:
            dep(b.w)
            for pe, (pk, pc, pidx) in b.r.items():
                dep((pe, pk, pc, pidx))
        for t in extra:
            dep(t)

        if cc:
            self.cccnt = getattr(self, "cccnt", 0) + 1
            kind = ("cc", 0)
            mycnt = self.cccnt
        elif dma:
            k = self.dslot[eng]
            self.dslot[eng] = (k + 1) % NSLOT[eng]
            if self.dcnt[(eng, k)] > 0:
                waits[(eng, ("d", k))] = max(
                    waits.get((eng, ("d", k)), 0), self.dcnt[(eng, k)])
            self.dcnt[(eng, k)] += 16
            kind = ("d", k)
            mycnt = self.dcnt[(eng, k)]
        else:
            self.cnt[eng] += 1
            kind = "c"
            mycnt = self.cnt[eng]

        if cc:
            pass
        final = []
        for (pe, pk), pc in sorted(waits.items(), key=str):
            key = (eng, pe, pk)
            if self.waited.get(key, -1) >= pc:
                continue
            self.waited[key] = pc
            final.append((pe, pk, pc))

        self.idx[eng] = idx
        self.pend[eng].append((final, fn, kind))
        t = (eng, kind, mycnt, idx)
        for b in reads:
            b.r[eng] = (kind, mycnt, idx)
        for b in writes:
            b.w = t
            b.r = {}
        return t

    def flush(self, blk):
        for eng in ENGINES:
            ops = self.pend[eng]
            if not ops:
                continue
            self.pend[eng] = []

            def emit(proxy, ops=ops, eng=eng):
                for waits, fn, kind in ops:
                    for pe, pk, pc in waits:
                        if pk == "c":
                            _WAITERS[(pe, "c")](proxy, self.semc[pe], pc)
                        elif pk[0] == "cc":
                            proxy.wait_ge(self.ccsem, 1)
                        else:
                            _WAITERS[(pe, "d")](proxy, self.semd[pe][pk[1]], pc)
                    instr = fn(proxy)
                    if kind == "c":
                        instr.then_inc(self.semc[eng], 1)
                    elif kind[0] == "cc":
                        instr.then_inc(self.ccsem)
                    else:
                        instr.then_inc(self.semd[eng][kind[1]], 16)

            getattr(blk, eng)(emit)

    def barrier(self):
        toks = []
        if getattr(self, "cccnt", 0) > 0:
            toks.append(("gpsimd", ("cc", 0), 1, 0))
        for e in ENGINES:
            if self.cnt[e] > 0:
                toks.append((e, "c", self.cnt[e], self.idx[e]))
            for k in range(NSLOT[e]):
                if self.dcnt[(e, k)] > 0:
                    toks.append((e, ("d", k), self.dcnt[(e, k)], self.idx[e]))
        for e in ENGINES:
            self.add(e, lambda p: p.nop(), extra=toks)


def _blin_swz(w):
    return np.ascontiguousarray(
        np.asarray(w, np.float32).reshape(16, 128, 16, 128)
        .transpose(0, 3, 2, 1).reshape(2048, 2048))


def _ffn_swz(w):
    return np.ascontiguousarray(
        np.asarray(w, np.int8).reshape(16, 128, 64, 128)
        .transpose(2, 1, 0, 3).reshape(8192, 2048))


def build_program(n_pre=NG, n_main=NG, do_prep=True, do_groups=True,
                  do_f1=True, do_f2=True):
    nc = bass.Bass("TRN2", target_bir_lowering=False)
    p = Prog(nc)
    taps = []

    def din(name, shape, dtype):
        return nc.dram_tensor(name, shape, dtype, kind="ExternalInput").ap()

    xm = din("xm", [TOK, H], F32)
    selmask = din("selmask", [128, 8], F32)
    wsw = {m: din(f"w{m}sw", [H, H], F32) for m in "figo"}
    wg8 = din("wg8", [IFF, H], I8)
    wu8 = din("wu8", [IFF, H], I8)
    wd8 = din("wd8", [IFF, H], I8)
    norm_r = {m: din(f"norm{m}_r", [1, H], F32) for m in "fig"}
    normo_fm_h = din("normo_fm_h", [128, KC], F32)
    rmsw_fm_h = din("rmsw_fm_h", [128, KC], F32)
    sgate_fm_h = din("sgate_fm_h", [128, OCN], F32)
    sup_fm_h = din("sup_fm_h", [128, OCN], F32)
    sdown_r = din("sdown_r", [1, H], F32)
    identb_h = din("identb_h", [128, 128], BF16)
    identf_h = din("identf_h", [128, 128], F32)
    onesf_h = din("onesf_h", [1, 128], F32)
    onesb_h = din("onesb_h", [128, 1], BF16)
    onescf_h = din("onescf_h", [128, 1], F32)
    out_d = nc.dram_tensor("out", [TOK, H], F32, kind="ExternalOutput").ap()

    wq_s = {m: nc.dram_tensor(f"wq_{m}", [H, H], BF16, kind="Internal").ap()
            for m in "figo"}
    wdb_s = nc.dram_tensor("wdb", [IFF, H], BF16, kind="Internal").ap()
    xnT_s = nc.dram_tensor("xnT_s", [H, TOK], BF16, kind="Internal").ap()
    sattn_s = nc.dram_tensor("sattn_s", [TOK, H], F32, kind="Internal").ap()
    fuse_s = nc.dram_tensor("fuse_s", [IFF, TOK], BF16, kind="Internal").ap()
    sloc_s = nc.dram_tensor("sloc_s", [4 * 128, KC * G], F32, kind="Internal").ap()
    cprod_s = nc.dram_tensor("cprod_s", [4 * 128, KC * G], F32, kind="Internal").ap()
    xqg_s = nc.dram_tensor("xqg_s", [4 * 128, KC * G], BF16, kind="Internal").ap()
    carry_s = nc.dram_tensor("carry_s", [128, KC], F32, kind="Internal").ap()
    ag_s = nc.dram_tensor("ag_s", [8 * 128, KC], F32, kind="Internal").ap()
    sgrow_s = nc.dram_tensor("sgrow_s", [4, 512], F32, kind="Internal").ap()

    dbuf = {name: Buf() for name in
            ["wq_f", "wq_i", "wq_g", "wq_o", "wdb",
             "xnT", "sattn", "fuse", "out", "sloc", "cprod", "xqg",
             "carry", "ag", "sgrow"]}

    ctx = contextlib.ExitStack()
    for e in ENGINES:
        p.semc[e] = ctx.enter_context(nc.semaphore(f"sc_{e}"))
        p.semd[e] = [ctx.enter_context(nc.semaphore(f"sd_{e}{k}"))
                     for k in range(NSLOT[e])]
    p.ccsem = ctx.enter_context(nc.semaphore("ccsem"))

    def sb(name, shape, dtype, scope):
        h = scope.enter_context(nc.sbuf_tensor(name, shape, dtype))
        return h.ap(), Buf()

    def tap(name, ap, buf, shape, dtype):
        if name not in _TAPS:
            return
        dtype = {"float32": F32, "bfloat16": BF16}[dtype] if isinstance(dtype, str) else dtype
        t = nc.dram_tensor("tap_" + name, shape, dtype, kind="ExternalOutput").ap()
        taps.append(name)
        p.add("gpsimd", lambda g, t=t, ap=ap: g.dma_start(out=t, in_=ap),
              reads=[buf], dma=True)

    identb, b_identb = sb("identb", [128, 128], BF16, ctx)
    identf, b_identf = sb("identf", [128, 128], F32, ctx)
    onesf, b_onesf = sb("onesf", [1, 128], F32, ctx)
    onesb, b_onesb = sb("onesb", [128, 1], BF16, ctx)
    onescf, b_onescf = sb("onescf", [128, 1], F32, ctx)
    normo_fm, b_normo = sb("normo_fm", [128, KC], F32, ctx)
    rmsw_fm, b_rmsw = sb("rmsw_fm", [128, KC], F32, ctx)
    sgate_fm, b_sgate = sb("sgate_fm", [128, OCN], F32, ctx)
    sup_fm, b_sup = sb("sup_fm", [128, OCN], F32, ctx)
    swc, b_swc_t = sb("swc", [128, 8], F32, ctx)
    swcol = {m: swc[:, j:j + 1] for j, m in enumerate("figo")}
    swocol = {m: swc[:, 4 + j:5 + j] for j, m in enumerate("figo")}
    b_swcol = {m: Buf() for m in "figo"}
    b_swocol = {m: Buf() for m in "figo"}

    pall_t = ctx.enter_context(nc.psum_tensor("pall", [128, 4096], F32))
    pall = pall_t.ap()
    pallb = pall.bitcast(BF16)
    ps = [pall[:, 512 * i:512 * (i + 1)] for i in range(8)]
    ps2 = [pall[:, 1024 * i:1024 * (i + 1)] for i in range(4)]
    b_ps = [Buf() for _ in range(8)]

    def psb(i):
        return pallb[:, 1024 * i:1024 * (i + 1)]

    for ap_, buf_, src in [(identb, b_identb, identb_h), (identf, b_identf, identf_h),
                           (onesf, b_onesf, onesf_h), (onesb, b_onesb, onesb_h), (onescf, b_onescf, onescf_h),
                           (normo_fm, b_normo, normo_fm_h), (rmsw_fm, b_rmsw, rmsw_fm_h),
                           (sgate_fm, b_sgate, sgate_fm_h), (sup_fm, b_sup, sup_fm_h)]:
        p.add("sync", lambda s, o=ap_, i=src: s.dma_start(out=o, in_=i),
              writes=[buf_], dma=True)

    blk = ctx.enter_context(nc.Block())

    if do_prep:
        with contextlib.ExitStack() as sc:
            wfp, b_wfp = zip(*[sb(f"wfp{i}", [128, H], F32, sc) for i in range(4)])
            wab, b_wab = zip(*[sb(f"wab{i}", [128, H], F32, sc) for i in range(2)])
            q32, b_q32 = zip(*[sb(f"q32_{i}", [128, H], I32, sc) for i in range(2)])
            qbf, b_qbf = zip(*[sb(f"qbf{i}", [128, H], BF16, sc) for i in range(2)])
            prow, b_prow = sb("prow", [1, 512], F32, sc)

            def s1_iter(m, ec):
                i0 = 2 + ec % 2
                ea = ec % 2
                src = wsw[m]
                p.add("sync", lambda s, o=wfp[i0], i=src[ec * 128:(ec + 1) * 128, :]:
                      s.dma_start(out=o, in_=i), writes=[b_wfp[i0]], dma=True)
                p.add("scalar", lambda a, i0=i0, ea=ea: a.activation(
                    out=wab[ea], in_=wfp[i0], func=AF.Abs),
                    reads=[b_wfp[i0]], writes=[b_wab[ea]])
                for j in range(4):
                    p.add("tensor", lambda t, ea=ea, j=j, st=(ec == 0 and j == 0),
                          sp=(ec == KC - 1 and j == 3):
                          t.matmul(ps[5][0:1, :], onescf,
                                   wab[ea][:, j * 512:(j + 1) * 512],
                                   start=st, stop=sp),
                          reads=[b_wab[ea], b_onescf], writes=[b_ps[5]])

            def s1_tail(m):
                p.add("vector", lambda v: v.tensor_reduce(
                    out=prow[0:1, 0:1], in_=ps[5][0:1, :], axis=AX.X, op=ALU.add),
                    reads=[b_ps[5]], writes=[b_prow])
                p.add("vector", lambda v: v.tensor_scalar(
                    out=prow[0:1, 1:2], in0=prow[0:1, 0:1], scalar1=1.0 / (H * H),
                    scalar2=1e-5, op0=ALU.mult, op1=ALU.max),
                    reads=[b_prow], writes=[b_prow])
                p.add("vector", lambda v: v.tensor_scalar(
                    out=prow[0:1, 2:3], in0=prow[0:1, 1:2], scalar1=1.0 / 127.0,
                    scalar2=None, op0=ALU.mult), reads=[b_prow], writes=[b_prow])
                p.add("vector", lambda v: v.reciprocal(
                    out=prow[0:1, 3:4], in_=prow[0:1, 1:2]),
                    reads=[b_prow], writes=[b_prow])
                p.add("tensor", lambda t: t.matmul(
                    ps[6][:, 0:2], onesf, prow[0:1, 2:4], start=True, stop=True),
                    reads=[b_prow, b_onesf], writes=[b_ps[6]])
                jm = "figo".index(m)
                p.add("vector", lambda v, jm=jm, m=m: v.tensor_copy(
                    out=swc[:, 4 + jm:5 + jm], in_=ps[6][:, 0:1]),
                    reads=[b_ps[6]], writes=[b_swocol[m]])
                p.add("vector", lambda v, jm=jm, m=m: v.tensor_copy(
                    out=swc[:, jm:jm + 1], in_=ps[6][:, 1:2]),
                    reads=[b_ps[6]], writes=[b_swcol[m]])

            def s2_iter(m, ec):
                i0 = ec % 2
                src = wsw[m]
                p.add("sync", lambda s, o=wfp[i0], i=src[ec * 128:(ec + 1) * 128, :]:
                      s.dma_start(out=o, in_=i), writes=[b_wfp[i0]], dma=True)
                p.add("vector", lambda v, i0=i0, m=m: v.tensor_scalar(
                    out=q32[i0], in0=wfp[i0], scalar1=swcol[m], scalar2=None,
                    op0=ALU.mult),
                    reads=[b_wfp[i0], b_swcol[m]], writes=[b_q32[i0]])
                p.add("vector", lambda v, i0=i0: v.tensor_scalar(
                    out=qbf[i0], in0=q32[i0], scalar1=1.0, scalar2=-1.0,
                    op0=ALU.min, op1=ALU.max), reads=[b_q32[i0]], writes=[b_qbf[i0]])
                p.add("gpsimd", lambda g, i0=i0, m=m, ec=ec: g.dma_start(
                    out=wq_s[m][ec * 128:(ec + 1) * 128, :], in_=qbf[i0]),
                    reads=[b_qbf[i0]], writes=[dbuf["wq_" + m]], dma=True)

            order = "figo"
            for ec in range(KC):
                s1_iter("f", ec)
            s1_tail("f")
            for mi in range(4):
                m = order[mi]
                nxt = order[mi + 1] if mi + 1 < 4 else None
                for ec in range(KC):
                    s2_iter(m, ec)
                    if nxt is not None:
                        s1_iter(nxt, ec)
                if nxt is not None:
                    s1_tail(nxt)
            p.barrier()
            p.flush(blk)

    if do_groups:
      with contextlib.ExitStack() as sc:
        normbc, b_normbc = zip(*[sb(f"normbc{i}", [128, H], F32, sc) for i in range(1)])
        A, b_A = sb("Abuf", [128, KC * G], F32, sc)
        sfull, b_s = sb("sfull", [128, KC * G], F32, sc)
        xg, b_xg = zip(*[sb(f"xg{i}", [128, H], F32, sc) for i in range(2)])
        wb, b_wb = sb("wbuf", [128, H], F32, sc)
        xq8 = wb.bitcast(I8)[:, 0:H]
        b_xq8 = b_wb
        xqtm, b_xqtm = zip(*[sb(f"xqtm{i}", [128, H], BF16, sc) for i in range(2)])
        xqT2, b_xqT2_t = sb("xqT2", [128, 2 * KC * G], BF16, sc)
        xqT = [xqT2[:, 0:KC * G], xqT2[:, KC * G:]]
        b_xqT = [Buf(), Buf()]
        yqT, b_yqT = xqT[0], b_xqT[0]
        wqblk, b_wqblk = zip(*[sb(f"wqblk{i}", [128, H], BF16, sc) for i in range(4)])
        ch, b_ch = zip(*[sb(f"ch{i}", [128, 1024], F32, sc) for i in range(6)])
        zf, b_zf = ch[0], b_ch[0]
        zi, b_zi = ch[1], b_ch[1]
        thc, b_thc = ch[2], b_ch[2]
        fc, b_fc = ch[3], b_ch[3]
        fm1, b_fm1 = ch[4], b_ch[4]
        fic, b_fic = ch[5], b_ch[5]
        zg, b_zg = ch[0], b_ch[0]
        thg, b_thg = ch[2], b_ch[2]
        fgc, b_fgc = ch[3], b_ch[3]
        shatc, b_shatc = ch[4], b_ch[4]
        tyc, b_tyc = ch[5], b_ch[5]
        wyc, b_wyc = ch[1], b_ch[1]
        mrun, b_mrun = zip(*[sb(f"mrun{i}", [128, 512], F32, sc) for i in range(2)])
        sc3, b_sc3 = zip(*[sb(f"scbc{i}", [128, 1024], F32, sc) for i in range(3)])
        s2c, b_s2c = zip(*[sb(f"s2c{i}", [128, 512], BF16, sc) for i in range(1)])
        yq8c, b_yq8c = sb("yq8c", [128, 512], I8, sc)
        safm, b_safm = sb("safm", [128, 512], F32, sc)
        sasm, b_sasm = zip(*[sb(f"sasm{i}", [128, 128], F32, sc) for i in range(2)])
        xnstage, b_xnstage = sb("xnstage", [128, 8 * 128], BF16, sc)
        carry, b_carry = sb("carry", [128, KC], F32, sc)
        cols, b_cols = sb("cols", [128, 64], F32, sc)
        rows, b_rows = sb("rows", [1, 1024], F32, sc)

        p.add("vector", lambda v: v.memset(carry, 0.0), writes=[b_carry])

        def ln_pg_loads(xsrc, g, pg):
            for ti in range(2):
                tt = pg * 2 + ti
                r0 = (g % NG) * G + tt * 128
                p.add("sync", lambda s, ti=ti, i=xsrc[r0:r0 + 128, :]:
                      s.dma_start(out=xg[ti], in_=i), writes=[b_xg[ti]], dma=True)
                p.add("vector", lambda v, tt=tt, ti=ti: v.reduce_sum(
                    out=cols[:, tt:tt + 1], in_=xg[ti], axis=AX.X),
                    reads=[b_xg[ti]], writes=[b_cols])
                p.add("scalar", lambda a, tt=tt, ti=ti: a.activation(
                    out=xqtm[ti], in_=xg[ti], func=AF.Square,
                    accum_out=cols[:, 8 + tt:9 + tt]),
                    reads=[b_xg[ti]], writes=[b_xqtm[ti], b_cols])

        def ln_pg_chain(pg):
            t0 = pg * 2
            p.add("vector", lambda v, t0=t0: v.tensor_scalar(
                out=cols[:, 4 + t0:6 + t0], in0=cols[:, t0:t0 + 2],
                scalar1=1.0 / H, scalar2=None, op0=ALU.mult),
                reads=[b_cols], writes=[b_cols])
            p.add("vector", lambda v, t0=t0: v.tensor_tensor(
                out=cols[:, 12 + t0:14 + t0], in0=cols[:, 4 + t0:6 + t0],
                in1=cols[:, 4 + t0:6 + t0], op=ALU.mult),
                reads=[b_cols], writes=[b_cols])
            p.add("vector", lambda v, t0=t0: v.scalar_tensor_tensor(
                out=cols[:, 12 + t0:14 + t0], in0=cols[:, 8 + t0:10 + t0],
                scalar=1.0 / H, in1=cols[:, 12 + t0:14 + t0],
                op0=ALU.mult, op1=ALU.subtract),
                reads=[b_cols], writes=[b_cols])
            p.add("vector", lambda v, t0=t0: v.tensor_scalar(
                out=cols[:, 12 + t0:14 + t0], in0=cols[:, 12 + t0:14 + t0],
                scalar1=1e-5, scalar2=None, op0=ALU.add),
                reads=[b_cols], writes=[b_cols])
            p.add("scalar", lambda a, t0=t0: a.activation(
                out=cols[:, 12 + t0:14 + t0], in_=cols[:, 12 + t0:14 + t0],
                func=AF.Sqrt), reads=[b_cols], writes=[b_cols])
            p.add("vector", lambda v, t0=t0: v.reciprocal(
                out=cols[:, 12 + t0:14 + t0], in_=cols[:, 12 + t0:14 + t0]),
                reads=[b_cols], writes=[b_cols])

        def ln_astore(pg, ti):
            tt = pg * 2 + ti
            p.add("vector", lambda v, tt=tt, ti=ti: v.tensor_scalar(
                out=A[:, tt * H:(tt + 1) * H], in0=xg[ti],
                scalar1=cols[:, 4 + tt:5 + tt],
                scalar2=cols[:, 12 + tt:13 + tt],
                op0=ALU.subtract, op1=ALU.mult),
                reads=[b_xg[ti], b_cols], writes=[b_A])
            p.add("scalar", lambda a, tt=tt, ti=ti: a.activation(
                out=xqtm[ti], in_=A[:, tt * H:(tt + 1) * H], func=AF.Square,
                accum_out=cols[:, 16 + tt:17 + tt]),
                reads=[b_A], writes=[b_xqtm[ti], b_cols])

        def ln_rms_tail():
            p.add("vector", lambda v: v.tensor_scalar(
                out=cols[:, 20:24], in0=cols[:, 16:20],
                scalar1=1.0 / H, scalar2=1e-8, op0=ALU.mult, op1=ALU.add),
                reads=[b_cols], writes=[b_cols])
            p.add("scalar", lambda a: a.activation(
                out=cols[:, 20:24], in_=cols[:, 20:24], func=AF.Sqrt),
                reads=[b_cols], writes=[b_cols])
            p.add("vector", lambda v: v.reciprocal(
                out=cols[:, 20:24], in_=cols[:, 20:24]),
                reads=[b_cols], writes=[b_cols])

        def xnT_tt(g, tt):
            i0 = tt % 2
            p.add("vector", lambda v, i0=i0, tt=tt: v.tensor_copy(
                out=xqtm[i0], in_=A[:, tt * H:(tt + 1) * H]),
                reads=[b_A], writes=[b_xqtm[i0]])
            gl = (g - NG) * G + tt * 128
            for half in range(2):
                for kc4 in range(2):
                    for j in range(4):
                        kc = half * 8 + kc4 * 4 + j
                        p.add("tensor", lambda t, i0=i0, kc=kc, j=j: t.transpose(
                            psb(6)[:, j * 128:(j + 1) * 128],
                            xqtm[i0][:, kc * 128:(kc + 1) * 128], identb),
                            reads=[b_xqtm[i0], b_identb], writes=[b_ps[6]])
                    p.add("scalar", lambda a, kc4=kc4: a.copy(
                        out=xnstage[:, kc4 * 512:(kc4 + 1) * 512],
                        in_=psb(6)[:, 0:512]),
                        reads=[b_ps[6]], writes=[b_xnstage])
                p.add("gpsimd", lambda gp, gl=gl, half=half: gp.dma_start(
                    out=xnT_s[half * 1024:(half + 1) * 1024, gl:gl + 128]
                        .rearrange("(kc p) t -> p kc t", p=128),
                    in_=xnstage.rearrange("p (kc t) -> p kc t", t=128)),
                    reads=[b_xnstage], writes=[dbuf["xnT"]], dma=True)

        def s1_ln_gen(g):
            ln_pg_loads(xm, g, 0)
            yield
            ln_pg_chain(0)
            yield
            ln_astore(0, 0)
            yield
            ln_astore(0, 1)
            xnT_tt(g + NG, 0)
            yield
            ln_pg_loads(xm, g, 1)
            yield
            ln_pg_chain(1)
            xnT_tt(g + NG, 1)
            yield
            ln_astore(1, 0)
            ln_astore(1, 1)
            yield
            ln_rms_tail()
            xnT_tt(g + NG, 2)
            yield
            xnT_tt(g + NG, 3)

        def bcast_scale(row_ap, sidx):
            p.add("tensor", lambda t, row_ap=row_ap: t.matmul(
                ps[6], onesf, row_ap, start=True, stop=True),
                reads=[b_rows, b_onesf], writes=[b_ps[6]])
            p.add("scalar", lambda a, sidx=sidx: a.copy(
                out=sc3[sidx][:, 0:512], in_=ps[6]),
                reads=[b_ps[6]], writes=[b_sc3[sidx]])
            p.add("scalar", lambda a, sidx=sidx: a.copy(
                out=sc3[sidx][:, 512:1024], in_=ps[6]),
                reads=[b_ps[6]], writes=[b_sc3[sidx]])

        def quant(br, dstT, b_dstT, row_only=False):
            nb, b_nb = normbc[0], b_normbc[0]
            for tt in range(TTN):
                p.add("vector", lambda v, tt=tt, nb=nb: v.tensor_tensor(
                    out=wb, in0=A[:, tt * H:(tt + 1) * H], in1=nb, op=ALU.mult),
                    reads=[b_A, b_nb], writes=[b_wb])
                p.add("vector", lambda v, tt=tt: v.reduce_max(
                    out=cols[:, 24 + tt:25 + tt], in_=wb, axis=AX.X,
                    apply_absolute_value=True), reads=[b_wb], writes=[b_cols])
            p.add("vector", lambda v: v.tensor_scalar(
                out=cols[:, 28:32], in0=cols[:, 24:28],
                scalar1=1e-5, scalar2=1.0 / 127.0, op0=ALU.max, op1=ALU.mult),
                reads=[b_cols], writes=[b_cols])
            p.add("vector", lambda v: v.reciprocal(
                out=cols[:, 32:36], in_=cols[:, 28:32]),
                reads=[b_cols], writes=[b_cols])
            p.add("vector", lambda v: v.tensor_tensor(
                out=cols[:, 36:40], in0=cols[:, 28:32], in1=cols[:, 20:24],
                op=ALU.mult), reads=[b_cols], writes=[b_cols])
            p.add("vector", lambda v, br=br: v.tensor_scalar(
                out=cols[:, 36:40], in0=cols[:, 36:40],
                scalar1=swocol[br], scalar2=127.0, op0=ALU.mult, op1=ALU.mult),
                reads=[b_cols, b_swocol[br]], writes=[b_cols])
            for tt in range(TTN):
                p.add("vector", lambda v, tt=tt, nb=nb: v.scalar_tensor_tensor(
                    out=xq8, in0=A[:, tt * H:(tt + 1) * H],
                    scalar=cols[:, 32 + tt:33 + tt], in1=nb,
                    op0=ALU.mult, op1=ALU.mult),
                    reads=[b_A, b_cols, b_nb], writes=[b_xq8])
                i0 = tt % 2
                p.add("scalar", lambda a, i0=i0: a.copy(
                    out=xqtm[i0], in_=xq8), reads=[b_xq8], writes=[b_xqtm[i0]])
                for kc4 in range(4):
                    for j in range(4):
                        kc = kc4 * 4 + j
                        p.add("tensor", lambda t, i0=i0, kc=kc, j=j: t.transpose(
                            psb(4)[:, j * 128:(j + 1) * 128],
                            xqtm[i0][:, kc * 128:(kc + 1) * 128], identb),
                            reads=[b_xqtm[i0], b_identb], writes=[b_ps[4]])
                    p.add("scalar", lambda a, kc4=kc4, tt=tt, dstT=dstT: a.copy(
                        out=dstT.rearrange("p (kc t) -> p kc t", t=G)
                            [:, kc4 * 4:(kc4 + 1) * 4, tt * 128:(tt + 1) * 128],
                        in_=psb(4)[:, 0:512].rearrange("p (kc t) -> p kc t", t=128)),
                        reads=[b_ps[4]], writes=[b_dstT])
            for tt in range(TTN):
                p.add("tensor", lambda t, tt=tt: t.transpose(
                    ps[5][0:1, tt * 128:(tt + 1) * 128],
                    cols[:, 36 + tt:37 + tt], identf),
                    reads=[b_cols, b_identf], writes=[b_ps[5]])
            p.add("scalar", lambda a: a.copy(
                out=rows[0:1, 0:512], in_=ps[5][0:1, 0:512]),
                reads=[b_ps[5]], writes=[b_rows])
            if row_only:
                return
            sidx = {"f": 0, "i": 1, "g": 2}[br]
            bcast_scale(rows[0:1, 0:512], sidx)

        def wq_stream(m, ec, slot):
            p.add("sync", lambda s, o=wqblk[slot], i=wq_s[m][ec * 128:(ec + 1) * 128, :]:
                  s.dma_start(out=o, in_=i),
                  reads=[dbuf["wq_" + m]], writes=[b_wqblk[slot]], dma=True)

        def blin_mm(srcT, b_srcT, wslot, bank, chunk):
            for kc in range(KC):
                p.add("tensor", lambda t, kc=kc: t.matmul(
                    ps[bank], wqblk[wslot][:, kc * 128:(kc + 1) * 128],
                    srcT[:, kc * G:kc * G + G],
                    start=(kc == 0), stop=(kc == KC - 1)),
                    reads=[b_wqblk[wslot], b_srcT], writes=[b_ps[bank]])

        sout = [sfull[:, 0:1024], sfull[:, 1024:2048]]
        cout = [sfull[:, 2048:3072], sfull[:, 3072:4096]]
        b_sout = [Buf(), Buf()]
        b_cout = [Buf(), Buf()]
        onestile, b_ones1 = sb("onestile", [128, 512], BF16, sc)
        smsb, b_smsb = sb("smsb", [128, 8], F32, sc)
        agsb, b_agsb = sb("agsb", [128, 128], F32, sc)
        s_in, b_s_in = zip(*[sb(f"s_in{i}", [128, KC], F32, sc) for i in range(2)])
        ccarry, b_ccarry = sb("ccarry", [128, KC], F32, sc)

        p.add("vector", lambda v: v.memset(onestile, 1.0), writes=[b_ones1])
        p.add("vector", lambda v: v.memset(ccarry, 1.0), writes=[b_ccarry])
        p.add("sync", lambda s: s.dma_start(out=smsb, in_=selmask),
              writes=[b_smsb], dma=True)

        def s1_ln(g):
            for _ in s1_ln_gen(g):
                pass
            if g == 0:
                tap("xn", A, b_A, [128, KC * G], "float32")

        def s1_quants(g):
            p.add("sync", lambda s: s.dma_start(
                out=normbc[0], in_=norm_r["g"].partition_broadcast(128)),
                writes=[b_normbc[0]], dma=True)
            quant("g", xqT[0], b_xqT[0], row_only=True)
            p.add("gpsimd", lambda gp, g=g: gp.dma_start(
                out=sgrow_s[g:g + 1, :], in_=rows[0:1, 0:512]),
                reads=[b_rows], writes=[dbuf["sgrow"]], dma=True)
            p.add("gpsimd", lambda gp, g=g: gp.dma_start(
                out=xqg_s[g * 128:(g + 1) * 128, :], in_=xqT[0]),
                reads=[b_xqT[0]], writes=[dbuf["xqg"]], dma=True)
            p.add("sync", lambda s: s.dma_start(
                out=normbc[0], in_=norm_r["f"].partition_broadcast(128)),
                writes=[b_normbc[0]], dma=True)
            quant("f", xqT[0], b_xqT[0])
            if g == 0:
                tap("xqTf", xqT[0], b_xqT[0], [128, KC * G], "bfloat16")
            p.add("sync", lambda s: s.dma_start(
                out=normbc[0], in_=norm_r["i"].partition_broadcast(128)),
                writes=[b_normbc[0]], dma=True)
            quant("i", xqT[1], b_xqT[1])

        def s1_loop(g, next_g):
            gen = s1_ln_gen(next_g) if next_g is not None else None
            for ep in range(KC // 2):
                pset = ep % 2
                ca, cb = 2 * ep, 2 * ep + 1
                sf = 2 * pset
                si = 2 - 2 * pset
                wq_stream("f", ca, sf)
                wq_stream("f", cb, sf + 1)
                wq_stream("i", ca, si)
                wq_stream("i", cb, si + 1)
                blin_mm(xqT[0], b_xqT[0], sf, 2 * pset + 0, ca)
                blin_mm(xqT[0], b_xqT[0], sf + 1, 2 * pset + 1, cb)
                blin_mm(xqT[1], b_xqT[1], si, 2 * pset + 4, ca)
                blin_mm(xqT[1], b_xqT[1], si + 1, 2 * pset + 5, cb)
                p.add("vector", lambda v, pset=pset: v.tensor_tensor(
                    out=zf, in0=ps2[pset], in1=sc3[0], op=ALU.mult),
                    reads=[b_ps[2 * pset], b_ps[2 * pset + 1], b_sc3[0]],
                    writes=[b_zf])
                p.add("vector", lambda v, pset=pset: v.tensor_tensor(
                    out=zi, in0=ps2[2 + pset], in1=sc3[1], op=ALU.mult),
                    reads=[b_ps[4 + 2 * pset], b_ps[5 + 2 * pset], b_sc3[1]],
                    writes=[b_zi])
                p.add("scalar", lambda a: a.activation(
                    out=thc, in_=zf, func=AF.Tanh, scale=0.5),
                    reads=[b_zf], writes=[b_thc])
                p.add("gpsimd", lambda gp: gp.tensor_scalar(
                    out=fc, in0=thc, scalar1=0.5, scalar2=0.5,
                    op0=ALU.mult, op1=ALU.add), reads=[b_thc], writes=[b_fc])
                p.add("gpsimd", lambda gp: gp.tensor_scalar(
                    out=fm1, in0=fc, scalar1=-1.0, scalar2=1.0,
                    op0=ALU.mult, op1=ALU.add), reads=[b_fc], writes=[b_fm1])
                p.add("scalar", lambda a: a.activation(
                    out=thc, in_=zi, func=AF.Tanh, scale=0.5),
                    reads=[b_zi], writes=[b_thc])
                p.add("gpsimd", lambda gp: gp.tensor_scalar(
                    out=fic, in0=thc, scalar1=0.5, scalar2=0.5,
                    op0=ALU.mult, op1=ALU.add), reads=[b_thc], writes=[b_fic])
                p.add("gpsimd", lambda gp: gp.tensor_tensor(
                    out=thc, in0=fic, in1=fm1, op=ALU.mult),
                    reads=[b_fic, b_fm1], writes=[b_thc])
                if gen is not None:
                    next(gen, None)
                p.add("vector", lambda v: v.tensor_tensor(
                    out=zf, in0=zi, in1=thc, op=ALU.mult),
                    reads=[b_zi, b_thc], writes=[b_zf])
                so, bso = sout[pset], b_sout[pset]
                co, bco = cout[pset], b_cout[pset]
                for sub, cx in ((0, ca), (1, cb)):
                    p.add("vector", lambda v, sub=sub, cx=cx, so=so: v.tensor_tensor_scan(
                        out=so[:, sub * G:(sub + 1) * G],
                        data0=fc[:, sub * G:(sub + 1) * G],
                        data1=zf[:, sub * G:(sub + 1) * G],
                        initial=carry[:, cx:cx + 1], op0=ALU.mult, op1=ALU.add),
                        reads=[b_fc, b_zf, b_carry], writes=[bso])
                    p.add("vector", lambda v, sub=sub, cx=cx, co=co: v.tensor_tensor_scan(
                        out=co[:, sub * G:(sub + 1) * G],
                        data0=fc[:, sub * G:(sub + 1) * G],
                        data1=onestile,
                        initial=ccarry[:, cx:cx + 1], op0=ALU.mult, op1=ALU.mult),
                        reads=[b_fc, b_ones1, b_ccarry], writes=[bco])
                p.add("vector", lambda v, ca=ca, so=so: v.tensor_copy(
                    out=carry[:, ca:ca + 2],
                    in_=so.rearrange("p (c t) -> p c t", t=G)
                        [:, 0:2, G - 1:G].rearrange("p c t -> p (c t)")),
                    reads=[bso], writes=[b_carry])
                p.add("vector", lambda v, ca=ca, co=co: v.tensor_copy(
                    out=ccarry[:, ca:ca + 2],
                    in_=co.rearrange("p (c t) -> p c t", t=G)
                        [:, 0:2, G - 1:G].rearrange("p c t -> p (c t)")),
                    reads=[bco], writes=[b_ccarry])
                r0 = g * 128
                p.add("gpsimd", lambda gp, r0=r0, ep=ep, so=so: gp.dma_start(
                    out=sloc_s[r0:r0 + 128, ep * 1024:(ep + 1) * 1024], in_=so),
                    reads=[bso], writes=[dbuf["sloc"]], dma=True)
                p.add("gpsimd", lambda gp, r0=r0, ep=ep, co=co: gp.dma_start(
                    out=cprod_s[r0:r0 + 128, ep * 1024:(ep + 1) * 1024], in_=co),
                    reads=[bco], writes=[dbuf["cprod"]], dma=True)
            if gen is not None:
                for _ in gen:
                    pass

        def s1_group_driver():
            s1_ln(0)
            s1_quants(0)
            for g in range(4):
                s1_loop(g, g + 1 if g < 3 else None)
                if g < 3:
                    s1_quants(g + 1)

        def exchange():
            p.add("gpsimd", lambda gp: gp.dma_start(out=carry_s, in_=carry),
                  reads=[b_carry], writes=[dbuf["carry"]], dma=True)
            p.add("gpsimd", lambda gp: gp.collective_compute(
                "AllGather", ALU.bypass, replica_groups=[list(range(8))],
                ins=[carry_s.opt()], outs=[ag_s.opt()]),
                reads=[dbuf["carry"]], writes=[dbuf["ag"]], cc=True)
            p.add("sync", lambda s: s.dma_start(
                out=agsb.rearrange("p (j c) -> p j c", c=KC),
                in_=ag_s.rearrange("(j p) c -> p j c", p=128)),
                reads=[dbuf["ag"]], writes=[b_agsb], dma=True)
            p.add("vector", lambda v: v.tensor_scalar(
                out=s_in[0], in0=agsb[:, 0:KC], scalar1=smsb[:, 0:1],
                scalar2=None, op0=ALU.mult),
                reads=[b_agsb, b_smsb], writes=[b_s_in[0]])
            for j in range(1, 8):
                p.add("vector", lambda v, j=j: v.scalar_tensor_tensor(
                    out=s_in[j % 2], in0=agsb[:, j * KC:(j + 1) * KC],
                    scalar=smsb[:, j:j + 1], in1=s_in[(j + 1) % 2],
                    op0=ALU.mult, op1=ALU.add),
                    reads=[b_agsb, b_smsb, b_s_in[(j + 1) % 2]],
                    writes=[b_s_in[j % 2]])

        sfin, b_sfin = s_in[7 % 2], b_s_in[7 % 2]

        def d_load(g):
            r0 = g * 128
            p.add("scalar", lambda a, r0=r0: a.dma_start(
                out=sfull, in_=sloc_s[r0:r0 + 128, :]),
                reads=[dbuf["sloc"]],
                writes=[b_s, b_sout[0], b_sout[1], b_cout[0], b_cout[1]], dma=True)
            p.add("scalar", lambda a, r0=r0: a.dma_start(
                out=A, in_=cprod_s[r0:r0 + 128, :]),
                reads=[dbuf["cprod"]], writes=[b_A], dma=True)

        def d_fix_head(g):
            gq, b_gq = xqT[g % 2], b_xqT[g % 2]
            r0 = g * 128
            p.add("scalar", lambda a, r0=r0, gq=gq: a.dma_start(
                out=gq, in_=xqg_s[r0:r0 + 128, :]),
                reads=[dbuf["xqg"]], writes=[b_gq], dma=True)
            p.add("sync", lambda s, g=g: s.dma_start(
                out=rows[0:1, 0:512], in_=sgrow_s[g:g + 1, :]),
                reads=[dbuf["sgrow"]], writes=[b_rows], dma=True)
            bcast_scale(rows[0:1, 0:512], 2)

        def d_fix_slices(g):
            for ep in range(KC // 2):
                for sub in range(2):
                    cx = 2 * ep + sub
                    p.add("vector", lambda v, ep=ep, sub=sub, cx=cx:
                          v.scalar_tensor_tensor(
                              out=zf[:, sub * G:(sub + 1) * G],
                              in0=A[:, cx * G:(cx + 1) * G],
                              scalar=sfin[:, cx:cx + 1],
                              in1=sfull[:, cx * G:(cx + 1) * G],
                              op0=ALU.mult, op1=ALU.add),
                          reads=[b_A, b_sfin, b_s], writes=[b_zf])
                wins = [b_sout[0], b_sout[1], b_cout[0], b_cout[1]]
                ww = [b_s] + ([wins[ep]] if ep < 4 else [])
                p.add("scalar", lambda a, ep=ep: a.copy(
                    out=sfull[:, ep * 1024:(ep + 1) * 1024], in_=zf),
                    reads=[b_zf], writes=ww)
                for sub in range(2):
                    ec = 2 * ep + sub
                    p.add("scalar", lambda a, ec=ec: a.activation(
                        out=s2c[0], in_=sfull[:, ec * G:(ec + 1) * G],
                        func=AF.Square), reads=[b_s], writes=[b_s2c[0]])
                    p.add("tensor", lambda t, ec=ec: t.matmul(
                        ps[3][0:1, :], onesb, s2c[0],
                        start=(ec == 0), stop=(ec == KC - 1)),
                        reads=[b_s2c[0], b_onesb], writes=[b_ps[3]])
                yield
            if g == 0:
                tap("s_main", sfull, b_s, [128, KC * G], "float32")

        def d_fix_tail(g):
            p.add("vector", lambda v: v.tensor_scalar(
                out=rows[0:1, 512:1024], in0=ps[3][0:1, :], scalar1=1.0 / H,
                scalar2=1e-5, op0=ALU.mult, op1=ALU.add),
                reads=[b_ps[3]], writes=[b_rows])
            p.add("scalar", lambda a: a.activation(
                out=rows[0:1, 512:1024], in_=rows[0:1, 512:1024], func=AF.Sqrt),
                reads=[b_rows], writes=[b_rows])
            p.add("vector", lambda v: v.reciprocal(
                out=rows[0:1, 512:1024], in_=rows[0:1, 512:1024]),
                reads=[b_rows], writes=[b_rows])
            bcast_scale(rows[0:1, 512:1024], 0)
        def d_part1(g):
            rstdbc, b_rstdbc = sc3[0], b_sc3[0]
            zg2 = [zg, wb[:, 0:1024]]
            b_zg2 = [b_zg, b_wb]
            sh2 = [shatc, wb[:, 1024:2048]]
            b_sh2 = [b_shatc, b_wb]
            ty2 = [tyc, xqtm[0].bitcast(F32)[:, 0:1024]]
            b_ty2 = [b_tyc, b_xqtm[0]]

            def front(i):
                pset = i % 2
                pi = i % 2
                ca, cb = 2 * i, 2 * i + 1
                sg_ = 2 * pset
                wq_stream("g", ca, sg_)
                wq_stream("g", cb, sg_ + 1)
                blin_mm(xqT[g % 2], b_xqT[g % 2], sg_, 2 * pset + 0, ca)
                blin_mm(xqT[g % 2], b_xqT[g % 2], sg_ + 1, 2 * pset + 1, cb)
                p.add("vector", lambda v, pset=pset, pi=pi: v.tensor_tensor(
                    out=zg2[pi], in0=ps2[pset], in1=sc3[2], op=ALU.mult),
                    reads=[b_ps[2 * pset], b_ps[2 * pset + 1], b_sc3[2]],
                    writes=[b_zg2[pi]])
                for sub, cx in ((0, ca), (1, cb)):
                    p.add("vector", lambda v, sub=sub, cx=cx, pi=pi:
                          v.scalar_tensor_tensor(
                              out=sh2[pi][:, sub * G:(sub + 1) * G],
                              in0=sfull[:, cx * G:(cx + 1) * G],
                              scalar=rmsw_fm[:, cx:cx + 1], in1=rstdbc[:, 0:512],
                              op0=ALU.mult, op1=ALU.mult),
                          reads=[b_s, b_rmsw, b_rstdbc], writes=[b_sh2[pi]])
                p.add("scalar", lambda a, pi=pi: a.activation(
                    out=thg, in_=zg2[pi], func=AF.Tanh, scale=0.5),
                    reads=[b_zg2[pi]], writes=[b_thg])
                p.add("gpsimd", lambda gp: gp.tensor_scalar(
                    out=fgc, in0=thg, scalar1=0.5, scalar2=0.5,
                    op0=ALU.mult, op1=ALU.add), reads=[b_thg], writes=[b_fgc])
                p.add("gpsimd", lambda gp, pi=pi: gp.tensor_tensor(
                    out=ty2[pi], in0=fgc, in1=sh2[pi], op=ALU.mult),
                    reads=[b_fgc, b_sh2[pi]], writes=[b_ty2[pi]])

            def tail(i):
                pi = i % 2
                ca, cb = 2 * i, 2 * i + 1
                p.add("vector", lambda v, ca=ca, pi=pi: v.tensor_tensor(
                    out=A[:, ca * G:(ca + 2) * G], in0=zg2[pi], in1=ty2[pi],
                    op=ALU.mult), reads=[b_zg2[pi], b_ty2[pi]], writes=[b_A])
                for sub, cx in ((0, ca), (1, cb)):
                    p.add("vector", lambda v, sub=sub, cx=cx: v.tensor_scalar(
                        out=wyc[:, sub * G:(sub + 1) * G],
                        in0=A[:, cx * G:(cx + 1) * G],
                        scalar1=normo_fm[:, cx:cx + 1], scalar2=None,
                        op0=ALU.mult),
                        reads=[b_A, b_normo], writes=[b_wyc])
                if i == 0:
                    p.add("vector", lambda v: v.tensor_tensor(
                        out=mrun[0], in0=wyc[:, 0:512], in1=wyc[:, 512:1024],
                        op=ALU.max), reads=[b_wyc], writes=[b_mrun[0]])
                    p.add("vector", lambda v: v.tensor_tensor(
                        out=mrun[1], in0=wyc[:, 0:512], in1=wyc[:, 512:1024],
                        op=ALU.min), reads=[b_wyc], writes=[b_mrun[1]])
                else:
                    for sub in range(2):
                        p.add("vector", lambda v, sub=sub: v.tensor_tensor(
                            out=mrun[0], in0=mrun[0],
                            in1=wyc[:, sub * G:(sub + 1) * G], op=ALU.max),
                            reads=[b_wyc, b_mrun[0]], writes=[b_mrun[0]])
                        p.add("vector", lambda v, sub=sub: v.tensor_tensor(
                            out=mrun[1], in0=mrun[1],
                            in1=wyc[:, sub * G:(sub + 1) * G], op=ALU.min),
                            reads=[b_wyc, b_mrun[1]], writes=[b_mrun[1]])

            def acc(i):
                for cx in (2 * i, 2 * i + 1):
                    p.add("scalar", lambda a, cx=cx: a.activation(
                        out=s2c[0], in_=A[:, cx * G:(cx + 1) * G],
                        func=AF.Square), reads=[b_A], writes=[b_s2c[0]])
                    p.add("tensor", lambda t, cx=cx: t.matmul(
                        ps[5][0:1, :], onesb, s2c[0],
                        start=(cx == 0), stop=(cx == KC - 1)),
                        reads=[b_s2c[0], b_onesb], writes=[b_ps[5]])

            front(0)
            for i in range(KC // 2):
                if i + 1 < KC // 2:
                    front(i + 1)
                tail(i)
                if i - 1 >= 0:
                    acc(i - 1)
            acc(KC // 2 - 1)
            if g == 0:
                tap("y", A, b_A, [128, KC * G], "float32")
            p.add("vector", lambda v: v.tensor_scalar(
                out=rows[0:1, 512:1024], in0=ps[5][0:1, :], scalar1=1.0 / H,
                scalar2=1e-8, op0=ALU.mult, op1=ALU.add),
                reads=[b_ps[5]], writes=[b_rows])
            p.add("scalar", lambda a: a.activation(
                out=rows[0:1, 512:1024], in_=rows[0:1, 512:1024], func=AF.Sqrt),
                reads=[b_rows], writes=[b_rows])
            p.add("vector", lambda v: v.reciprocal(
                out=rows[0:1, 512:1024], in_=rows[0:1, 512:1024]),
                reads=[b_rows], writes=[b_rows])
            p.add("vector", lambda v: v.tensor_scalar(
                out=mrun[1], in0=mrun[1], scalar1=-1.0, scalar2=None,
                op0=ALU.mult), reads=[b_mrun[1]], writes=[b_mrun[1]])
            p.add("vector", lambda v: v.tensor_tensor(
                out=mrun[0], in0=mrun[0], in1=mrun[1], op=ALU.max),
                reads=[b_mrun[0], b_mrun[1]], writes=[b_mrun[0]])
            for j in range(4):
                p.add("tensor", lambda t, j=j: t.transpose(
                    ps[7][:, j * 128:(j + 1) * 128],
                    mrun[0][:, j * 128:(j + 1) * 128], identf),
                    reads=[b_mrun[0], b_identf], writes=[b_ps[7]])
            for j in range(4):
                p.add("vector", lambda v, j=j: v.reduce_max(
                    out=cols[:, 40 + j:41 + j], in_=ps[7][:, j * 128:(j + 1) * 128],
                    axis=AX.X), reads=[b_ps[7]], writes=[b_cols])
            p.add("vector", lambda v: v.tensor_scalar(
                out=cols[:, 44:48], in0=cols[:, 40:44],
                scalar1=1e-5, scalar2=1.0 / 127.0, op0=ALU.max, op1=ALU.mult),
                reads=[b_cols], writes=[b_cols])
            p.add("vector", lambda v: v.reciprocal(
                out=cols[:, 48:52], in_=cols[:, 44:48]),
                reads=[b_cols], writes=[b_cols])
            p.add("vector", lambda v: v.tensor_scalar(
                out=cols[:, 52:56], in0=cols[:, 44:48],
                scalar1=swocol["o"], scalar2=127.0, op0=ALU.mult, op1=ALU.mult),
                reads=[b_cols, b_swocol["o"]], writes=[b_cols])
            for j in range(4):
                p.add("tensor", lambda t, j=j: t.transpose(
                    ps[5][0:1, j * 128:(j + 1) * 128], cols[:, 48 + j:49 + j],
                    identf), reads=[b_cols, b_identf], writes=[b_ps[5]])
            p.add("scalar", lambda a: a.copy(
                out=rows[0:1, 0:512], in_=ps[5][0:1, 0:512]),
                reads=[b_ps[5]], writes=[b_rows])
            bcast_scale(rows[0:1, 0:512], 2)
            for j in range(4):
                p.add("tensor", lambda t, j=j: t.transpose(
                    ps[5][0:1, j * 128:(j + 1) * 128], cols[:, 52 + j:53 + j],
                    identf), reads=[b_cols, b_identf], writes=[b_ps[5]])
            p.add("scalar", lambda a: a.copy(
                out=rows[0:1, 0:512], in_=ps[5][0:1, 0:512]),
                reads=[b_ps[5]], writes=[b_rows])
            p.add("vector", lambda v: v.tensor_tensor(
                out=rows[0:1, 0:512], in0=rows[0:1, 0:512],
                in1=rows[0:1, 512:1024], op=ALU.mult),
                reads=[b_rows], writes=[b_rows])
            bcast_scale(rows[0:1, 0:512], 1)
            yq82 = [yq8c, s2c[0].bitcast(I8)[:, 0:512]]
            b_yq82 = [b_yq8c, b_s2c[0]]
            for ec in range(KC):
                e2 = ec % 2
                p.add("scalar", lambda a, e2=e2, ec=ec: a.activation(
                    out=wyc[:, e2 * G:(e2 + 1) * G],
                    in_=A[:, ec * G:(ec + 1) * G], func=AF.Copy,
                    scale=normo_fm[:, ec:ec + 1]),
                    reads=[b_A, b_normo], writes=[b_wyc])
                p.add("vector", lambda v, e2=e2: v.tensor_tensor(
                    out=yq82[e2], in0=wyc[:, e2 * G:(e2 + 1) * G],
                    in1=sc3[2][:, 0:512], op=ALU.mult),
                    reads=[b_wyc, b_sc3[2]], writes=[b_yq82[e2]])
                if e2 == 0:
                    p.add("gpsimd", lambda gp, ec=ec, g=g, e2=e2: gp.tensor_copy(
                        out=xqT[g % 2][:, ec * G:(ec + 1) * G], in_=yq82[e2]),
                        reads=[b_yq82[e2]], writes=[b_xqT[g % 2]])
                else:
                    p.add("scalar", lambda a, ec=ec, g=g, e2=e2: a.activation(
                        out=xqT[g % 2][:, ec * G:(ec + 1) * G], in_=yq82[e2],
                        func=AF.Copy),
                        reads=[b_yq82[e2]], writes=[b_xqT[g % 2]])
            if g == 0:
                tap("yqT", yqT, b_yqT, [128, KC * G], "bfloat16")
        def d_part2(g, fixgen):
            yqg, b_yqg = xqT[g % 2], b_xqT[g % 2]
            for ep in range(KC // 2):
                pset = ep % 2
                ca, cb = 2 * ep, 2 * ep + 1
                so_ = 2 - 2 * pset
                wq_stream("o", ca, so_)
                wq_stream("o", cb, so_ + 1)
                blin_mm(yqg, b_yqg, so_, 4 + 2 * pset, ca)
                blin_mm(yqg, b_yqg, so_ + 1, 5 + 2 * pset, cb)
                if fixgen is not None:
                    next(fixgen, None)
                for sub, cx in ((0, ca), (1, cb)):
                    bank = 4 + 2 * pset + sub
                    p.add("vector", lambda v, bank=bank: v.tensor_tensor(
                        out=safm, in0=ps[bank], in1=sc3[1][:, 0:512],
                        op=ALU.mult),
                        reads=[b_ps[bank], b_sc3[1]], writes=[b_safm])
                    for j in range(4):
                        p.add("tensor", lambda t, j=j: t.transpose(
                            ps[0][:, j * 128:j * 128 + 128],
                            safm[:, j * 128:(j + 1) * 128], identf),
                            reads=[b_safm, b_identf], writes=[b_ps[0]])
                    p.add("scalar", lambda a, sub=sub: a.copy(
                        out=mrun[sub], in_=ps[0][:, 0:512]),
                        reads=[b_ps[0]], writes=[b_mrun[sub]])
                    p.add("gpsimd", lambda gp, sub=sub, cx=cx, g=g: gp.dma_start(
                        out=sattn_s[g * G:(g + 1) * G, cx * 128:(cx + 1) * 128]
                            .rearrange("(j q) c -> q j c", q=128),
                        in_=mrun[sub].rearrange("q (j c) -> q j c", c=128)),
                        reads=[b_mrun[sub]], writes=[dbuf["sattn"]], dma=True)
            if fixgen is not None:
                for _ in fixgen:
                    pass
            if g == 0:
                tap("sattn0", safm, b_safm, [128, 512], "float32")

        s1_group_driver()
        d_load(0)
        d_fix_head(0)
        exchange()
        for _ in d_fix_slices(0):
            pass
        d_fix_tail(0)
        for g in range(4):
            d_part1(g)
            if g + 1 < 4:
                d_load(g + 1)
                d_fix_head(g + 1)
                fixgen = d_fix_slices(g + 1)
            else:
                fixgen = None
            d_part2(g, fixgen)
            if g + 1 < 4:
                d_fix_tail(g + 1)
        p.barrier()
        p.flush(blk)

    if do_f1:
        with contextlib.ExitStack() as sc:
            xnT_sb, b_xnT_sb = sb("xnT_sb", [128, KC * TOK], BF16, sc)
            wg8t, b_wg8t = zip(*[sb(f"wg8t{i}", [128, H], I8, sc) for i in range(4)])
            wu8t, b_wu8t = zip(*[sb(f"wu8t{i}", [128, H], I8, sc) for i in range(4)])
            wd8t, b_wd8t = zip(*[sb(f"wd8t{i}", [128, H], I8, sc) for i in range(2)])
            wgblk, b_wgblk = zip(*[sb(f"wgblk{i}", [128, H], BF16, sc) for i in range(3)])
            wublk, b_wublk = zip(*[sb(f"wublk{i}", [128, H], BF16, sc) for i in range(3)])
            wdbt, b_wdbt = zip(*[sb(f"wdbt{i}", [128, H], BF16, sc) for i in range(2)])
            gbf, b_gbf = zip(*[sb(f"gbf{i}", [128, 512], F32, sc) for i in range(4)])
            ubf, b_ubf = zip(*[sb(f"ubf{i}", [128, 512], F32, sc) for i in range(4)])
            fbf, b_fbf = zip(*[sb(f"fbf{i}", [128, 512], BF16, sc) for i in range(4)])
            cengs = ["vector", "scalar"]

            def conv(eng, out, in_, reads, writes):
                if eng == "scalar":
                    p.add(eng, lambda a, o=out, i=in_: a.activation(
                        out=o, in_=i, func=AF.Copy), reads=reads, writes=writes)
                else:
                    p.add(eng, lambda v, o=out, i=in_: v.tensor_copy(
                        out=o, in_=i), reads=reads, writes=writes)

            for kc in range(KC):
                p.add("sync", lambda s, kc=kc: s.dma_start(
                    out=xnT_sb[:, kc * TOK:(kc + 1) * TOK],
                    in_=xnT_s[kc * 128:(kc + 1) * 128, :]),
                    reads=[dbuf["xnT"]], writes=[b_xnT_sb], dma=True)
            for oc in range(OCN):
                o2 = oc % 3
                i4 = oc % 4
                sl = slice(oc * 128, (oc + 1) * 128)
                p.add("sync", lambda s, o=wg8t[i4], i=wg8[sl, :]:
                      s.dma_start(out=o, in_=i), writes=[b_wg8t[i4]], dma=True)
                p.add("sync", lambda s, o=wu8t[i4], i=wu8[sl, :]:
                      s.dma_start(out=o, in_=i), writes=[b_wu8t[i4]], dma=True)
                conv(cengs[oc % 2], wgblk[o2], wg8t[i4],
                     [b_wg8t[i4]], [b_wgblk[o2]])
                conv(cengs[(oc + 1) % 2], wublk[o2], wu8t[i4],
                     [b_wu8t[i4]], [b_wublk[o2]])
                i2 = oc % 2
                p.add("sync", lambda s, o=wd8t[i2], i=wd8[sl, :]:
                      s.dma_start(out=o, in_=i), writes=[b_wd8t[i2]], dma=True)
                conv(cengs[oc % 2], wdbt[i2], wd8t[i2],
                     [b_wd8t[i2]], [b_wdbt[i2]])
                p.add("gpsimd", lambda g, o=wdb_s[sl, :], i=wdbt[i2]:
                      g.dma_start(out=o, in_=i), reads=[b_wdbt[i2]],
                      writes=[dbuf["wdb"]], dma=True)
                for hc in range(KC):
                    for tt in range(4):
                        p.add("tensor", lambda t, hc=hc, tt=tt, o2=o2:
                              t.matmul(ps[tt],
                                       wgblk[o2][:, hc * 128:(hc + 1) * 128],
                                       xnT_sb[:, hc * TOK + tt * 512:
                                              hc * TOK + tt * 512 + 512],
                                       start=(hc == 0), stop=(hc == KC - 1)),
                              reads=[b_wgblk[o2], b_xnT_sb],
                              writes=[b_ps[tt]])
                for hc in range(KC):
                    for tt in range(4):
                        p.add("tensor", lambda t, hc=hc, tt=tt, o2=o2:
                              t.matmul(ps[4 + tt],
                                       wublk[o2][:, hc * 128:(hc + 1) * 128],
                                       xnT_sb[:, hc * TOK + tt * 512:
                                              hc * TOK + tt * 512 + 512],
                                       start=(hc == 0), stop=(hc == KC - 1)),
                              reads=[b_wublk[o2], b_xnT_sb],
                              writes=[b_ps[4 + tt]])
                for tt in range(4):
                    p.add("scalar", lambda a, tt=tt, oc=oc:
                          a.activation(out=gbf[tt], in_=ps[tt], func=AF.Silu,
                                       scale=sgate_fm[:, oc:oc + 1]),
                          reads=[b_ps[tt], b_sgate], writes=[b_gbf[tt]])
                    p.add("vector", lambda v, tt=tt, oc=oc:
                          v.tensor_scalar(out=ubf[tt], in0=ps[4 + tt],
                                          scalar1=sup_fm[:, oc:oc + 1],
                                          scalar2=None, op0=ALU.mult),
                          reads=[b_ps[4 + tt], b_sup], writes=[b_ubf[tt]])
                    p.add("vector", lambda v, tt=tt: v.tensor_tensor(
                        out=fbf[tt], in0=gbf[tt], in1=ubf[tt], op=ALU.mult),
                        reads=[b_gbf[tt], b_ubf[tt]], writes=[b_fbf[tt]])
                    p.add("gpsimd", lambda gp, oc=oc, tt=tt: gp.dma_start(
                        out=fuse_s[oc * 128:(oc + 1) * 128,
                                   tt * 512:(tt + 1) * 512],
                        in_=fbf[tt]),
                        reads=[b_fbf[tt]], writes=[dbuf["fuse"]], dma=True)
            p.barrier()
            p.flush(blk)

    for m in "figo":
        if do_prep and f"wq_{m}" in _TAPS:
            t = nc.dram_tensor(f"tap_wq_{m}", [H, H], BF16,
                               kind="ExternalOutput").ap()
            taps.append(f"wq_{m}")
            p.add("gpsimd", lambda g, t=t, m=m: g.dma_start(out=t, in_=wq_s[m]),
                  reads=[dbuf["wq_" + m]], dma=True)
    if do_prep and "swc" in _TAPS:
        t = nc.dram_tensor("tap_swc", [128, 8], F32, kind="ExternalOutput").ap()
        taps.append("swc")
        p.add("gpsimd", lambda g, t=t: g.dma_start(out=t, in_=swc),
              reads=[b_swcol["o"], b_swocol["o"]], dma=True)
    p.flush(blk)
    if do_f1 and "fuse_full" in _TAPS:
        t = nc.dram_tensor("tap_fuse_full", [IFF, TOK], BF16,
                           kind="ExternalOutput").ap()
        taps.append("fuse_full")
        p.add("gpsimd", lambda g, t=t: g.dma_start(out=t, in_=fuse_s),
              reads=[dbuf["fuse"]], dma=True)
    if do_groups and "sattn_full" in _TAPS:
        t = nc.dram_tensor("tap_sattn_full", [TOK, H], F32,
                           kind="ExternalOutput").ap()
        taps.append("sattn_full")
        p.add("gpsimd", lambda g, t=t: g.dma_start(out=t, in_=sattn_s),
              reads=[dbuf["sattn"]], dma=True)
    if do_groups and "xnT_full" in _TAPS:
        t = nc.dram_tensor("tap_xnT_full", [H, TOK], BF16,
                           kind="ExternalOutput").ap()
        taps.append("xnT_full")
        p.add("gpsimd", lambda g, t=t: g.dma_start(out=t, in_=xnT_s),
              reads=[dbuf["xnT"]], dma=True)
    p.flush(blk)

    if do_f2:
        with contextlib.ExitStack() as sc:
            sdownbc, b_sdownbc = sb("sdownbc", [128, H], F32, sc)
            wdblk, b_wdblk = zip(*[sb(f"wdblk{i}", [128, H], BF16, sc) for i in range(4)])
            fch, b_fch = zip(*[sb(f"fch{i}", [128, 256], BF16, sc) for i in range(4)])
            satile, b_satile = zip(*[sb(f"satile{i}", [128, H], F32, sc) for i in range(2)])
            xtile, b_xtile = zip(*[sb(f"xtile{i}", [128, H], F32, sc) for i in range(2)])
            otile, b_otile = zip(*[sb(f"otile{i}", [128, H], F32, sc) for i in range(2)])
            t1c, b_t1c = zip(*[sb(f"t1c{i}", [128, 512], F32, sc) for i in range(2)])
            gfc, b_gfc = zip(*[sb(f"gfc{i}", [128, 512], F32, sc) for i in range(2)])
            sgc, b_sgc = zip(*[sb(f"sgc{i}", [128, 512], F32, sc) for i in range(2)])
            p.add("sync", lambda s: s.dma_start(
                out=sdownbc, in_=sdown_r.partition_broadcast(128)),
                writes=[b_sdownbc], dma=True)
            for pa in range(8):
                t0 = pa * 256
                for ic in range(OCN):
                    i2 = ic % 4
                    p.add("sync", lambda s, ic=ic, i2=i2: s.dma_start(
                        out=wdblk[i2], in_=wdb_s[ic * 128:(ic + 1) * 128, :]),
                        reads=[dbuf["wdb"]], writes=[b_wdblk[i2]], dma=True)
                    p.add("sync", lambda s, ic=ic, i2=i2, t0=t0: s.dma_start(
                        out=fch[i2], in_=fuse_s[ic * 128:(ic + 1) * 128, t0:t0 + 256]),
                        reads=[dbuf["fuse"]], writes=[b_fch[i2]], dma=True)
                    for ti in range(2):
                        for nh in range(4):
                            p.add("tensor", lambda t, ic=ic, i2=i2, ti=ti, nh=nh:
                                  t.matmul(ps[ti * 4 + nh],
                                           fch[i2][:, ti * 128:(ti + 1) * 128],
                                           wdblk[i2][:, nh * 512:(nh + 1) * 512],
                                           start=(ic == 0), stop=(ic == OCN - 1)),
                                  reads=[b_fch[i2], b_wdblk[i2]],
                                  writes=[b_ps[ti * 4 + nh]])
                for ti in range(2):
                    r0 = t0 + ti * 128
                    i2 = ti % 2
                    p.add("gpsimd", lambda gp, r0=r0, i2=i2: gp.dma_start(
                        out=xtile[i2], in_=xm[r0:r0 + 128, :]),
                        writes=[b_xtile[i2]], dma=True)
                    p.add("gpsimd", lambda gp, r0=r0, i2=i2: gp.dma_start(
                        out=satile[i2], in_=sattn_s[r0:r0 + 128, :]),
                        reads=[dbuf["sattn"]], writes=[b_satile[i2]], dma=True)
                    for nh in range(4):
                        n2 = nh % 2
                        p.add("vector", lambda v, ti=ti, nh=nh, n2=n2: v.tensor_tensor(
                            out=t1c[n2], in0=ps[ti * 4 + nh],
                            in1=sdownbc[:, nh * 512:(nh + 1) * 512], op=ALU.mult),
                            reads=[b_ps[ti * 4 + nh], b_sdownbc], writes=[b_t1c[n2]])
                        p.add("scalar", lambda a, n2=n2: a.activation(
                            out=gfc[n2], in_=t1c[n2], func=AF.Silu),
                            reads=[b_t1c[n2]], writes=[b_gfc[n2]])
                        p.add("vector", lambda v, i2=i2, nh=nh, n2=n2: v.tensor_tensor(
                            out=sgc[n2], in0=gfc[n2],
                            in1=satile[i2][:, nh * 512:(nh + 1) * 512], op=ALU.mult),
                            reads=[b_gfc[n2], b_satile[i2]], writes=[b_sgc[n2]])
                        p.add("vector", lambda v, i2=i2, nh=nh, n2=n2: v.tensor_tensor(
                            out=otile[i2][:, nh * 512:(nh + 1) * 512], in0=sgc[n2],
                            in1=xtile[i2][:, nh * 512:(nh + 1) * 512], op=ALU.add),
                            reads=[b_sgc[n2], b_xtile[i2]], writes=[b_otile[i2]])
                    p.add("gpsimd", lambda gp, r0=r0, i2=i2: gp.dma_start(
                        out=out_d[r0:r0 + 128, :], in_=otile[i2]),
                        reads=[b_otile[i2]], writes=[dbuf["out"]], dma=True)
            p.add("sync", lambda s: s.nop(),
                  extra=[dbuf["out"].w] if dbuf["out"].w else [])
            p.flush(blk)

    p.flush(blk)
    ctx.close()
    return nc, taps


def _host_prep(Wi, Wf, Wg, Wo, wg8, wu8, wd8, norms):
    bf = ml_dtypes.bfloat16
    normi, normf, normg, normo, rmsw, sgate, sup, sdown = norms
    return {
        "wfsw": _blin_swz(Wf),
        "wisw": _blin_swz(Wi),
        "wgsw": _blin_swz(Wg),
        "wosw": _blin_swz(Wo),
        "wg8": _ffn_swz(wg8),
        "wu8": _ffn_swz(wu8),
        "wd8": np.ascontiguousarray(np.asarray(wd8, np.int8)),
        "normf_r": np.asarray(normf, np.float32).reshape(1, H),
        "normi_r": np.asarray(normi, np.float32).reshape(1, H),
        "normg_r": np.asarray(normg, np.float32).reshape(1, H),
        "normo_fm_h": np.ascontiguousarray(
            np.asarray(normo, np.float32).reshape(KC, 128).T),
        "rmsw_fm_h": np.ascontiguousarray(
            np.asarray(rmsw, np.float32).reshape(KC, 128).T),
        "sgate_fm_h": np.ascontiguousarray(
            np.asarray(sgate, np.float32).reshape(OCN, 128).T),
        "sup_fm_h": np.ascontiguousarray(
            np.asarray(sup, np.float32).reshape(OCN, 128).T),
        "sdown_r": np.asarray(sdown, np.float32).reshape(1, H),
        "identb_h": np.eye(128).astype(bf),
        "identf_h": np.eye(128, dtype=np.float32),
        "onesf_h": np.ones((1, 128), np.float32),
        "onesb_h": np.ones((128, 1), bf),
        "onescf_h": np.ones((128, 1), np.float32),
    }


def kernel(L0_attn_x, L0_attn_Wi, L0_attn_Wf, L0_attn_Wg, L0_attn_Wo,
           L0_attn_norm_i, L0_attn_norm_f, L0_attn_norm_g, L0_attn_norm_o,
           L0_attn_rms_w, L0_ffn_w_gate, L0_ffn_w_up, L0_ffn_w_down,
           L0_ffn_s_gate, L0_ffn_s_up, L0_ffn_s_down):
    x = np.asarray(L0_attn_x, np.float32)
    B, L, _ = x.shape
    shared = _host_prep(L0_attn_Wi, L0_attn_Wf, L0_attn_Wg, L0_attn_Wo,
                        L0_ffn_w_gate, L0_ffn_w_up, L0_ffn_w_down,
                        (L0_attn_norm_i, L0_attn_norm_f, L0_attn_norm_g,
                         L0_attn_norm_o, L0_attn_rms_w, L0_ffn_s_gate,
                         L0_ffn_s_up, L0_ffn_s_down))
    nc, _ = build_program()
    in_maps = []
    for c in range(8):
        b = c % 4
        hf = c // 4
        m = dict(shared)
        m["xm"] = np.ascontiguousarray(x[b, hf * TOK:(hf + 1) * TOK])
        sm = np.zeros((128, 8), np.float32)
        if hf == 1:
            sm[:, b] = 1.0
        m["selmask"] = sm
        in_maps.append(m)
    res = run_bass_kernel_spmd(nc, in_maps, core_ids=list(range(8)))
    out = np.empty((B, L, H), np.float32)
    for c, r in enumerate(res.results):
        b = c % 4
        hf = c // 4
        out[b, hf * TOK:(hf + 1) * TOK] = r["out"]
    return out


def _dev_run(in_map, taps=(), n=1):
    _TAPS.update(taps)
    nc, _ = build_program()
    outs = []
    for _ in range(n):
        res = run_bass_kernel_spmd(nc, [dict(in_map)], core_ids=[0])
        outs.append(res.results[0])
    return outs


# version 28
# speedup vs baseline: 1.0983x; 1.0836x over previous
import contextlib
import numpy as np
import ml_dtypes

try:
    import concourse.bass as bass
except ImportError:
    import sys
    sys.path.insert(0, "/opt/trn_rl_repo")
    import concourse.bass as bass
import concourse.mybir as mybir
from concourse.bass_utils import run_bass_kernel_spmd

F32 = mybir.dt.float32
BF16 = mybir.dt.bfloat16
I8 = mybir.dt.int8
I32 = mybir.dt.int32
AF = mybir.ActivationFunctionType
ALU = mybir.AluOpType
AX = mybir.AxisListType

H = 2048
IFF = 8192
KC = H // 128
OCN = IFF // 128
G = 512
TTN = G // 128
NG = 4
TOK = 2048
ENGINES = ("sync", "tensor", "vector", "scalar", "gpsimd")
SELF_GAP = 4

_TAPS = set()



def _w_sync_c(pr, sem, v):
    pr.wait_ge(sem, v)

def _w_tensor_c(pr, sem, v):
    pr.wait_ge(sem, v)

def _w_vector_c(pr, sem, v):
    pr.wait_ge(sem, v)

def _w_scalar_c(pr, sem, v):
    pr.wait_ge(sem, v)

def _w_gpsimd_c(pr, sem, v):
    pr.wait_ge(sem, v)

def _w_sync_d(pr, sem, v):
    pr.wait_ge(sem, v)

def _w_tensor_d(pr, sem, v):
    pr.wait_ge(sem, v)

def _w_vector_d(pr, sem, v):
    pr.wait_ge(sem, v)

def _w_scalar_d(pr, sem, v):
    pr.wait_ge(sem, v)

def _w_gpsimd_d(pr, sem, v):
    pr.wait_ge(sem, v)

_WAITERS = {("sync", "c"): _w_sync_c, ("tensor", "c"): _w_tensor_c,
            ("vector", "c"): _w_vector_c, ("scalar", "c"): _w_scalar_c,
            ("gpsimd", "c"): _w_gpsimd_c,
            ("sync", "d"): _w_sync_d, ("tensor", "d"): _w_tensor_d,
            ("vector", "d"): _w_vector_d, ("scalar", "d"): _w_scalar_d,
            ("gpsimd", "d"): _w_gpsimd_d}


class Buf:
    __slots__ = ("w", "r")

    def __init__(self):
        self.w = None
        self.r = {}


NSLOT = {"sync": 24, "gpsimd": 16, "vector": 2, "scalar": 2, "tensor": 1}


class Prog:

    def __init__(self, nc):
        self.nc = nc
        self.cnt = {e: 0 for e in ENGINES}
        self.dcnt = {(e, k): 0 for e in ENGINES for k in range(NSLOT[e])}
        self.dslot = {e: 0 for e in ENGINES}
        self.idx = {e: 0 for e in ENGINES}
        self.pend = {e: [] for e in ENGINES}
        self.waited = {}
        self.semc = {}
        self.semd = {}

    def add(self, eng, fn, reads=(), writes=(), dma=False, cc=False, extra=()):
        idx = self.idx[eng] + 1
        waits = {}

        def dep(t):
            if t is None:
                return
            pe, pk, pc, pidx = t
            if pc <= 0:
                return
            if pk == "c" and pe == eng:
                if pe != "tensor" and idx - pidx <= SELF_GAP:
                    waits[(pe, pk)] = max(waits.get((pe, pk), 0), pc)
            else:
                waits[(pe, pk)] = max(waits.get((pe, pk), 0), pc)

        for b in reads:
            dep(b.w)
        for b in writes:
            dep(b.w)
            for pe, (pk, pc, pidx) in b.r.items():
                dep((pe, pk, pc, pidx))
        for t in extra:
            dep(t)

        if cc:
            self.cccnt = getattr(self, "cccnt", 0) + 1
            kind = ("cc", 0)
            mycnt = self.cccnt
        elif dma:
            k = self.dslot[eng]
            self.dslot[eng] = (k + 1) % NSLOT[eng]
            if self.dcnt[(eng, k)] > 0:
                waits[(eng, ("d", k))] = max(
                    waits.get((eng, ("d", k)), 0), self.dcnt[(eng, k)])
            self.dcnt[(eng, k)] += 16
            kind = ("d", k)
            mycnt = self.dcnt[(eng, k)]
        else:
            self.cnt[eng] += 1
            kind = "c"
            mycnt = self.cnt[eng]

        if cc:
            pass
        final = []
        for (pe, pk), pc in sorted(waits.items(), key=str):
            key = (eng, pe, pk)
            if self.waited.get(key, -1) >= pc:
                continue
            self.waited[key] = pc
            final.append((pe, pk, pc))

        self.idx[eng] = idx
        self.pend[eng].append((final, fn, kind))
        t = (eng, kind, mycnt, idx)
        for b in reads:
            b.r[eng] = (kind, mycnt, idx)
        for b in writes:
            b.w = t
            b.r = {}
        return t

    def flush(self, blk):
        for eng in ENGINES:
            ops = self.pend[eng]
            if not ops:
                continue
            self.pend[eng] = []

            def emit(proxy, ops=ops, eng=eng):
                for waits, fn, kind in ops:
                    for pe, pk, pc in waits:
                        if pk == "c":
                            _WAITERS[(pe, "c")](proxy, self.semc[pe], pc)
                        elif pk[0] == "cc":
                            proxy.wait_ge(self.ccsem, 1)
                        else:
                            _WAITERS[(pe, "d")](proxy, self.semd[pe][pk[1]], pc)
                    instr = fn(proxy)
                    if kind == "c":
                        instr.then_inc(self.semc[eng], 1)
                    elif kind[0] == "cc":
                        instr.then_inc(self.ccsem)
                    else:
                        instr.then_inc(self.semd[eng][kind[1]], 16)

            getattr(blk, eng)(emit)

    def barrier(self):
        toks = []
        if getattr(self, "cccnt", 0) > 0:
            toks.append(("gpsimd", ("cc", 0), 1, 0))
        for e in ENGINES:
            if self.cnt[e] > 0:
                toks.append((e, "c", self.cnt[e], self.idx[e]))
            for k in range(NSLOT[e]):
                if self.dcnt[(e, k)] > 0:
                    toks.append((e, ("d", k), self.dcnt[(e, k)], self.idx[e]))
        for e in ENGINES:
            self.add(e, lambda p: p.nop(), extra=toks)


def _blin_swz(w):
    return np.ascontiguousarray(
        np.asarray(w, np.float32).reshape(16, 128, 16, 128)
        .transpose(0, 3, 2, 1).reshape(2048, 2048))


def _ffn_swz(w):
    return np.ascontiguousarray(
        np.asarray(w, np.int8).reshape(16, 128, 64, 128)
        .transpose(2, 1, 0, 3).reshape(8192, 2048))


def build_program(n_pre=NG, n_main=NG, do_prep=True, do_groups=True,
                  do_f1=True, do_f2=True):
    nc = bass.Bass("TRN2", target_bir_lowering=False)
    p = Prog(nc)
    taps = []

    def din(name, shape, dtype):
        return nc.dram_tensor(name, shape, dtype, kind="ExternalInput").ap()

    xm = din("xm", [TOK, H], F32)
    selmask = din("selmask", [128, 8], F32)
    wsw = {m: din(f"w{m}sw", [H, H], F32) for m in "figo"}
    wg8 = din("wg8", [IFF, H], I8)
    wu8 = din("wu8", [IFF, H], I8)
    wd8 = din("wd8", [IFF, H], I8)
    norm_r = {m: din(f"norm{m}_r", [1, H], F32) for m in "fig"}
    normo_fm_h = din("normo_fm_h", [128, KC], F32)
    rmsw_fm_h = din("rmsw_fm_h", [128, KC], F32)
    sgate_fm_h = din("sgate_fm_h", [128, OCN], F32)
    sup_fm_h = din("sup_fm_h", [128, OCN], F32)
    sdown_r = din("sdown_r", [1, H], F32)
    identb_h = din("identb_h", [128, 128], BF16)
    identf_h = din("identf_h", [128, 128], F32)
    onesf_h = din("onesf_h", [1, 128], F32)
    onesb_h = din("onesb_h", [128, 1], BF16)
    onescf_h = din("onescf_h", [128, 1], F32)
    out_d = nc.dram_tensor("out", [TOK, H], F32, kind="ExternalOutput").ap()

    wq_s = {m: nc.dram_tensor(f"wq_{m}", [H, H], BF16, kind="Internal").ap()
            for m in "figo"}
    wdb_s = nc.dram_tensor("wdb", [IFF, H], BF16, kind="Internal").ap()
    xnT_s = nc.dram_tensor("xnT_s", [H, TOK], BF16, kind="Internal").ap()
    sattn_s = nc.dram_tensor("sattn_s", [TOK, H], F32, kind="Internal").ap()
    fuse_s = nc.dram_tensor("fuse_s", [IFF, TOK], BF16, kind="Internal").ap()
    sloc_s = nc.dram_tensor("sloc_s", [4 * 128, KC * G], F32, kind="Internal").ap()
    cprod_s = nc.dram_tensor("cprod_s", [4 * 128, KC * G], F32, kind="Internal").ap()
    xqg_s = nc.dram_tensor("xqg_s", [4 * 128, KC * G], BF16, kind="Internal").ap()
    carry_s = nc.dram_tensor("carry_s", [128, KC], F32, kind="Internal").ap()
    ag_s = nc.dram_tensor("ag_s", [8 * 128, KC], F32, kind="Internal").ap()
    sgrow_s = nc.dram_tensor("sgrow_s", [4, 512], F32, kind="Internal").ap()

    dbuf = {name: Buf() for name in
            ["wq_f", "wq_i", "wq_g", "wq_o", "wdb",
             "xnT", "sattn", "fuse", "out", "sloc", "cprod", "xqg",
             "carry", "ag", "sgrow"]}

    ctx = contextlib.ExitStack()
    for e in ENGINES:
        p.semc[e] = ctx.enter_context(nc.semaphore(f"sc_{e}"))
        p.semd[e] = [ctx.enter_context(nc.semaphore(f"sd_{e}{k}"))
                     for k in range(NSLOT[e])]
    p.ccsem = ctx.enter_context(nc.semaphore("ccsem"))

    def sb(name, shape, dtype, scope):
        h = scope.enter_context(nc.sbuf_tensor(name, shape, dtype))
        return h.ap(), Buf()

    def tap(name, ap, buf, shape, dtype):
        if name not in _TAPS:
            return
        dtype = {"float32": F32, "bfloat16": BF16}[dtype] if isinstance(dtype, str) else dtype
        t = nc.dram_tensor("tap_" + name, shape, dtype, kind="ExternalOutput").ap()
        taps.append(name)
        p.add("gpsimd", lambda g, t=t, ap=ap: g.dma_start(out=t, in_=ap),
              reads=[buf], dma=True)

    identb, b_identb = sb("identb", [128, 128], BF16, ctx)
    identf, b_identf = sb("identf", [128, 128], F32, ctx)
    onesf, b_onesf = sb("onesf", [1, 128], F32, ctx)
    onesb, b_onesb = sb("onesb", [128, 1], BF16, ctx)
    onescf, b_onescf = sb("onescf", [128, 1], F32, ctx)
    normo_fm, b_normo = sb("normo_fm", [128, KC], F32, ctx)
    rmsw_fm, b_rmsw = sb("rmsw_fm", [128, KC], F32, ctx)
    sgate_fm, b_sgate = sb("sgate_fm", [128, OCN], F32, ctx)
    sup_fm, b_sup = sb("sup_fm", [128, OCN], F32, ctx)
    swc, b_swc_t = sb("swc", [128, 8], F32, ctx)
    swcol = {m: swc[:, j:j + 1] for j, m in enumerate("figo")}
    swocol = {m: swc[:, 4 + j:5 + j] for j, m in enumerate("figo")}
    b_swcol = {m: Buf() for m in "figo"}
    b_swocol = {m: Buf() for m in "figo"}

    pall_t = ctx.enter_context(nc.psum_tensor("pall", [128, 4096], F32))
    pall = pall_t.ap()
    pallb = pall.bitcast(BF16)
    ps = [pall[:, 512 * i:512 * (i + 1)] for i in range(8)]
    ps2 = [pall[:, 1024 * i:1024 * (i + 1)] for i in range(4)]
    b_ps = [Buf() for _ in range(8)]

    def psb(i):
        return pallb[:, 1024 * i:1024 * (i + 1)]

    for ap_, buf_, src in [(identb, b_identb, identb_h), (identf, b_identf, identf_h),
                           (onesf, b_onesf, onesf_h), (onesb, b_onesb, onesb_h), (onescf, b_onescf, onescf_h),
                           (normo_fm, b_normo, normo_fm_h), (rmsw_fm, b_rmsw, rmsw_fm_h),
                           (sgate_fm, b_sgate, sgate_fm_h), (sup_fm, b_sup, sup_fm_h)]:
        p.add("sync", lambda s, o=ap_, i=src: s.dma_start(out=o, in_=i),
              writes=[buf_], dma=True)

    blk = ctx.enter_context(nc.Block())

    if do_prep:
        with contextlib.ExitStack() as sc:
            wfp, b_wfp = zip(*[sb(f"wfp{i}", [128, H], F32, sc) for i in range(4)])
            wab, b_wab = zip(*[sb(f"wab{i}", [128, H], F32, sc) for i in range(2)])
            q32, b_q32 = zip(*[sb(f"q32_{i}", [128, H], I32, sc) for i in range(2)])
            qbf, b_qbf = zip(*[sb(f"qbf{i}", [128, H], BF16, sc) for i in range(2)])
            prow, b_prow = sb("prow", [1, 512], F32, sc)

            def s1_iter(m, ec):
                i0 = 2 + ec % 2
                ea = ec % 2
                src = wsw[m]
                p.add("sync", lambda s, o=wfp[i0], i=src[ec * 128:(ec + 1) * 128, :]:
                      s.dma_start(out=o, in_=i), writes=[b_wfp[i0]], dma=True)
                p.add("scalar", lambda a, i0=i0, ea=ea: a.activation(
                    out=wab[ea], in_=wfp[i0], func=AF.Abs),
                    reads=[b_wfp[i0]], writes=[b_wab[ea]])
                for j in range(4):
                    p.add("tensor", lambda t, ea=ea, j=j, st=(ec == 0 and j == 0),
                          sp=(ec == KC - 1 and j == 3):
                          t.matmul(ps[5][0:1, :], onescf,
                                   wab[ea][:, j * 512:(j + 1) * 512],
                                   start=st, stop=sp),
                          reads=[b_wab[ea], b_onescf], writes=[b_ps[5]])

            def s1_tail(m):
                p.add("vector", lambda v: v.tensor_reduce(
                    out=prow[0:1, 0:1], in_=ps[5][0:1, :], axis=AX.X, op=ALU.add),
                    reads=[b_ps[5]], writes=[b_prow])
                p.add("vector", lambda v: v.tensor_scalar(
                    out=prow[0:1, 1:2], in0=prow[0:1, 0:1], scalar1=1.0 / (H * H),
                    scalar2=1e-5, op0=ALU.mult, op1=ALU.max),
                    reads=[b_prow], writes=[b_prow])
                p.add("vector", lambda v: v.tensor_scalar(
                    out=prow[0:1, 2:3], in0=prow[0:1, 1:2], scalar1=1.0 / 127.0,
                    scalar2=None, op0=ALU.mult), reads=[b_prow], writes=[b_prow])
                p.add("vector", lambda v: v.reciprocal(
                    out=prow[0:1, 3:4], in_=prow[0:1, 1:2]),
                    reads=[b_prow], writes=[b_prow])
                p.add("tensor", lambda t: t.matmul(
                    ps[6][:, 0:2], onesf, prow[0:1, 2:4], start=True, stop=True),
                    reads=[b_prow, b_onesf], writes=[b_ps[6]])
                jm = "figo".index(m)
                p.add("vector", lambda v, jm=jm, m=m: v.tensor_copy(
                    out=swc[:, 4 + jm:5 + jm], in_=ps[6][:, 0:1]),
                    reads=[b_ps[6]], writes=[b_swocol[m]])
                p.add("vector", lambda v, jm=jm, m=m: v.tensor_copy(
                    out=swc[:, jm:jm + 1], in_=ps[6][:, 1:2]),
                    reads=[b_ps[6]], writes=[b_swcol[m]])

            def s2_iter(m, ec):
                i0 = ec % 2
                src = wsw[m]
                p.add("sync", lambda s, o=wfp[i0], i=src[ec * 128:(ec + 1) * 128, :]:
                      s.dma_start(out=o, in_=i), writes=[b_wfp[i0]], dma=True)
                p.add("vector", lambda v, i0=i0, m=m: v.tensor_scalar(
                    out=q32[i0], in0=wfp[i0], scalar1=swcol[m], scalar2=None,
                    op0=ALU.mult),
                    reads=[b_wfp[i0], b_swcol[m]], writes=[b_q32[i0]])
                p.add("vector", lambda v, i0=i0: v.tensor_scalar(
                    out=qbf[i0], in0=q32[i0], scalar1=1.0, scalar2=-1.0,
                    op0=ALU.min, op1=ALU.max), reads=[b_q32[i0]], writes=[b_qbf[i0]])
                p.add("gpsimd", lambda g, i0=i0, m=m, ec=ec: g.dma_start(
                    out=wq_s[m][ec * 128:(ec + 1) * 128, :], in_=qbf[i0]),
                    reads=[b_qbf[i0]], writes=[dbuf["wq_" + m]], dma=True)

            order = "figo"
            for ec in range(KC):
                s1_iter("f", ec)
            s1_tail("f")
            for mi in range(4):
                m = order[mi]
                nxt = order[mi + 1] if mi + 1 < 4 else None
                for ec in range(KC):
                    s2_iter(m, ec)
                    if nxt is not None:
                        s1_iter(nxt, ec)
                if nxt is not None:
                    s1_tail(nxt)
            p.barrier()
            p.flush(blk)

    if do_groups:
      with contextlib.ExitStack() as sc:
        normbc, b_normbc = zip(*[sb(f"normbc{i}", [128, H], F32, sc) for i in range(1)])
        A, b_A = sb("Abuf", [128, KC * G], F32, sc)
        sfull, b_s = sb("sfull", [128, KC * G], F32, sc)
        xg, b_xg = zip(*[sb(f"xg{i}", [128, H], F32, sc) for i in range(2)])
        wb, b_wb = sb("wbuf", [128, H], F32, sc)
        xq8 = wb.bitcast(I8)[:, 0:H]
        b_xq8 = b_wb
        xqtm, b_xqtm = zip(*[sb(f"xqtm{i}", [128, H], BF16, sc) for i in range(2)])
        xqT2, b_xqT2_t = sb("xqT2", [128, 2 * KC * G], BF16, sc)
        xqT = [xqT2[:, 0:KC * G], xqT2[:, KC * G:]]
        b_xqT = [Buf(), Buf()]
        yqT, b_yqT = xqT[0], b_xqT[0]
        wqblk, b_wqblk = zip(*[sb(f"wqblk{i}", [128, H], BF16, sc) for i in range(4)])
        ch, b_ch = zip(*[sb(f"ch{i}", [128, 1024], F32, sc) for i in range(6)])
        zf, b_zf = ch[0], b_ch[0]
        zi, b_zi = ch[1], b_ch[1]
        thc, b_thc = ch[2], b_ch[2]
        fc, b_fc = ch[3], b_ch[3]
        fm1, b_fm1 = ch[4], b_ch[4]
        fic, b_fic = ch[5], b_ch[5]
        zg, b_zg = ch[0], b_ch[0]
        thg, b_thg = ch[2], b_ch[2]
        fgc, b_fgc = ch[3], b_ch[3]
        shatc, b_shatc = ch[4], b_ch[4]
        tyc, b_tyc = ch[5], b_ch[5]
        wyc, b_wyc = ch[1], b_ch[1]
        mrun, b_mrun = zip(*[sb(f"mrun{i}", [128, 512], F32, sc) for i in range(2)])
        sc3, b_sc3 = zip(*[sb(f"scbc{i}", [128, 1024], F32, sc) for i in range(3)])
        s2c, b_s2c = zip(*[sb(f"s2c{i}", [128, 512], BF16, sc) for i in range(1)])
        yq8c, b_yq8c = sb("yq8c", [128, 512], I8, sc)
        safm, b_safm = sb("safm", [128, 512], F32, sc)
        sasm, b_sasm = zip(*[sb(f"sasm{i}", [128, 128], F32, sc) for i in range(2)])
        xnstage, b_xnstage = sb("xnstage", [128, 8 * 128], BF16, sc)
        carry, b_carry = sb("carry", [128, KC], F32, sc)
        cols, b_cols = sb("cols", [128, 64], F32, sc)
        rows, b_rows = sb("rows", [1, 1024], F32, sc)

        p.add("vector", lambda v: v.memset(carry, 0.0), writes=[b_carry])

        def ln_pg_loads(xsrc, g, pg):
            for ti in range(2):
                tt = pg * 2 + ti
                r0 = (g % NG) * G + tt * 128
                p.add("sync", lambda s, ti=ti, i=xsrc[r0:r0 + 128, :]:
                      s.dma_start(out=xg[ti], in_=i), writes=[b_xg[ti]], dma=True)
                p.add("vector", lambda v, tt=tt, ti=ti: v.reduce_sum(
                    out=cols[:, tt:tt + 1], in_=xg[ti], axis=AX.X),
                    reads=[b_xg[ti]], writes=[b_cols])
                p.add("scalar", lambda a, tt=tt, ti=ti: a.activation(
                    out=xqtm[ti], in_=xg[ti], func=AF.Square,
                    accum_out=cols[:, 8 + tt:9 + tt]),
                    reads=[b_xg[ti]], writes=[b_xqtm[ti], b_cols])

        def ln_pg_chain(pg):
            t0 = pg * 2
            p.add("vector", lambda v, t0=t0: v.tensor_scalar(
                out=cols[:, 4 + t0:6 + t0], in0=cols[:, t0:t0 + 2],
                scalar1=1.0 / H, scalar2=None, op0=ALU.mult),
                reads=[b_cols], writes=[b_cols])
            p.add("vector", lambda v, t0=t0: v.tensor_tensor(
                out=cols[:, 12 + t0:14 + t0], in0=cols[:, 4 + t0:6 + t0],
                in1=cols[:, 4 + t0:6 + t0], op=ALU.mult),
                reads=[b_cols], writes=[b_cols])
            p.add("vector", lambda v, t0=t0: v.scalar_tensor_tensor(
                out=cols[:, 12 + t0:14 + t0], in0=cols[:, 8 + t0:10 + t0],
                scalar=1.0 / H, in1=cols[:, 12 + t0:14 + t0],
                op0=ALU.mult, op1=ALU.subtract),
                reads=[b_cols], writes=[b_cols])
            p.add("vector", lambda v, t0=t0: v.tensor_scalar(
                out=cols[:, 12 + t0:14 + t0], in0=cols[:, 12 + t0:14 + t0],
                scalar1=1e-5, scalar2=None, op0=ALU.add),
                reads=[b_cols], writes=[b_cols])
            p.add("scalar", lambda a, t0=t0: a.activation(
                out=cols[:, 12 + t0:14 + t0], in_=cols[:, 12 + t0:14 + t0],
                func=AF.Sqrt), reads=[b_cols], writes=[b_cols])
            p.add("vector", lambda v, t0=t0: v.reciprocal(
                out=cols[:, 12 + t0:14 + t0], in_=cols[:, 12 + t0:14 + t0]),
                reads=[b_cols], writes=[b_cols])

        def ln_astore(pg, ti):
            tt = pg * 2 + ti
            p.add("vector", lambda v, tt=tt, ti=ti: v.tensor_scalar(
                out=A[:, tt * H:(tt + 1) * H], in0=xg[ti],
                scalar1=cols[:, 4 + tt:5 + tt],
                scalar2=cols[:, 12 + tt:13 + tt],
                op0=ALU.subtract, op1=ALU.mult),
                reads=[b_xg[ti], b_cols], writes=[b_A])
            p.add("scalar", lambda a, tt=tt, ti=ti: a.activation(
                out=xqtm[ti], in_=A[:, tt * H:(tt + 1) * H], func=AF.Square,
                accum_out=cols[:, 16 + tt:17 + tt]),
                reads=[b_A], writes=[b_xqtm[ti], b_cols])

        def ln_rms_tail():
            p.add("vector", lambda v: v.tensor_scalar(
                out=cols[:, 20:24], in0=cols[:, 16:20],
                scalar1=1.0 / H, scalar2=1e-8, op0=ALU.mult, op1=ALU.add),
                reads=[b_cols], writes=[b_cols])
            p.add("scalar", lambda a: a.activation(
                out=cols[:, 20:24], in_=cols[:, 20:24], func=AF.Sqrt),
                reads=[b_cols], writes=[b_cols])
            p.add("vector", lambda v: v.reciprocal(
                out=cols[:, 20:24], in_=cols[:, 20:24]),
                reads=[b_cols], writes=[b_cols])

        def xnT_tt(g, tt):
            i0 = tt % 2
            p.add("vector", lambda v, i0=i0, tt=tt: v.tensor_copy(
                out=xqtm[i0], in_=A[:, tt * H:(tt + 1) * H]),
                reads=[b_A], writes=[b_xqtm[i0]])
            gl = (g - NG) * G + tt * 128
            for half in range(2):
                for kc4 in range(2):
                    for j in range(4):
                        kc = half * 8 + kc4 * 4 + j
                        p.add("tensor", lambda t, i0=i0, kc=kc, j=j: t.transpose(
                            psb(6)[:, j * 128:(j + 1) * 128],
                            xqtm[i0][:, kc * 128:(kc + 1) * 128], identb),
                            reads=[b_xqtm[i0], b_identb], writes=[b_ps[6]])
                    p.add("scalar", lambda a, kc4=kc4: a.copy(
                        out=xnstage[:, kc4 * 512:(kc4 + 1) * 512],
                        in_=psb(6)[:, 0:512]),
                        reads=[b_ps[6]], writes=[b_xnstage])
                p.add("gpsimd", lambda gp, gl=gl, half=half: gp.dma_start(
                    out=xnT_s[half * 1024:(half + 1) * 1024, gl:gl + 128]
                        .rearrange("(kc p) t -> p kc t", p=128),
                    in_=xnstage.rearrange("p (kc t) -> p kc t", t=128)),
                    reads=[b_xnstage], writes=[dbuf["xnT"]], dma=True)

        def s1_ln_gen(g):
            ln_pg_loads(xm, g, 0)
            yield
            ln_pg_chain(0)
            yield
            ln_astore(0, 0)
            yield
            ln_astore(0, 1)
            xnT_tt(g + NG, 0)
            yield
            ln_pg_loads(xm, g, 1)
            yield
            ln_pg_chain(1)
            xnT_tt(g + NG, 1)
            yield
            ln_astore(1, 0)
            ln_astore(1, 1)
            yield
            ln_rms_tail()
            xnT_tt(g + NG, 2)
            yield
            xnT_tt(g + NG, 3)

        def bcast_scale(row_ap, sidx):
            p.add("tensor", lambda t, row_ap=row_ap: t.matmul(
                ps[6], onesf, row_ap, start=True, stop=True),
                reads=[b_rows, b_onesf], writes=[b_ps[6]])
            p.add("scalar", lambda a, sidx=sidx: a.copy(
                out=sc3[sidx][:, 0:512], in_=ps[6]),
                reads=[b_ps[6]], writes=[b_sc3[sidx]])
            p.add("scalar", lambda a, sidx=sidx: a.copy(
                out=sc3[sidx][:, 512:1024], in_=ps[6]),
                reads=[b_ps[6]], writes=[b_sc3[sidx]])

        def quant(br, dstT, b_dstT, row_only=False):
            nb, b_nb = normbc[0], b_normbc[0]
            for tt in range(TTN):
                p.add("vector", lambda v, tt=tt, nb=nb: v.tensor_tensor(
                    out=wb, in0=A[:, tt * H:(tt + 1) * H], in1=nb, op=ALU.mult),
                    reads=[b_A, b_nb], writes=[b_wb])
                p.add("vector", lambda v, tt=tt: v.reduce_max(
                    out=cols[:, 24 + tt:25 + tt], in_=wb, axis=AX.X,
                    apply_absolute_value=True), reads=[b_wb], writes=[b_cols])
            p.add("vector", lambda v: v.tensor_scalar(
                out=cols[:, 28:32], in0=cols[:, 24:28],
                scalar1=1e-5, scalar2=1.0 / 127.0, op0=ALU.max, op1=ALU.mult),
                reads=[b_cols], writes=[b_cols])
            p.add("vector", lambda v: v.reciprocal(
                out=cols[:, 32:36], in_=cols[:, 28:32]),
                reads=[b_cols], writes=[b_cols])
            p.add("vector", lambda v: v.tensor_tensor(
                out=cols[:, 36:40], in0=cols[:, 28:32], in1=cols[:, 20:24],
                op=ALU.mult), reads=[b_cols], writes=[b_cols])
            p.add("vector", lambda v, br=br: v.tensor_scalar(
                out=cols[:, 36:40], in0=cols[:, 36:40],
                scalar1=swocol[br], scalar2=127.0, op0=ALU.mult, op1=ALU.mult),
                reads=[b_cols, b_swocol[br]], writes=[b_cols])
            for tt in range(TTN):
                p.add("vector", lambda v, tt=tt, nb=nb: v.scalar_tensor_tensor(
                    out=xq8, in0=A[:, tt * H:(tt + 1) * H],
                    scalar=cols[:, 32 + tt:33 + tt], in1=nb,
                    op0=ALU.mult, op1=ALU.mult),
                    reads=[b_A, b_cols, b_nb], writes=[b_xq8])
                i0 = tt % 2
                p.add("scalar", lambda a, i0=i0: a.copy(
                    out=xqtm[i0], in_=xq8), reads=[b_xq8], writes=[b_xqtm[i0]])
                for kc4 in range(4):
                    for j in range(4):
                        kc = kc4 * 4 + j
                        p.add("tensor", lambda t, i0=i0, kc=kc, j=j: t.transpose(
                            psb(4)[:, j * 128:(j + 1) * 128],
                            xqtm[i0][:, kc * 128:(kc + 1) * 128], identb),
                            reads=[b_xqtm[i0], b_identb], writes=[b_ps[4]])
                    p.add("scalar", lambda a, kc4=kc4, tt=tt, dstT=dstT: a.copy(
                        out=dstT.rearrange("p (kc t) -> p kc t", t=G)
                            [:, kc4 * 4:(kc4 + 1) * 4, tt * 128:(tt + 1) * 128],
                        in_=psb(4)[:, 0:512].rearrange("p (kc t) -> p kc t", t=128)),
                        reads=[b_ps[4]], writes=[b_dstT])
            for tt in range(TTN):
                p.add("tensor", lambda t, tt=tt: t.transpose(
                    ps[5][0:1, tt * 128:(tt + 1) * 128],
                    cols[:, 36 + tt:37 + tt], identf),
                    reads=[b_cols, b_identf], writes=[b_ps[5]])
            p.add("scalar", lambda a: a.copy(
                out=rows[0:1, 0:512], in_=ps[5][0:1, 0:512]),
                reads=[b_ps[5]], writes=[b_rows])
            if row_only:
                return
            sidx = {"f": 0, "i": 1, "g": 2}[br]
            bcast_scale(rows[0:1, 0:512], sidx)

        def wq_stream(m, ec, slot):
            p.add("sync", lambda s, o=wqblk[slot], i=wq_s[m][ec * 128:(ec + 1) * 128, :]:
                  s.dma_start(out=o, in_=i),
                  reads=[dbuf["wq_" + m]], writes=[b_wqblk[slot]], dma=True)

        def blin_mm(srcT, b_srcT, wslot, bank, chunk):
            for kc in range(KC):
                p.add("tensor", lambda t, kc=kc: t.matmul(
                    ps[bank], wqblk[wslot][:, kc * 128:(kc + 1) * 128],
                    srcT[:, kc * G:kc * G + G],
                    start=(kc == 0), stop=(kc == KC - 1)),
                    reads=[b_wqblk[wslot], b_srcT], writes=[b_ps[bank]])

        sout = [sfull[:, 0:1024], sfull[:, 1024:2048]]
        cout = [sfull[:, 2048:3072], sfull[:, 3072:4096]]
        b_sout = [Buf(), Buf()]
        b_cout = [Buf(), Buf()]
        onestile, b_ones1 = sb("onestile", [128, 512], BF16, sc)
        smsb, b_smsb = sb("smsb", [128, 8], F32, sc)
        agsb, b_agsb = sb("agsb", [128, 128], F32, sc)
        s_in, b_s_in = zip(*[sb(f"s_in{i}", [128, KC], F32, sc) for i in range(2)])
        ccarry, b_ccarry = sb("ccarry", [128, KC], F32, sc)

        p.add("vector", lambda v: v.memset(onestile, 1.0), writes=[b_ones1])
        p.add("vector", lambda v: v.memset(ccarry, 1.0), writes=[b_ccarry])
        p.add("sync", lambda s: s.dma_start(out=smsb, in_=selmask),
              writes=[b_smsb], dma=True)

        def s1_ln(g):
            for _ in s1_ln_gen(g):
                pass
            if g == 0:
                tap("xn", A, b_A, [128, KC * G], "float32")

        def s1_quants(g):
            p.add("sync", lambda s: s.dma_start(
                out=normbc[0], in_=norm_r["g"].partition_broadcast(128)),
                writes=[b_normbc[0]], dma=True)
            quant("g", xqT[0], b_xqT[0], row_only=True)
            p.add("gpsimd", lambda gp, g=g: gp.dma_start(
                out=sgrow_s[g:g + 1, :], in_=rows[0:1, 0:512]),
                reads=[b_rows], writes=[dbuf["sgrow"]], dma=True)
            p.add("gpsimd", lambda gp, g=g: gp.dma_start(
                out=xqg_s[g * 128:(g + 1) * 128, :], in_=xqT[0]),
                reads=[b_xqT[0]], writes=[dbuf["xqg"]], dma=True)
            p.add("sync", lambda s: s.dma_start(
                out=normbc[0], in_=norm_r["f"].partition_broadcast(128)),
                writes=[b_normbc[0]], dma=True)
            quant("f", xqT[0], b_xqT[0])
            if g == 0:
                tap("xqTf", xqT[0], b_xqT[0], [128, KC * G], "bfloat16")
            p.add("sync", lambda s: s.dma_start(
                out=normbc[0], in_=norm_r["i"].partition_broadcast(128)),
                writes=[b_normbc[0]], dma=True)
            quant("i", xqT[1], b_xqT[1])

        def s1_loop(g, next_g):
            gen = s1_ln_gen(next_g) if next_g is not None else None
            for ep in range(KC // 2):
                pset = ep % 2
                ca, cb = 2 * ep, 2 * ep + 1
                sf = 2 * pset
                si = 2 - 2 * pset
                wq_stream("f", ca, sf)
                wq_stream("f", cb, sf + 1)
                wq_stream("i", ca, si)
                wq_stream("i", cb, si + 1)
                blin_mm(xqT[0], b_xqT[0], sf, 2 * pset + 0, ca)
                blin_mm(xqT[0], b_xqT[0], sf + 1, 2 * pset + 1, cb)
                blin_mm(xqT[1], b_xqT[1], si, 2 * pset + 4, ca)
                blin_mm(xqT[1], b_xqT[1], si + 1, 2 * pset + 5, cb)
                p.add("vector", lambda v, pset=pset: v.tensor_tensor(
                    out=zf, in0=ps2[pset], in1=sc3[0], op=ALU.mult),
                    reads=[b_ps[2 * pset], b_ps[2 * pset + 1], b_sc3[0]],
                    writes=[b_zf])
                p.add("vector", lambda v, pset=pset: v.tensor_tensor(
                    out=zi, in0=ps2[2 + pset], in1=sc3[1], op=ALU.mult),
                    reads=[b_ps[4 + 2 * pset], b_ps[5 + 2 * pset], b_sc3[1]],
                    writes=[b_zi])
                p.add("scalar", lambda a: a.activation(
                    out=thc, in_=zf, func=AF.Tanh, scale=0.5),
                    reads=[b_zf], writes=[b_thc])
                p.add("gpsimd", lambda gp: gp.tensor_scalar(
                    out=fc, in0=thc, scalar1=0.5, scalar2=0.5,
                    op0=ALU.mult, op1=ALU.add), reads=[b_thc], writes=[b_fc])
                p.add("gpsimd", lambda gp: gp.tensor_scalar(
                    out=fm1, in0=fc, scalar1=-1.0, scalar2=1.0,
                    op0=ALU.mult, op1=ALU.add), reads=[b_fc], writes=[b_fm1])
                p.add("scalar", lambda a: a.activation(
                    out=thc, in_=zi, func=AF.Tanh, scale=0.5),
                    reads=[b_zi], writes=[b_thc])
                p.add("gpsimd", lambda gp: gp.tensor_scalar(
                    out=fic, in0=thc, scalar1=0.5, scalar2=0.5,
                    op0=ALU.mult, op1=ALU.add), reads=[b_thc], writes=[b_fic])
                p.add("gpsimd", lambda gp: gp.tensor_tensor(
                    out=thc, in0=fic, in1=fm1, op=ALU.mult),
                    reads=[b_fic, b_fm1], writes=[b_thc])
                if gen is not None:
                    next(gen, None)
                p.add("vector", lambda v: v.tensor_tensor(
                    out=zf, in0=zi, in1=thc, op=ALU.mult),
                    reads=[b_zi, b_thc], writes=[b_zf])
                so, bso = sout[pset], b_sout[pset]
                co, bco = cout[pset], b_cout[pset]
                for sub, cx in ((0, ca), (1, cb)):
                    p.add("vector", lambda v, sub=sub, cx=cx, so=so: v.tensor_tensor_scan(
                        out=so[:, sub * G:(sub + 1) * G],
                        data0=fc[:, sub * G:(sub + 1) * G],
                        data1=zf[:, sub * G:(sub + 1) * G],
                        initial=carry[:, cx:cx + 1], op0=ALU.mult, op1=ALU.add),
                        reads=[b_fc, b_zf, b_carry], writes=[bso])
                    p.add("vector", lambda v, sub=sub, cx=cx, co=co: v.tensor_tensor_scan(
                        out=co[:, sub * G:(sub + 1) * G],
                        data0=fc[:, sub * G:(sub + 1) * G],
                        data1=onestile,
                        initial=ccarry[:, cx:cx + 1], op0=ALU.mult, op1=ALU.mult),
                        reads=[b_fc, b_ones1, b_ccarry], writes=[bco])
                p.add("vector", lambda v, ca=ca, so=so: v.tensor_copy(
                    out=carry[:, ca:ca + 2],
                    in_=so.rearrange("p (c t) -> p c t", t=G)
                        [:, 0:2, G - 1:G].rearrange("p c t -> p (c t)")),
                    reads=[bso], writes=[b_carry])
                p.add("vector", lambda v, ca=ca, co=co: v.tensor_copy(
                    out=ccarry[:, ca:ca + 2],
                    in_=co.rearrange("p (c t) -> p c t", t=G)
                        [:, 0:2, G - 1:G].rearrange("p c t -> p (c t)")),
                    reads=[bco], writes=[b_ccarry])
                r0 = g * 128
                p.add("gpsimd", lambda gp, r0=r0, ep=ep, so=so: gp.dma_start(
                    out=sloc_s[r0:r0 + 128, ep * 1024:(ep + 1) * 1024], in_=so),
                    reads=[bso], writes=[dbuf["sloc"]], dma=True)
                p.add("gpsimd", lambda gp, r0=r0, ep=ep, co=co: gp.dma_start(
                    out=cprod_s[r0:r0 + 128, ep * 1024:(ep + 1) * 1024], in_=co),
                    reads=[bco], writes=[dbuf["cprod"]], dma=True)
            if gen is not None:
                for _ in gen:
                    pass

        def s1_group_driver():
            s1_ln(0)
            s1_quants(0)
            for g in range(4):
                s1_loop(g, g + 1 if g < 3 else None)
                if g < 3:
                    s1_quants(g + 1)

        def exchange():
            p.add("gpsimd", lambda gp: gp.dma_start(out=carry_s, in_=carry),
                  reads=[b_carry], writes=[dbuf["carry"]], dma=True)
            p.add("gpsimd", lambda gp: gp.collective_compute(
                "AllGather", ALU.bypass, replica_groups=[list(range(8))],
                ins=[carry_s.opt()], outs=[ag_s.opt()]),
                reads=[dbuf["carry"]], writes=[dbuf["ag"]], cc=True)
            p.add("sync", lambda s: s.dma_start(
                out=agsb.rearrange("p (j c) -> p j c", c=KC),
                in_=ag_s.rearrange("(j p) c -> p j c", p=128)),
                reads=[dbuf["ag"]], writes=[b_agsb], dma=True)
            p.add("vector", lambda v: v.tensor_scalar(
                out=s_in[0], in0=agsb[:, 0:KC], scalar1=smsb[:, 0:1],
                scalar2=None, op0=ALU.mult),
                reads=[b_agsb, b_smsb], writes=[b_s_in[0]])
            for j in range(1, 8):
                p.add("vector", lambda v, j=j: v.scalar_tensor_tensor(
                    out=s_in[j % 2], in0=agsb[:, j * KC:(j + 1) * KC],
                    scalar=smsb[:, j:j + 1], in1=s_in[(j + 1) % 2],
                    op0=ALU.mult, op1=ALU.add),
                    reads=[b_agsb, b_smsb, b_s_in[(j + 1) % 2]],
                    writes=[b_s_in[j % 2]])

        sfin, b_sfin = s_in[7 % 2], b_s_in[7 % 2]

        def d_load(g):
            r0 = g * 128
            p.add("scalar", lambda a, r0=r0: a.dma_start(
                out=sfull, in_=sloc_s[r0:r0 + 128, :]),
                reads=[dbuf["sloc"]],
                writes=[b_s, b_sout[0], b_sout[1], b_cout[0], b_cout[1]], dma=True)
            p.add("scalar", lambda a, r0=r0: a.dma_start(
                out=A, in_=cprod_s[r0:r0 + 128, :]),
                reads=[dbuf["cprod"]], writes=[b_A], dma=True)

        def d_fix_head(g):
            gq, b_gq = xqT[g % 2], b_xqT[g % 2]
            r0 = g * 128
            p.add("scalar", lambda a, r0=r0, gq=gq: a.dma_start(
                out=gq, in_=xqg_s[r0:r0 + 128, :]),
                reads=[dbuf["xqg"]], writes=[b_gq], dma=True)
            p.add("sync", lambda s, g=g: s.dma_start(
                out=rows[0:1, 0:512], in_=sgrow_s[g:g + 1, :]),
                reads=[dbuf["sgrow"]], writes=[b_rows], dma=True)
            bcast_scale(rows[0:1, 0:512], 2)

        def d_fix_slices(g):
            for ep in range(KC // 2):
                for sub in range(2):
                    cx = 2 * ep + sub
                    p.add("vector", lambda v, ep=ep, sub=sub, cx=cx:
                          v.scalar_tensor_tensor(
                              out=zf[:, sub * G:(sub + 1) * G],
                              in0=A[:, cx * G:(cx + 1) * G],
                              scalar=sfin[:, cx:cx + 1],
                              in1=sfull[:, cx * G:(cx + 1) * G],
                              op0=ALU.mult, op1=ALU.add),
                          reads=[b_A, b_sfin, b_s], writes=[b_zf])
                wins = [b_sout[0], b_sout[1], b_cout[0], b_cout[1]]
                ww = [b_s] + ([wins[ep]] if ep < 4 else [])
                p.add("scalar", lambda a, ep=ep: a.copy(
                    out=sfull[:, ep * 1024:(ep + 1) * 1024], in_=zf),
                    reads=[b_zf], writes=ww)
                for sub in range(2):
                    ec = 2 * ep + sub
                    p.add("scalar", lambda a, ec=ec: a.activation(
                        out=s2c[0], in_=sfull[:, ec * G:(ec + 1) * G],
                        func=AF.Square), reads=[b_s], writes=[b_s2c[0]])
                    p.add("tensor", lambda t, ec=ec: t.matmul(
                        ps[3][0:1, :], onesb, s2c[0],
                        start=(ec == 0), stop=(ec == KC - 1)),
                        reads=[b_s2c[0], b_onesb], writes=[b_ps[3]])
                yield
            if g == 0:
                tap("s_main", sfull, b_s, [128, KC * G], "float32")

        def d_fix_tail(g):
            p.add("vector", lambda v: v.tensor_scalar(
                out=rows[0:1, 512:1024], in0=ps[3][0:1, :], scalar1=1.0 / H,
                scalar2=1e-5, op0=ALU.mult, op1=ALU.add),
                reads=[b_ps[3]], writes=[b_rows])
            p.add("scalar", lambda a: a.activation(
                out=rows[0:1, 512:1024], in_=rows[0:1, 512:1024], func=AF.Sqrt),
                reads=[b_rows], writes=[b_rows])
            p.add("vector", lambda v: v.reciprocal(
                out=rows[0:1, 512:1024], in_=rows[0:1, 512:1024]),
                reads=[b_rows], writes=[b_rows])
            bcast_scale(rows[0:1, 512:1024], 0)
        def d_part1(g):
            rstdbc, b_rstdbc = sc3[0], b_sc3[0]
            zg2 = [zg, wb[:, 0:1024]]
            b_zg2 = [b_zg, b_wb]
            sh2 = [shatc, wb[:, 1024:2048]]
            b_sh2 = [b_shatc, b_wb]
            ty2 = [tyc, xqtm[0].bitcast(F32)[:, 0:1024]]
            b_ty2 = [b_tyc, b_xqtm[0]]

            def front(i):
                pset = i % 2
                pi = i % 2
                ca, cb = 2 * i, 2 * i + 1
                sg_ = 2 * pset
                wq_stream("g", ca, sg_)
                wq_stream("g", cb, sg_ + 1)
                blin_mm(xqT[g % 2], b_xqT[g % 2], sg_, 2 * pset + 0, ca)
                blin_mm(xqT[g % 2], b_xqT[g % 2], sg_ + 1, 2 * pset + 1, cb)
                p.add("vector", lambda v, pset=pset, pi=pi: v.tensor_tensor(
                    out=zg2[pi], in0=ps2[pset], in1=sc3[2], op=ALU.mult),
                    reads=[b_ps[2 * pset], b_ps[2 * pset + 1], b_sc3[2]],
                    writes=[b_zg2[pi]])
                for sub, cx in ((0, ca), (1, cb)):
                    p.add("vector", lambda v, sub=sub, cx=cx, pi=pi:
                          v.scalar_tensor_tensor(
                              out=sh2[pi][:, sub * G:(sub + 1) * G],
                              in0=sfull[:, cx * G:(cx + 1) * G],
                              scalar=rmsw_fm[:, cx:cx + 1], in1=rstdbc[:, 0:512],
                              op0=ALU.mult, op1=ALU.mult),
                          reads=[b_s, b_rmsw, b_rstdbc], writes=[b_sh2[pi]])
                p.add("scalar", lambda a, pi=pi: a.activation(
                    out=thg, in_=zg2[pi], func=AF.Tanh, scale=0.5),
                    reads=[b_zg2[pi]], writes=[b_thg])
                p.add("gpsimd", lambda gp: gp.tensor_scalar(
                    out=fgc, in0=thg, scalar1=0.5, scalar2=0.5,
                    op0=ALU.mult, op1=ALU.add), reads=[b_thg], writes=[b_fgc])
                p.add("gpsimd", lambda gp, pi=pi: gp.tensor_tensor(
                    out=ty2[pi], in0=fgc, in1=sh2[pi], op=ALU.mult),
                    reads=[b_fgc, b_sh2[pi]], writes=[b_ty2[pi]])

            def tail(i):
                pi = i % 2
                ca, cb = 2 * i, 2 * i + 1
                p.add("vector", lambda v, ca=ca, pi=pi: v.tensor_tensor(
                    out=A[:, ca * G:(ca + 2) * G], in0=zg2[pi], in1=ty2[pi],
                    op=ALU.mult), reads=[b_zg2[pi], b_ty2[pi]], writes=[b_A])
                for sub, cx in ((0, ca), (1, cb)):
                    p.add("vector", lambda v, sub=sub, cx=cx: v.tensor_scalar(
                        out=wyc[:, sub * G:(sub + 1) * G],
                        in0=A[:, cx * G:(cx + 1) * G],
                        scalar1=normo_fm[:, cx:cx + 1], scalar2=None,
                        op0=ALU.mult),
                        reads=[b_A, b_normo], writes=[b_wyc])
                if i == 0:
                    p.add("vector", lambda v: v.tensor_tensor(
                        out=mrun[0], in0=wyc[:, 0:512], in1=wyc[:, 512:1024],
                        op=ALU.max), reads=[b_wyc], writes=[b_mrun[0]])
                    p.add("vector", lambda v: v.tensor_tensor(
                        out=mrun[1], in0=wyc[:, 0:512], in1=wyc[:, 512:1024],
                        op=ALU.min), reads=[b_wyc], writes=[b_mrun[1]])
                else:
                    for sub in range(2):
                        p.add("vector", lambda v, sub=sub: v.tensor_tensor(
                            out=mrun[0], in0=mrun[0],
                            in1=wyc[:, sub * G:(sub + 1) * G], op=ALU.max),
                            reads=[b_wyc, b_mrun[0]], writes=[b_mrun[0]])
                        p.add("vector", lambda v, sub=sub: v.tensor_tensor(
                            out=mrun[1], in0=mrun[1],
                            in1=wyc[:, sub * G:(sub + 1) * G], op=ALU.min),
                            reads=[b_wyc, b_mrun[1]], writes=[b_mrun[1]])

            def acc(i):
                for cx in (2 * i, 2 * i + 1):
                    p.add("scalar", lambda a, cx=cx: a.activation(
                        out=s2c[0], in_=A[:, cx * G:(cx + 1) * G],
                        func=AF.Square), reads=[b_A], writes=[b_s2c[0]])
                    p.add("tensor", lambda t, cx=cx: t.matmul(
                        ps[5][0:1, :], onesb, s2c[0],
                        start=(cx == 0), stop=(cx == KC - 1)),
                        reads=[b_s2c[0], b_onesb], writes=[b_ps[5]])

            front(0)
            for i in range(KC // 2):
                if i + 1 < KC // 2:
                    front(i + 1)
                tail(i)
                if i - 1 >= 0:
                    acc(i - 1)
            acc(KC // 2 - 1)
            if g == 0:
                tap("y", A, b_A, [128, KC * G], "float32")
            p.add("vector", lambda v: v.tensor_scalar(
                out=rows[0:1, 512:1024], in0=ps[5][0:1, :], scalar1=1.0 / H,
                scalar2=1e-8, op0=ALU.mult, op1=ALU.add),
                reads=[b_ps[5]], writes=[b_rows])
            p.add("scalar", lambda a: a.activation(
                out=rows[0:1, 512:1024], in_=rows[0:1, 512:1024], func=AF.Sqrt),
                reads=[b_rows], writes=[b_rows])
            p.add("vector", lambda v: v.reciprocal(
                out=rows[0:1, 512:1024], in_=rows[0:1, 512:1024]),
                reads=[b_rows], writes=[b_rows])
            p.add("vector", lambda v: v.tensor_scalar(
                out=mrun[1], in0=mrun[1], scalar1=-1.0, scalar2=None,
                op0=ALU.mult), reads=[b_mrun[1]], writes=[b_mrun[1]])
            p.add("vector", lambda v: v.tensor_tensor(
                out=mrun[0], in0=mrun[0], in1=mrun[1], op=ALU.max),
                reads=[b_mrun[0], b_mrun[1]], writes=[b_mrun[0]])
            for j in range(4):
                p.add("tensor", lambda t, j=j: t.transpose(
                    ps[7][:, j * 128:(j + 1) * 128],
                    mrun[0][:, j * 128:(j + 1) * 128], identf),
                    reads=[b_mrun[0], b_identf], writes=[b_ps[7]])
            for j in range(4):
                p.add("vector", lambda v, j=j: v.reduce_max(
                    out=cols[:, 40 + j:41 + j], in_=ps[7][:, j * 128:(j + 1) * 128],
                    axis=AX.X), reads=[b_ps[7]], writes=[b_cols])
            p.add("vector", lambda v: v.tensor_scalar(
                out=cols[:, 44:48], in0=cols[:, 40:44],
                scalar1=1e-5, scalar2=1.0 / 127.0, op0=ALU.max, op1=ALU.mult),
                reads=[b_cols], writes=[b_cols])
            p.add("vector", lambda v: v.reciprocal(
                out=cols[:, 48:52], in_=cols[:, 44:48]),
                reads=[b_cols], writes=[b_cols])
            p.add("vector", lambda v: v.tensor_scalar(
                out=cols[:, 52:56], in0=cols[:, 44:48],
                scalar1=swocol["o"], scalar2=127.0, op0=ALU.mult, op1=ALU.mult),
                reads=[b_cols, b_swocol["o"]], writes=[b_cols])
            for j in range(4):
                p.add("tensor", lambda t, j=j: t.transpose(
                    ps[5][0:1, j * 128:(j + 1) * 128], cols[:, 48 + j:49 + j],
                    identf), reads=[b_cols, b_identf], writes=[b_ps[5]])
            p.add("scalar", lambda a: a.copy(
                out=rows[0:1, 0:512], in_=ps[5][0:1, 0:512]),
                reads=[b_ps[5]], writes=[b_rows])
            bcast_scale(rows[0:1, 0:512], 2)
            for j in range(4):
                p.add("tensor", lambda t, j=j: t.transpose(
                    ps[5][0:1, j * 128:(j + 1) * 128], cols[:, 52 + j:53 + j],
                    identf), reads=[b_cols, b_identf], writes=[b_ps[5]])
            p.add("scalar", lambda a: a.copy(
                out=rows[0:1, 0:512], in_=ps[5][0:1, 0:512]),
                reads=[b_ps[5]], writes=[b_rows])
            p.add("vector", lambda v: v.tensor_tensor(
                out=rows[0:1, 0:512], in0=rows[0:1, 0:512],
                in1=rows[0:1, 512:1024], op=ALU.mult),
                reads=[b_rows], writes=[b_rows])
            bcast_scale(rows[0:1, 0:512], 1)
            yq82 = [yq8c, s2c[0].bitcast(I8)[:, 0:512]]
            b_yq82 = [b_yq8c, b_s2c[0]]
            for ec in range(KC):
                e2 = ec % 2
                p.add("scalar", lambda a, e2=e2, ec=ec: a.activation(
                    out=wyc[:, e2 * G:(e2 + 1) * G],
                    in_=A[:, ec * G:(ec + 1) * G], func=AF.Copy,
                    scale=normo_fm[:, ec:ec + 1]),
                    reads=[b_A, b_normo], writes=[b_wyc])
                p.add("vector", lambda v, e2=e2: v.tensor_tensor(
                    out=yq82[e2], in0=wyc[:, e2 * G:(e2 + 1) * G],
                    in1=sc3[2][:, 0:512], op=ALU.mult),
                    reads=[b_wyc, b_sc3[2]], writes=[b_yq82[e2]])
                if e2 == 0:
                    p.add("gpsimd", lambda gp, ec=ec, g=g, e2=e2: gp.tensor_copy(
                        out=xqT[g % 2][:, ec * G:(ec + 1) * G], in_=yq82[e2]),
                        reads=[b_yq82[e2]], writes=[b_xqT[g % 2]])
                else:
                    p.add("scalar", lambda a, ec=ec, g=g, e2=e2: a.activation(
                        out=xqT[g % 2][:, ec * G:(ec + 1) * G], in_=yq82[e2],
                        func=AF.Copy),
                        reads=[b_yq82[e2]], writes=[b_xqT[g % 2]])
            if g == 0:
                tap("yqT", yqT, b_yqT, [128, KC * G], "bfloat16")
        def d_part2(g, fixgen):
            yqg, b_yqg = xqT[g % 2], b_xqT[g % 2]
            for ep in range(KC // 2):
                pset = ep % 2
                ca, cb = 2 * ep, 2 * ep + 1
                so_ = 2 - 2 * pset
                wq_stream("o", ca, so_)
                wq_stream("o", cb, so_ + 1)
                blin_mm(yqg, b_yqg, so_, 4 + 2 * pset, ca)
                blin_mm(yqg, b_yqg, so_ + 1, 5 + 2 * pset, cb)
                if fixgen is not None:
                    next(fixgen, None)
                for sub, cx in ((0, ca), (1, cb)):
                    bank = 4 + 2 * pset + sub
                    p.add("vector", lambda v, bank=bank: v.tensor_tensor(
                        out=safm, in0=ps[bank], in1=sc3[1][:, 0:512],
                        op=ALU.mult),
                        reads=[b_ps[bank], b_sc3[1]], writes=[b_safm])
                    for j in range(4):
                        p.add("tensor", lambda t, j=j: t.transpose(
                            ps[0][:, j * 128:j * 128 + 128],
                            safm[:, j * 128:(j + 1) * 128], identf),
                            reads=[b_safm, b_identf], writes=[b_ps[0]])
                    p.add("scalar", lambda a, sub=sub: a.copy(
                        out=mrun[sub], in_=ps[0][:, 0:512]),
                        reads=[b_ps[0]], writes=[b_mrun[sub]])
                    p.add("gpsimd", lambda gp, sub=sub, cx=cx, g=g: gp.dma_start(
                        out=sattn_s[g * G:(g + 1) * G, cx * 128:(cx + 1) * 128]
                            .rearrange("(j q) c -> q j c", q=128),
                        in_=mrun[sub].rearrange("q (j c) -> q j c", c=128)),
                        reads=[b_mrun[sub]], writes=[dbuf["sattn"]], dma=True)
            if fixgen is not None:
                for _ in fixgen:
                    pass
            if g == 0:
                tap("sattn0", safm, b_safm, [128, 512], "float32")

        s1_group_driver()
        d_load(0)
        d_fix_head(0)
        exchange()
        for _ in d_fix_slices(0):
            pass
        d_fix_tail(0)
        for g in range(4):
            d_part1(g)
            if g + 1 < 4:
                d_load(g + 1)
                d_fix_head(g + 1)
                fixgen = d_fix_slices(g + 1)
            else:
                fixgen = None
            d_part2(g, fixgen)
            if g + 1 < 4:
                d_fix_tail(g + 1)
        p.barrier()
        p.flush(blk)

    if do_f1:
        with contextlib.ExitStack() as sc:
            xnT_sb, _bx = sb("xnT_sb", [128, KC * TOK], BF16, sc)
            b_xnT_kc = [Buf() for _ in range(KC)]
            wg8t, b_wg8t = zip(*[sb(f"wg8t{i}", [128, H], I8, sc) for i in range(4)])
            wu8t, b_wu8t = zip(*[sb(f"wu8t{i}", [128, H], I8, sc) for i in range(4)])
            wd8t, b_wd8t = zip(*[sb(f"wd8t{i}", [128, H], I8, sc) for i in range(2)])
            wgblk, b_wgblk = zip(*[sb(f"wgblk{i}", [128, H], BF16, sc) for i in range(3)])
            wublk, b_wublk = zip(*[sb(f"wublk{i}", [128, H], BF16, sc) for i in range(3)])
            wdbt, b_wdbt = zip(*[sb(f"wdbt{i}", [128, H], BF16, sc) for i in range(2)])
            gbf, b_gbf = zip(*[sb(f"gbf{i}", [128, 512], F32, sc) for i in range(4)])
            ubf, b_ubf = zip(*[sb(f"ubf{i}", [128, 512], F32, sc) for i in range(4)])
            fbf, b_fbf = zip(*[sb(f"fbf{i}", [128, 512], BF16, sc) for i in range(4)])
            cengs = ["vector", "scalar"]

            def conv(eng, out, in_, reads, writes):
                if eng == "scalar":
                    p.add(eng, lambda a, o=out, i=in_: a.activation(
                        out=o, in_=i, func=AF.Copy), reads=reads, writes=writes)
                else:
                    p.add(eng, lambda v, o=out, i=in_: v.tensor_copy(
                        out=o, in_=i), reads=reads, writes=writes)

            for kc in range(KC):
                p.add("gpsimd", lambda gp, kc=kc: gp.dma_start(
                    out=xnT_sb[:, kc * TOK:(kc + 1) * TOK],
                    in_=xnT_s[kc * 128:(kc + 1) * 128, :]),
                    reads=[dbuf["xnT"]], writes=[b_xnT_kc[kc]], dma=True)
            for oc in range(OCN):
                o2 = oc % 3
                i4 = oc % 4
                sl = slice(oc * 128, (oc + 1) * 128)
                p.add("sync", lambda s, o=wg8t[i4], i=wg8[sl, :]:
                      s.dma_start(out=o, in_=i), writes=[b_wg8t[i4]], dma=True)
                p.add("sync", lambda s, o=wu8t[i4], i=wu8[sl, :]:
                      s.dma_start(out=o, in_=i), writes=[b_wu8t[i4]], dma=True)
                conv(cengs[oc % 2], wgblk[o2], wg8t[i4],
                     [b_wg8t[i4]], [b_wgblk[o2]])
                conv(cengs[(oc + 1) % 2], wublk[o2], wu8t[i4],
                     [b_wu8t[i4]], [b_wublk[o2]])
                i2 = oc % 2
                p.add("sync", lambda s, o=wd8t[i2], i=wd8[sl, :]:
                      s.dma_start(out=o, in_=i), writes=[b_wd8t[i2]], dma=True)
                conv(cengs[oc % 2], wdbt[i2], wd8t[i2],
                     [b_wd8t[i2]], [b_wdbt[i2]])
                p.add("gpsimd", lambda g, o=wdb_s[sl, :], i=wdbt[i2]:
                      g.dma_start(out=o, in_=i), reads=[b_wdbt[i2]],
                      writes=[dbuf["wdb"]], dma=True)
                for hc in range(KC):
                    for tt in range(4):
                        p.add("tensor", lambda t, hc=hc, tt=tt, o2=o2:
                              t.matmul(ps[tt],
                                       wgblk[o2][:, hc * 128:(hc + 1) * 128],
                                       xnT_sb[:, hc * TOK + tt * 512:
                                              hc * TOK + tt * 512 + 512],
                                       start=(hc == 0), stop=(hc == KC - 1)),
                              reads=[b_wgblk[o2], b_xnT_kc[hc]],
                              writes=[b_ps[tt]])
                for hc in range(KC):
                    for tt in range(4):
                        p.add("tensor", lambda t, hc=hc, tt=tt, o2=o2:
                              t.matmul(ps[4 + tt],
                                       wublk[o2][:, hc * 128:(hc + 1) * 128],
                                       xnT_sb[:, hc * TOK + tt * 512:
                                              hc * TOK + tt * 512 + 512],
                                       start=(hc == 0), stop=(hc == KC - 1)),
                              reads=[b_wublk[o2], b_xnT_kc[hc]],
                              writes=[b_ps[4 + tt]])
                for tt in range(4):
                    p.add("scalar", lambda a, tt=tt, oc=oc:
                          a.activation(out=gbf[tt], in_=ps[tt], func=AF.Silu,
                                       scale=sgate_fm[:, oc:oc + 1]),
                          reads=[b_ps[tt], b_sgate], writes=[b_gbf[tt]])
                    p.add("vector", lambda v, tt=tt, oc=oc:
                          v.tensor_scalar(out=ubf[tt], in0=ps[4 + tt],
                                          scalar1=sup_fm[:, oc:oc + 1],
                                          scalar2=None, op0=ALU.mult),
                          reads=[b_ps[4 + tt], b_sup], writes=[b_ubf[tt]])
                    p.add("vector", lambda v, tt=tt: v.tensor_tensor(
                        out=fbf[tt], in0=gbf[tt], in1=ubf[tt], op=ALU.mult),
                        reads=[b_gbf[tt], b_ubf[tt]], writes=[b_fbf[tt]])
                    p.add("gpsimd", lambda gp, oc=oc, tt=tt: gp.dma_start(
                        out=fuse_s[oc * 128:(oc + 1) * 128,
                                   tt * 512:(tt + 1) * 512],
                        in_=fbf[tt]),
                        reads=[b_fbf[tt]], writes=[dbuf["fuse"]], dma=True)
            p.barrier()
            p.flush(blk)

    for m in "figo":
        if do_prep and f"wq_{m}" in _TAPS:
            t = nc.dram_tensor(f"tap_wq_{m}", [H, H], BF16,
                               kind="ExternalOutput").ap()
            taps.append(f"wq_{m}")
            p.add("gpsimd", lambda g, t=t, m=m: g.dma_start(out=t, in_=wq_s[m]),
                  reads=[dbuf["wq_" + m]], dma=True)
    if do_prep and "swc" in _TAPS:
        t = nc.dram_tensor("tap_swc", [128, 8], F32, kind="ExternalOutput").ap()
        taps.append("swc")
        p.add("gpsimd", lambda g, t=t: g.dma_start(out=t, in_=swc),
              reads=[b_swcol["o"], b_swocol["o"]], dma=True)
    p.flush(blk)
    if do_f1 and "fuse_full" in _TAPS:
        t = nc.dram_tensor("tap_fuse_full", [IFF, TOK], BF16,
                           kind="ExternalOutput").ap()
        taps.append("fuse_full")
        p.add("gpsimd", lambda g, t=t: g.dma_start(out=t, in_=fuse_s),
              reads=[dbuf["fuse"]], dma=True)
    if do_groups and "sattn_full" in _TAPS:
        t = nc.dram_tensor("tap_sattn_full", [TOK, H], F32,
                           kind="ExternalOutput").ap()
        taps.append("sattn_full")
        p.add("gpsimd", lambda g, t=t: g.dma_start(out=t, in_=sattn_s),
              reads=[dbuf["sattn"]], dma=True)
    if do_groups and "xnT_full" in _TAPS:
        t = nc.dram_tensor("tap_xnT_full", [H, TOK], BF16,
                           kind="ExternalOutput").ap()
        taps.append("xnT_full")
        p.add("gpsimd", lambda g, t=t: g.dma_start(out=t, in_=xnT_s),
              reads=[dbuf["xnT"]], dma=True)
    p.flush(blk)

    if do_f2:
        with contextlib.ExitStack() as sc:
            sdownbc, b_sdownbc = sb("sdownbc", [128, H], F32, sc)
            wdblk, b_wdblk = zip(*[sb(f"wdblk{i}", [128, H], BF16, sc) for i in range(4)])
            fch, b_fch = zip(*[sb(f"fch{i}", [128, 256], BF16, sc) for i in range(4)])
            satile, b_satile = zip(*[sb(f"satile{i}", [128, H], F32, sc) for i in range(2)])
            xtile, b_xtile = zip(*[sb(f"xtile{i}", [128, H], F32, sc) for i in range(2)])
            otile, b_otile = zip(*[sb(f"otile{i}", [128, H], F32, sc) for i in range(2)])
            t1c, b_t1c = zip(*[sb(f"t1c{i}", [128, 512], F32, sc) for i in range(2)])
            gfc, b_gfc = zip(*[sb(f"gfc{i}", [128, 512], F32, sc) for i in range(2)])
            sgc, b_sgc = zip(*[sb(f"sgc{i}", [128, 512], F32, sc) for i in range(2)])
            p.add("sync", lambda s: s.dma_start(
                out=sdownbc, in_=sdown_r.partition_broadcast(128)),
                writes=[b_sdownbc], dma=True)
            for pa in range(8):
                t0 = pa * 256
                for ic in range(OCN):
                    i2 = ic % 4
                    p.add("sync", lambda s, ic=ic, i2=i2: s.dma_start(
                        out=wdblk[i2], in_=wdb_s[ic * 128:(ic + 1) * 128, :]),
                        reads=[dbuf["wdb"]], writes=[b_wdblk[i2]], dma=True)
                    p.add("sync", lambda s, ic=ic, i2=i2, t0=t0: s.dma_start(
                        out=fch[i2], in_=fuse_s[ic * 128:(ic + 1) * 128, t0:t0 + 256]),
                        reads=[dbuf["fuse"]], writes=[b_fch[i2]], dma=True)
                    for ti in range(2):
                        for nh in range(4):
                            p.add("tensor", lambda t, ic=ic, i2=i2, ti=ti, nh=nh:
                                  t.matmul(ps[ti * 4 + nh],
                                           fch[i2][:, ti * 128:(ti + 1) * 128],
                                           wdblk[i2][:, nh * 512:(nh + 1) * 512],
                                           start=(ic == 0), stop=(ic == OCN - 1)),
                                  reads=[b_fch[i2], b_wdblk[i2]],
                                  writes=[b_ps[ti * 4 + nh]])
                for ti in range(2):
                    r0 = t0 + ti * 128
                    i2 = ti % 2
                    p.add("sync", lambda s, r0=r0, i2=i2: s.dma_start(
                        out=xtile[i2], in_=xm[r0:r0 + 128, :]),
                        writes=[b_xtile[i2]], dma=True)
                    p.add("sync", lambda s, r0=r0, i2=i2: s.dma_start(
                        out=satile[i2], in_=sattn_s[r0:r0 + 128, :]),
                        reads=[dbuf["sattn"]], writes=[b_satile[i2]], dma=True)
                    for nh in range(4):
                        n2 = nh % 2
                        p.add("vector", lambda v, ti=ti, nh=nh, n2=n2: v.tensor_tensor(
                            out=t1c[n2], in0=ps[ti * 4 + nh],
                            in1=sdownbc[:, nh * 512:(nh + 1) * 512], op=ALU.mult),
                            reads=[b_ps[ti * 4 + nh], b_sdownbc], writes=[b_t1c[n2]])
                        p.add("scalar", lambda a, n2=n2: a.activation(
                            out=gfc[n2], in_=t1c[n2], func=AF.Silu),
                            reads=[b_t1c[n2]], writes=[b_gfc[n2]])
                        p.add("vector", lambda v, i2=i2, nh=nh, n2=n2: v.tensor_tensor(
                            out=sgc[n2], in0=gfc[n2],
                            in1=satile[i2][:, nh * 512:(nh + 1) * 512], op=ALU.mult),
                            reads=[b_gfc[n2], b_satile[i2]], writes=[b_sgc[n2]])
                        p.add("vector", lambda v, i2=i2, nh=nh, n2=n2: v.tensor_tensor(
                            out=otile[i2][:, nh * 512:(nh + 1) * 512], in0=sgc[n2],
                            in1=xtile[i2][:, nh * 512:(nh + 1) * 512], op=ALU.add),
                            reads=[b_sgc[n2], b_xtile[i2]], writes=[b_otile[i2]])
                    p.add("gpsimd", lambda gp, r0=r0, i2=i2: gp.dma_start(
                        out=out_d[r0:r0 + 128, :], in_=otile[i2]),
                        reads=[b_otile[i2]], writes=[dbuf["out"]], dma=True)
            p.add("sync", lambda s: s.nop(),
                  extra=[dbuf["out"].w] if dbuf["out"].w else [])
            p.flush(blk)

    p.flush(blk)
    ctx.close()
    return nc, taps


def _host_prep(Wi, Wf, Wg, Wo, wg8, wu8, wd8, norms):
    bf = ml_dtypes.bfloat16
    normi, normf, normg, normo, rmsw, sgate, sup, sdown = norms
    return {
        "wfsw": _blin_swz(Wf),
        "wisw": _blin_swz(Wi),
        "wgsw": _blin_swz(Wg),
        "wosw": _blin_swz(Wo),
        "wg8": _ffn_swz(wg8),
        "wu8": _ffn_swz(wu8),
        "wd8": np.ascontiguousarray(np.asarray(wd8, np.int8)),
        "normf_r": np.asarray(normf, np.float32).reshape(1, H),
        "normi_r": np.asarray(normi, np.float32).reshape(1, H),
        "normg_r": np.asarray(normg, np.float32).reshape(1, H),
        "normo_fm_h": np.ascontiguousarray(
            np.asarray(normo, np.float32).reshape(KC, 128).T),
        "rmsw_fm_h": np.ascontiguousarray(
            np.asarray(rmsw, np.float32).reshape(KC, 128).T),
        "sgate_fm_h": np.ascontiguousarray(
            np.asarray(sgate, np.float32).reshape(OCN, 128).T),
        "sup_fm_h": np.ascontiguousarray(
            np.asarray(sup, np.float32).reshape(OCN, 128).T),
        "sdown_r": np.asarray(sdown, np.float32).reshape(1, H),
        "identb_h": np.eye(128).astype(bf),
        "identf_h": np.eye(128, dtype=np.float32),
        "onesf_h": np.ones((1, 128), np.float32),
        "onesb_h": np.ones((128, 1), bf),
        "onescf_h": np.ones((128, 1), np.float32),
    }


def kernel(L0_attn_x, L0_attn_Wi, L0_attn_Wf, L0_attn_Wg, L0_attn_Wo,
           L0_attn_norm_i, L0_attn_norm_f, L0_attn_norm_g, L0_attn_norm_o,
           L0_attn_rms_w, L0_ffn_w_gate, L0_ffn_w_up, L0_ffn_w_down,
           L0_ffn_s_gate, L0_ffn_s_up, L0_ffn_s_down):
    x = np.asarray(L0_attn_x, np.float32)
    B, L, _ = x.shape
    shared = _host_prep(L0_attn_Wi, L0_attn_Wf, L0_attn_Wg, L0_attn_Wo,
                        L0_ffn_w_gate, L0_ffn_w_up, L0_ffn_w_down,
                        (L0_attn_norm_i, L0_attn_norm_f, L0_attn_norm_g,
                         L0_attn_norm_o, L0_attn_rms_w, L0_ffn_s_gate,
                         L0_ffn_s_up, L0_ffn_s_down))
    nc, _ = build_program()
    in_maps = []
    for c in range(8):
        b = c % 4
        hf = c // 4
        m = dict(shared)
        m["xm"] = np.ascontiguousarray(x[b, hf * TOK:(hf + 1) * TOK])
        sm = np.zeros((128, 8), np.float32)
        if hf == 1:
            sm[:, b] = 1.0
        m["selmask"] = sm
        in_maps.append(m)
    res = run_bass_kernel_spmd(nc, in_maps, core_ids=list(range(8)))
    out = np.empty((B, L, H), np.float32)
    for c, r in enumerate(res.results):
        b = c % 4
        hf = c // 4
        out[b, hf * TOK:(hf + 1) * TOK] = r["out"]
    return out


def _dev_run(in_map, taps=(), n=1):
    _TAPS.update(taps)
    nc, _ = build_program()
    outs = []
    for _ in range(n):
        res = run_bass_kernel_spmd(nc, [dict(in_map)], core_ids=[0])
        outs.append(res.results[0])
    return outs
